# version 05
# baseline (speedup 1.0000x reference)
import math
import numpy as np
import ml_dtypes
import concourse.bass as bass
import concourse.mybir as mybir
from concourse.bass_utils import run_bass_kernel_spmd
from contextlib import ExitStack

F32 = mybir.dt.float32
F16 = mybir.dt.float16
AF = mybir.ActivationFunctionType
ALU = mybir.AluOpType
AX = mybir.AxisListType

T = 4096
D = 4096
HEADS = 32
GROUPS = 8
DH = 128
TOWN = 2048
NCH = 32
SCALE = 1.0 / math.sqrt(DH)
EPS = 1e-5
KS = [256 * (s + 1) for s in range(16)]


def own_chunks(z):
    return [c for c in range(NCH) if c % 4 in ((0, 3) if z == 0 else (1, 2))]


class Plan:
    ENGINES = ("sync", "gpsimd", "tensor", "vector", "scalar")

    def __init__(self):
        self.ops = {e: [] for e in self.ENGINES}
        self.cnt = {}

    def emit(self, eng, fn, waits=(), inc=None, k=1):
        ws = tuple((s, v) for (s, v) in waits if v > 0)
        self.ops[eng].append((fn, ws, inc, k))
        if inc is None:
            return None
        self.cnt[inc] = self.cnt.get(inc, 0) + k
        return (inc, self.cnt[inc])

    def ev(self, name):
        return (name, self.cnt.get(name, 0))


def _w(*evs):
    d = {}
    for e in evs:
        if e is None:
            continue
        s, v = e
        if d.get(s, -1) < v:
            d[s] = v
    return tuple(d.items())


def build_program(debug=False, phases=("A1", "A2B", "C", "D", "E")):
    nc = bass.Bass("TRN2", target_bir_lowering=False)
    P = Plan()
    scratch_kind = "ExternalOutput" if debug else "Internal"

    xf = nc.dram_tensor("xf", [T, D], F32, kind="ExternalInput").ap()
    xo = nc.dram_tensor("xo", [TOWN, D], F32, kind="ExternalInput").ap()
    xocb = nc.dram_tensor("xocb", [16, 128, 16, 256], F32, kind="ExternalInput").ap()
    cosf_d = nc.dram_tensor("cosf", [128, T], F32, kind="ExternalInput").ap()
    sinf_d = nc.dram_tensor("sinf", [128, T], F32, kind="ExternalInput").ap()
    coso_d = nc.dram_tensor("coso", [128, TOWN], F32, kind="ExternalInput").ap()
    sino_d = nc.dram_tensor("sino", [128, TOWN], F32, kind="ExternalInput").ap()
    masks_d = nc.dram_tensor("masks", [16, 128, 256], F32, kind="ExternalInput").ap()
    wq_d = nc.dram_tensor("wq", [32, 32, 128, 128], F32, kind="ExternalInput").ap()
    wk_d = nc.dram_tensor("wk", [8, 32, 128, 128], F32, kind="ExternalInput").ap()
    wv_d = nc.dram_tensor("wv", [8, 32, 128, 128], F32, kind="ExternalInput").ap()
    wo_d = nc.dram_tensor("wo", [16, 128, 32, 256], F32, kind="ExternalInput").ap()
    wf_d = nc.dram_tensor("wf", [16, 128, 32, 256], F16, kind="ExternalInput").ap()
    bias_d = nc.dram_tensor("biasf", [1, D], F16, kind="ExternalInput").ap()
    idt_d = nc.dram_tensor("idt", [128, 128], F16, kind="ExternalInput").ap()
    rt_d = nc.dram_tensor("rt", [128, 128], F16, kind="ExternalInput").ap()
    oner_d = nc.dram_tensor("oner", [1, 128], F16, kind="ExternalInput").ap()
    cvec_d = nc.dram_tensor("cvec", [128, 4], F32, kind="ExternalInput").ap()

    k_scr = nc.dram_tensor("k_scr", [GROUPS, 128, T], F16, kind=scratch_kind).ap()
    v_scr = nc.dram_tensor("v_scr", [GROUPS, 128, T], F16, kind=scratch_kind).ap()
    ctx_scr = nc.dram_tensor("ctx_scr", [HEADS, 128, TOWN], F16, kind=scratch_kind).ap()
    x1_scr = nc.dram_tensor("x1_scr", [16, 128, 16, 256], F32, kind=scratch_kind).ap()
    out_cb = nc.dram_tensor("out_cb", [16, 128, 16, 256], F32, kind="ExternalOutput").ap()

    with ExitStack() as ctx:
        E = ctx.enter_context

        act = E(nc.sbuf_tensor("act", [128, 32 * 1024], F16))
        af3 = act[:].rearrange("p (c t) -> p c t", c=32)
        AR16 = E(nc.sbuf_tensor("AR16", [128, 38912], F16))
        AR32 = E(nc.sbuf_tensor("AR32", [128, 14464], F32))
        idt_s = E(nc.sbuf_tensor("idt_s", [128, 128], F16))
        rt_s = E(nc.sbuf_tensor("rt_s", [128, 128], F16))
        oner_s = E(nc.sbuf_tensor("oner_s", [1, 128], F16))
        cvec_s = E(nc.sbuf_tensor("cvec_s", [128, 4], F32))
        bias_s = E(nc.sbuf_tensor("bias_s", [1, D], F16))

        def v16(off, n):
            assert off + n <= 38912, (off, n)
            return AR16[:, off:off + n]

        def v32(off, n):
            assert off + n <= 14464, (off, n)
            return AR32[:, off:off + n]

        lntm = v16(0, 4096)
        sqb = v16(4096, 4096)
        wsb = [v16(8192, 4096), v16(12288, 4096)]
        cosf = v16(24576, 4096)
        sinf = v16(28672, 4096)
        kev = v16(32768, 1024)
        r1b = v16(33792, 1024)
        r2b = v16(34816, 1024)
        kgb = v16(0, 4096)
        vgf = v16(4096, 4096)
        pbufb = v16(16384, 2048)
        ptbb = v16(18432, 2048)
        qraw = v16(22528, 1024)
        coso = v16(24576, 2048)
        sino = v16(26624, 2048)
        masksb = v16(28672, 4096)
        vtm = v16(32768, 4160)
        vt3 = vtm.rearrange("p (c n) -> p c n", n=130)
        ctxhb_l = [v16(36928, 1024), v16(21504, 1024)]
        cbuf_l = [v16(37952, 128), v16(38592, 128)]
        qh_l = [v16(23552, 1024), v16(20480, 1024)]
        r2q = v16(38080, 512)
        wsc = [v16(8192, 8192), v16(16384, 8192)]
        lntmE_l = [v16(0, 256), v16(256, 256)]
        sqbC = v16(4096, 256)

        xsl = [v32(0, 4096), v32(4096, 4096)]
        stat = v32(8192, 96)
        fbuf = v32(8288, 2048)
        obA = v32(10336, 2048)
        obB = v32(12384, 2048)
        xcb = [v32(0, 2048), v32(2048, 2048)]
        ctxA_l = [v32(4096, 132), v32(4228, 132)]
        sum1 = stat[:, 32:48]
        sum2 = stat[:, 48:64]
        mu2 = stat[:, 64:80]
        rstd2 = stat[:, 32:48]
        nmr2 = stat[:, 48:64]

        psA = E(nc.psum_tensor("psA", [128, 2048], F32))
        psB = E(nc.psum_tensor("psB", [128, 512], F32))
        psC = E(nc.psum_tensor("psC", [128, 512], F32))
        psT = E(nc.psum_tensor("psT", [128, 1024], F16))
        psT2 = E(nc.psum_tensor("psT2", [128, 1024], F16))
        psAh = [psA[:, 0:1024], psA[:, 1024:2048]]
        psTb = [psT, psT2]
        ctxb = [psC, psB]

        eps_ap = cvec_s[:, 0:1]
        zero_ap = cvec_s[:, 1:2]

        def DMA(eng, dst, src, inc, waits=()):
            return P.emit(eng, lambda e, d=dst, s=src: e.dma_start(out=d, in_=s),
                          _w(*waits), inc, 16)

        def MM(out, lhsT, rhs, start, stop, waits=(), inc=None):
            return P.emit("tensor",
                          lambda e, o=out, l=lhsT, r=rhs, st=start, sp=stop:
                          e.matmul(o, l, r, start=st, stop=sp), _w(*waits), inc)

        def TP(out, in_, waits=(), inc=None):
            return P.emit("tensor",
                          lambda e, o=out, i=in_: e.transpose(o, i, idt_s[:]),
                          _w(*waits), inc)

        def ACTI(out, in_, func, bias, scale, waits=(), inc=None):
            return P.emit("scalar",
                          lambda e, o=out, i=in_, f=func, b=bias, s=scale:
                          e.activation(o, i, f, bias=b, scale=s), _w(*waits), inc)

        def TT(out, a, b, op, waits=(), inc=None):
            return P.emit("vector",
                          lambda e, o=out, x=a, y=b, p=op: e.tensor_tensor(o, x, y, op=p),
                          _w(*waits), inc)

        def TS(out, in0, s1, s2, op0, op1, waits=(), inc=None):
            return P.emit("vector",
                          lambda e, o=out, i=in0, a=s1, b=s2, p0=op0, p1=op1:
                          e.tensor_scalar(o, i, a, b, op0=p0, op1=p1), _w(*waits), inc)

        def RED(out, in_, op, waits=(), inc=None):
            return P.emit("vector",
                          lambda e, o=out, i=in_, p=op: e.tensor_reduce(o, i, axis=AX.X, op=p),
                          _w(*waits), inc)

        def CP(out, in_, waits=(), inc=None):
            return P.emit("vector", lambda e, o=out, i=in_: e.tensor_copy(o, i),
                          _w(*waits), inc)

        def RECIP(out, in_, waits=(), inc=None):
            return P.emit("vector", lambda e, o=out, i=in_: e.reciprocal(o, i),
                          _w(*waits), inc)

        def MS(ap, val, waits=(), inc=None):
            return P.emit("gpsimd", lambda e, a=ap, v=val: e.memset(a, v),
                          _w(*waits), inc)

        def WAIT_ONLY(eng, waits):
            P.emit(eng, lambda e: None, _w(*waits), None)

        DMA("sync", idt_s[:], idt_d, "ldc")
        DMA("sync", rt_s[:], rt_d, "ldc")
        DMA("sync", oner_s[:], oner_d, "ldc")
        DMA("sync", cvec_s[:], cvec_d, "ldc")
        DMA("gpsimd", cosf, cosf_d, "ldc")
        DMA("gpsimd", sinf, sinf_d, "ldc")
        EV_CONST = P.ev("ldc")

        tp_war = [None, None]
        tp_bank = [0]
        sqb_rd = [None]

        def ln_slab(xbuf, out16, waits):
            out_war = P.ev("pet")
            e_s1 = RED(stat[:, 0:1], xbuf, ALU.add, waits=waits, inc="dv")
            e_sq = ACTI(sqb, xbuf, AF.Square, 0.0, 1.0,
                        waits=waits + _w(sqb_rd[0]), inc="ac")
            e_s2 = RED(stat[:, 1:2], sqb, ALU.add, waits=(e_sq,), inc="dv")
            sqb_rd[0] = e_s2
            e_mu = TS(stat[:, 2:3], stat[:, 0:1], 1.0 / D, None, ALU.mult, ALU.bypass,
                      waits=(e_s1,), inc="dv")
            e_msq = TT(stat[:, 3:4], stat[:, 2:3], stat[:, 2:3], ALU.mult,
                       waits=(e_mu,), inc="dv")
            e_var = TS(stat[:, 4:5], stat[:, 1:2], 1.0 / D, stat[:, 3:4], ALU.mult,
                       ALU.subtract, waits=(e_s2, e_msq), inc="dv")
            e_l = ACTI(stat[:, 5:6], stat[:, 4:5], AF.Ln, eps_ap, 1.0,
                       waits=(e_var, EV_CONST), inc="ac")
            e_r = ACTI(stat[:, 6:7], stat[:, 5:6], AF.Exp, zero_ap, -0.5,
                       waits=(e_l,), inc="ac")
            e_n = TS(stat[:, 7:8], stat[:, 2:3], stat[:, 6:7], -1.0, ALU.mult, ALU.mult,
                     waits=(e_r, e_mu), inc="dv")
            e_ap = ACTI(out16, xbuf, AF.Identity, stat[:, 7:8], stat[:, 6:7],
                        waits=(e_n, e_r) + _w(out_war), inc="ac")
            return e_ap

        def transpose_group(blocks, dst, waits, evac="vector"):
            bank = tp_bank[0] % 2
            tp_bank[0] += 1
            pt = psTb[bank][:, 0:len(blocks) * 128]
            tp_ev = None
            for j, b in enumerate(blocks):
                tp_ev = TP(pt[:, j * 128:(j + 1) * 128], b,
                           waits=(waits + _w(tp_war[bank])) if j == 0 else (),
                           inc="pet")
            src = pt.rearrange("p (c t) -> p c t", c=len(blocks))
            if evac == "scalar":
                ev = P.emit("scalar",
                            lambda e, o=dst, i=src: e.activation(o, i, AF.Copy,
                                                                 bias=0.0, scale=1.0),
                            _w(tp_ev), "dvt")
            else:
                ev = CP(dst, src, waits=_w(tp_ev), inc="dvt")
            tp_war[bank] = ev
            return ev

        def transpose_slab(src16, tcol, waits, war_mm):
            last = None
            for icq in range(8):
                blocks = [src16[:, (icq * 4 + j) * 128:(icq * 4 + j + 1) * 128]
                          for j in range(4)]
                dst = af3[:, icq * 4:(icq + 1) * 4, tcol:tcol + 128]
                last = transpose_group(blocks, dst, waits + _w(*war_mm))
            return last

        ev_apply = {}
        ev_kv_mm_last = None
        ev_psA_free = [None, None]
        ev_cb_free = [None, None]
        ev_ws_free = [None, None]
        ev_kev_free = None

        if "A1" in phases:
            for TC in range(4):
                tcols = slice(TC * 1024, (TC + 1) * 1024)
                for sub in range(8):
                    slab = TC * 8 + sub
                    buf = slab % 2
                    e_ld = DMA("sync", xsl[buf], xf[slab * 128:(slab + 1) * 128, :],
                               "ldx", waits=_w(ev_apply.get(slab - 2)))
                    e_ap = ln_slab(xsl[buf], lntm, _w(e_ld))
                    ev_apply[slab] = e_ap
                    transpose_slab(lntm, sub * 128, _w(e_ap), (ev_kv_mm_last,))
                e_lnfm = P.ev("dvt")
                for idx in range(16):
                    kv, g = idx // 8, idx % 8
                    wsrc = (wk_d if kv == 0 else wv_d)[g]
                    wb = idx % 2
                    e_wl = DMA("gpsimd", wsb[wb].rearrange("p (a b) -> p a b", a=32),
                               wsrc.rearrange("a p b -> p a b"), "ldw",
                               waits=_w(ev_ws_free[wb]))
                    par = idx % 2
                    pa = psAh[par]
                    e_mm = None
                    for i in range(32):
                        w0 = _w(e_wl, e_lnfm, ev_psA_free[par]) if i == 0 else ()
                        MM(pa[:, 0:512], wsb[wb][:, i * 128:(i + 1) * 128],
                           af3[:, i, 0:512], i == 0, i == 31, waits=w0)
                        e_mm = MM(pa[:, 512:1024], wsb[wb][:, i * 128:(i + 1) * 128],
                                  af3[:, i, 512:1024], i == 0, i == 31,
                                  inc="pea" if i == 31 else None)
                    ev_ws_free[wb] = e_mm
                    e_ev = ACTI(kev, pa, AF.Copy, 0.0, 1.0,
                                waits=_w(e_mm, ev_kev_free), inc="ac")
                    ev_psA_free[par] = e_ev
                    if kv == 0:
                        e_r1 = []
                        for jj in range(2):
                            cb = ctxb[jj]
                            e_rk = MM(cb[:, 0:512], rt_s[:],
                                      kev[:, jj * 512:(jj + 1) * 512], True, True,
                                      waits=_w(e_ev, ev_cb_free[jj]), inc="per")
                            scols = slice(TC * 1024 + jj * 512, TC * 1024 + (jj + 1) * 512)
                            e1 = TT(r1b[:, jj * 512:(jj + 1) * 512], cb[:, 0:512],
                                    sinf[:, scols], ALU.mult,
                                    waits=_w(e_rk, EV_CONST), inc="dv")
                            ev_cb_free[jj] = e1
                            e_r1.append(e1)
                        e_r2 = TT(r2b, kev, cosf[:, tcols], ALU.mult,
                                  waits=_w(e_ev, EV_CONST), inc="dv")
                        e_kr = TT(kev, r1b, r2b, ALU.add,
                                  waits=_w(*e_r1, e_r2), inc="dv")
                        e_st = DMA("scalar", k_scr[g][:, tcols], kev, "stkv", waits=_w(e_kr))
                    else:
                        e_st = DMA("scalar", v_scr[g][:, tcols], kev, "stkv", waits=_w(e_ev))
                    ev_kev_free = e_st
                ev_kv_mm_last = P.ev("pea")

        ev_qproj_last = None
        ev_ctx_store = [None, None]
        ev_cbuf_free = [None, None]
        ev_qh_war = [None, None]
        ev_kg_free = None
        ev_vg_free = None
        ev_vtm_free = None
        ev_qraw_free = [None, None]
        ev_pbuf_free = [None, None]
        ev_ptb_free = [None, None]
        ev_kvready = (None, None)
        ev_maskload = None

        if "A2B" in phases:
            DMA("gpsimd", coso, coso_d, "ldc", waits=_w(P.ev("dv")))
            DMA("gpsimd", sino, sino_d, "ldc")
            ev_maskload = DMA("gpsimd", masksb.rearrange("p (s n) -> p s n", s=16),
                              masks_d.rearrange("s p n -> p s n"), "ldc",
                              waits=_w(P.ev("dv")))
            EV_BC = P.ev("ldc")

            for ph in range(2):
                gate_ac = P.ev("ac")
                for sub in range(8):
                    slab = ph * 8 + sub
                    buf = sub % 2
                    war = ev_apply.get(("o", ph, sub - 2))
                    e_ld = DMA("sync", xsl[buf], xo[slab * 128:(slab + 1) * 128, :],
                               "ldx", waits=_w(war, gate_ac if sub < 2 else None))
                    e_ap = ln_slab(xsl[buf], lntm, _w(e_ld))
                    ev_apply[("o", ph, sub)] = e_ap
                    transpose_slab(lntm, sub * 128, _w(e_ap),
                                   (ev_kv_mm_last, ev_qproj_last))
                e_lnfm = P.ev("dvt")

                for hp in range(HEADS // 2):
                    g = (2 * hp) // 4
                    e_wl = [None, None]
                    for lane in range(2):
                        h = 2 * hp + lane
                        e_wl[lane] = DMA("gpsimd",
                                         wsb[lane].rearrange("p (a b) -> p a b", a=32),
                                         wq_d[h].rearrange("a p b -> p a b"), "ldw",
                                         waits=_w(ev_ws_free[lane]))
                    if hp % 2 == 0:
                        a2done = (e_lnfm, P.ev("dv") if hp == 0 else None,
                                  P.ev("ac") if hp == 0 else None,
                                  P.ev("stkv") if (ph == 0 and hp == 0) else None)
                        e_kg = DMA("sync", kgb, k_scr[g], "ldkv",
                                   waits=_w(ev_kg_free, *a2done))
                        e_vg = DMA("sync", vgf, v_scr[g], "ldkv",
                                   waits=_w(ev_vg_free, *a2done))
                        e_msv = MS(vtm, 1.0, waits=_w(ev_vtm_free,
                                   P.ev("dv") if hp == 0 else None), inc="gp")
                        e_vt = None
                        for kq in range(8):
                            blocks = [vgf[:, (kq * 4 + j) * 128:(kq * 4 + j + 1) * 128]
                                      for j in range(4)]
                            e_vt = transpose_group(
                                blocks, vt3[:, kq * 4:(kq + 1) * 4, 0:128],
                                _w(e_vg, e_msv))
                        ev_kvready = (e_kg, e_vt)
                    e_qh_l = [None, None]
                    for lane in range(2):
                        qhb = qh_l[lane]
                        for tcq in range(2):
                            qc = slice(tcq * 512, (tcq + 1) * 512)
                            e_mm = None
                            for i in range(32):
                                w0 = _w(e_wl[lane], e_lnfm, ev_cb_free[1]) if i == 0 else ()
                                e_mm = MM(psB[:, 0:512],
                                          wsb[lane][:, i * 128:(i + 1) * 128],
                                          af3[:, i, qc], i == 0, i == 31, waits=w0,
                                          inc="peq" if i == 31 else None)
                            e_qe = ACTI(qraw[:, qc], psB[:, 0:512], AF.Copy, 0.0, 1.0,
                                        waits=_w(e_mm, *(ev_qraw_free[tcq] or ())),
                                        inc="ac")
                            ev_cb_free[1] = e_qe
                            e_rq = MM(psC[:, 0:512], rt_s[:], qraw[:, qc], True, True,
                                      waits=_w(e_qe, ev_cb_free[0]), inc="per")
                            ocs = slice(ph * 1024 + tcq * 512, ph * 1024 + (tcq + 1) * 512)
                            e_q1 = TT(qhb[:, qc], psC[:, 0:512], sino[:, ocs], ALU.mult,
                                      waits=_w(e_rq, EV_BC, ev_qh_war[lane]), inc="dv")
                            ev_cb_free[0] = e_q1
                            e_q2 = TT(r2q, qraw[:, qc], coso[:, ocs], ALU.mult,
                                      waits=_w(e_qe, EV_BC), inc="dv")
                            e_qh_l[lane] = TT(qhb[:, qc], qhb[:, qc], r2q, ALU.add,
                                              waits=_w(e_q1, e_q2), inc="dv")
                            ev_qraw_free[tcq] = (e_rq, e_q2)

                    subrows = []
                    for ls in range(8):
                        s = ph * 8 + ls
                        ks = KS[s]
                        n = (ks + 1023) // 1024
                        for r in range(n):
                            subrows.append(dict(
                                s=s, ls=ls, r=r, k0=r * 1024,
                                k1=min((r + 1) * 1024, ks), ks=ks,
                                first=(r == 0), last=(r == n - 1), nsub=n))

                    slot_state = [{}, {}]
                    sr_l = [None, None]

                    def emit_S(lane, srp, kvr=ev_kvready, e_qh_l=e_qh_l):
                        sr = dict(srp)
                        par = lane
                        nk = sr["k1"] - sr["k0"]
                        nj = (nk + 511) // 512
                        qtile = qh_l[lane][:, sr["ls"] * 128:(sr["ls"] + 1) * 128]
                        e_chunks = []
                        for jj in range(nj):
                            w = min(512, nk - jj * 512)
                            w0 = _w(e_qh_l[lane], kvr[0], ev_psA_free[par]) if jj == 0 else ()
                            e_chunks.append(
                                MM(psAh[par][:, jj * 512:jj * 512 + w], qtile,
                                   kgb[:, sr["k0"] + jj * 512:sr["k0"] + jj * 512 + w],
                                   True, True, waits=w0, inc="pes"))
                        sr["e_chunks"] = e_chunks
                        return sr

                    def emit_softmax(lane, sr):
                        par = lane
                        nk = sr["k1"] - sr["k0"]
                        nj = (nk + 511) // 512
                        e_chunks = sr["e_chunks"]
                        e_s = e_chunks[-1]
                        e_mask = None
                        if sr["last"]:
                            lo = sr["ks"] - 256 - sr["k0"]
                            mreg = psAh[par][:, lo:lo + 256]
                            e_mask = TT(mreg, mreg,
                                        masksb[:, sr["s"] * 256:(sr["s"] + 1) * 256],
                                        ALU.add, waits=_w(e_s, ev_maskload), inc="dv")
                        if nj == 2:
                            e_m0 = RED(stat[:, 16 + par:17 + par], psAh[par][:, 0:512],
                                       ALU.max, waits=_w(e_chunks[0]), inc="dv")
                            e_m1 = RED(stat[:, 8 + par:9 + par],
                                       psAh[par][:, 512:nk], ALU.max,
                                       waits=_w(e_s, e_mask), inc="dv")
                            e_max = TT(stat[:, 8 + par:9 + par], stat[:, 8 + par:9 + par],
                                       stat[:, 16 + par:17 + par], ALU.max,
                                       waits=_w(e_m0, e_m1), inc="dv")
                        else:
                            e_max = RED(stat[:, 8 + par:9 + par], psAh[par][:, 0:nk],
                                        ALU.max, waits=_w(e_s, e_mask), inc="dv")
                        st0 = slot_state[lane]
                        mbase = 10 if lane == 0 else 24
                        if sr["first"]:
                            mcol = mbase
                            old = mbase
                            e_mrun = CP(stat[:, mbase:mbase + 1], stat[:, 8 + par:9 + par],
                                        waits=_w(e_max), inc="dv")
                        else:
                            mcol = mbase + sr["r"] % 2
                            old = st0["mcol"]
                            e_mrun = TT(stat[:, mcol:mcol + 1], stat[:, old:old + 1],
                                        stat[:, 8 + par:9 + par], ALU.max,
                                        waits=_w(st0["e_mrun"], e_max), inc="dv")
                        e_nb = TS(stat[:, 14 + par:15 + par], stat[:, mcol:mcol + 1],
                                  -SCALE, None, ALU.mult, ALU.bypass,
                                  waits=_w(e_mrun), inc="dv")
                        e_exp = ACTI(pbufb[:, par * 1024:par * 1024 + nk],
                                     psAh[par][:, 0:nk], AF.Exp,
                                     stat[:, 14 + par:15 + par], SCALE,
                                     waits=_w(e_nb, ev_pbuf_free[par]), inc="ac")
                        ev_psA_free[par] = e_exp
                        if not sr["first"]:
                            e_fd = TT(stat[:, 12 + par:13 + par], stat[:, old:old + 1],
                                      stat[:, mcol:mcol + 1], ALU.subtract,
                                      waits=_w(e_mrun), inc="dv")
                            e_f = ACTI(stat[:, 28 + par:29 + par],
                                       stat[:, 12 + par:13 + par], AF.Exp,
                                       zero_ap, SCALE, waits=_w(e_fd), inc="ac")
                            sr["e_f"] = e_f
                            ev_psA_free[par] = e_f
                        st0["mcol"] = mcol
                        st0["e_mrun"] = e_mrun
                        sr["e_exp"] = e_exp
                        sr["e_mrun"], sr["mcol"] = e_mrun, mcol

                    def emit_TP_PV_combine(lane, kvr=ev_kvready, sr=None):
                        par = lane
                        nk = sr["k1"] - sr["k0"]
                        nkb = nk // 128
                        ev_evac = None
                        for kq in range((nkb + 3) // 4):
                            nblk = min(4, nkb - kq * 4)
                            blocks = [pbufb[:, par * 1024 + (kq * 4 + jj) * 128:
                                            par * 1024 + (kq * 4 + jj + 1) * 128]
                                      for jj in range(nblk)]
                            dst = ptbb[:, par * 1024 + kq * 512:
                                       par * 1024 + kq * 512 + nblk * 128]
                            ev_evac = transpose_group(
                                blocks, dst.rearrange("p (c t) -> p c t", c=nblk),
                                _w(sr["e_exp"], ev_ptb_free[par]),
                                evac="scalar" if (tp_bank[0] % 2) else "vector")
                        ev_pbuf_free[par] = P.ev("pet")
                        cb = ctxb[lane]
                        ctxA = ctxA_l[lane]
                        e_pv = None
                        for kc in range(nkb):
                            w0 = _w(ev_evac, kvr[1], ev_cb_free[lane]) if kc == 0 else ()
                            e_pv = MM(cb[:, 0:129],
                                      ptbb[:, par * 1024 + kc * 128:par * 1024 + (kc + 1) * 128],
                                      vt3[:, sr["k0"] // 128 + kc, 0:129],
                                      kc == 0, kc == nkb - 1, waits=w0,
                                      inc="pev" if kc == nkb - 1 else None)
                        ev_ptb_free[par] = e_pv
                        st0 = slot_state[lane]
                        if sr["nsub"] > 1:
                            if sr["first"]:
                                e_c = CP(ctxA[:, 0:129], cb[:, 0:129],
                                         waits=_w(e_pv, st0.get("e_ctxA_rd")),
                                         inc="dv")
                            else:
                                e_sc = TS(ctxA[:, 0:129], ctxA[:, 0:129],
                                          stat[:, 28 + par:29 + par],
                                          None, ALU.mult, ALU.bypass,
                                          waits=_w(sr["e_f"], st0["e_ctx"]),
                                          inc="dv")
                                e_c = TT(ctxA[:, 0:129], ctxA[:, 0:129], cb[:, 0:129],
                                         ALU.add, waits=_w(e_sc, e_pv), inc="dv")
                            st0["e_ctx"] = e_c
                            ev_cb_free[lane] = e_c
                        if sr["last"]:
                            if sr["nsub"] == 1:
                                lsrc, csrc, e_fin = cb[:, 128:129], cb[:, 0:128], _w(e_pv)
                            else:
                                lsrc, csrc, e_fin = (ctxA[:, 128:129], ctxA[:, 0:128],
                                                     _w(st0["e_ctx"]))
                            rlc = 30 + lane
                            e_rl = RECIP(stat[:, rlc:rlc + 1], lsrc, waits=e_fin, inc="dv")
                            e_cf = ACTI(cbuf_l[lane], csrc, AF.Identity, zero_ap,
                                        stat[:, rlc:rlc + 1],
                                        waits=_w(e_rl, ev_cbuf_free[lane]), inc="ac")
                            if sr["nsub"] == 1:
                                ev_cb_free[lane] = e_cf
                            else:
                                st0["e_ctxA_rd"] = e_cf
                            e_cv = transpose_group(
                                [cbuf_l[lane]],
                                ctxhb_l[lane][:, sr["ls"] * 128:(sr["ls"] + 1) * 128]
                                .rearrange("p (c t) -> p c t", c=1),
                                _w(e_cf, ev_ctx_store[lane]))
                            ev_cbuf_free[lane] = e_cv
                            sr["e_cv"] = e_cv

                    have_prev = False
                    e_cv_last = [None, None]
                    prev_srs = [None, None]
                    for srp in subrows:
                        cur = [emit_S(0, srp), emit_S(1, srp)]
                        if have_prev:
                            for lane in range(2):
                                emit_TP_PV_combine(lane, sr=prev_srs[lane])
                                if prev_srs[lane].get("e_cv"):
                                    e_cv_last[lane] = prev_srs[lane]["e_cv"]
                        for lane in range(2):
                            emit_softmax(lane, cur[lane])
                        prev_srs = cur
                        have_prev = True
                    for lane in range(2):
                        emit_TP_PV_combine(lane, sr=prev_srs[lane])
                        e_cv_last[lane] = prev_srs[lane]["e_cv"]

                    for lane in range(2):
                        h = 2 * hp + lane
                        e_cs = DMA("scalar", ctx_scr[h][:, ph * 1024:(ph + 1) * 1024],
                                   ctxhb_l[lane], "stcx", waits=_w(e_cv_last[lane]))
                        ev_ctx_store[lane] = e_cs
                        ev_qh_war[lane] = P.ev("pes")
                    ev_ws_free[0] = P.ev("peq")
                    ev_ws_free[1] = P.ev("peq")
                    if hp % 2 == 1:
                        ev_kg_free = P.ev("pes")
                        ev_vg_free = P.ev("pet")
                        ev_vtm_free = P.ev("pev")
                ev_qproj_last = P.ev("peq")

        ev_sum = [None] * 16
        if "C" in phases:
            gate_c = (P.ev("pet"), P.ev("pev"), P.ev("pes"), P.ev("per"),
                      P.ev("dv"), P.ev("ac"))
            e_z1 = MS(sum1, 0.0, waits=_w(P.ev("dv"), P.ev("ac")), inc="gp")
            e_z2 = MS(sum2, 0.0, inc="gp")
            for t in range(16):
                ev_sum[t] = (e_z1, e_z2)
            ev_ws_free_c = [gate_c, gate_c]
            ev_xcb_free = [(P.ev("ac"),), (P.ev("ac"),)]
            ev_fbuf_free = [(P.ev("ac"),), (P.ev("ac"),)]
            ev_psA_free = [P.ev("ac"), P.ev("ac")]
            ev_x1b_free = None
            ev_sq_free = sqb_rd[0]
            for th in range(2):
                act_gate = P.ev("peq") if th == 0 else P.ev("pec")
                e_cl = None
                for q8 in range(4):
                    src = ctx_scr[q8 * 8:(q8 + 1) * 8, :, th * 1024:(th + 1) * 1024]
                    e_cl = DMA("sync", af3[:, q8 * 8:(q8 + 1) * 8, :],
                               src.rearrange("h p t -> p h t"), "ldcx",
                               waits=_w(P.ev("stcx"), act_gate) if q8 == 0 else ())
                for ob in range(16):
                    wb = ob % 2
                    e_wl = DMA("gpsimd", wsc[wb].rearrange("p (a b) -> p a b", a=32),
                               wo_d[ob], "ldw", waits=_w(*ev_ws_free_c[wb]))
                    e_xl = DMA("sync", xcb[wb].rearrange("p (a b) -> p a b", a=8),
                               xocb[ob][:, th * 8:(th + 1) * 8, :], "ldx",
                               waits=_w(*ev_xcb_free[wb]))
                    e_evc = {}
                    for hf in range(2):
                        e_mm = None
                        for tc4 in range(4):
                            tc = hf * 4 + tc4
                            for i in range(32):
                                w0 = _w(e_wl, e_cl, ev_psA_free[hf]) if (tc4 == 0 and i == 0) else ()
                                e_mm = MM(psAh[hf][:, tc4 * 256:(tc4 + 1) * 256],
                                          af3[:, i, tc * 128:(tc + 1) * 128],
                                          wsc[wb][:, i * 256:(i + 1) * 256],
                                          i == 0, i == 31, waits=w0,
                                          inc="pec" if (tc4 == 3 and i == 31) else None)
                        e_evc[hf] = ACTI(fbuf[:, hf * 1024:(hf + 1) * 1024], psAh[hf],
                                         AF.Copy, 0.0, 1.0,
                                         waits=_w(e_mm, *ev_fbuf_free[hf]), inc="ac")
                        ev_psA_free[hf] = e_evc[hf]
                    ev_ws_free_c[wb] = ((P.ev("pec")),)
                    for tc in range(8):
                        hf = tc // 4
                        tcg = th * 8 + tc
                        ft = fbuf[:, tc * 256:(tc + 1) * 256]
                        xt = xcb[wb][:, tc * 256:(tc + 1) * 256]
                        x1t = obA[:, tc * 256:(tc + 1) * 256]
                        e_x1 = TT(x1t, ft, xt, ALU.add,
                                  waits=_w(e_evc[hf], e_xl, ev_x1b_free), inc="dv")
                        e_p1 = RED(stat[:, 16:17], x1t, ALU.add, waits=_w(e_x1), inc="dv")
                        e_a1 = TT(sum1[:, tcg:tcg + 1], sum1[:, tcg:tcg + 1],
                                  stat[:, 16:17], ALU.add,
                                  waits=_w(e_p1, *ev_sum[tcg]), inc="dv")
                        e_sq = ACTI(sqbC, x1t, AF.Square, 0.0, 1.0,
                                    waits=_w(e_x1, ev_sq_free), inc="ac")
                        e_p2 = RED(stat[:, 17:18], sqbC, ALU.add, waits=_w(e_sq), inc="dv")
                        ev_sq_free = e_p2
                        e_a2 = TT(sum2[:, tcg:tcg + 1], sum2[:, tcg:tcg + 1],
                                  stat[:, 17:18], ALU.add,
                                  waits=_w(e_p2, *ev_sum[tcg]), inc="dv")
                        ev_sum[tcg] = (e_a1, e_a2)
                    ev_fbuf_free[0] = ((P.ev("dv")),)
                    ev_fbuf_free[1] = ((P.ev("dv")),)
                    e_sx = DMA("scalar", x1_scr[ob][:, th * 8:(th + 1) * 8, :],
                               obA.rearrange("p (a b) -> p a b", a=8), "stx1",
                               waits=_w(*[x for t2 in range(8) for x in ev_sum[th * 8 + t2]]))
                    ev_x1b_free = e_sx
                    ev_xcb_free[wb] = ((P.ev("dv")),)

        ev_ln2fin = [None] * 16
        if "D" in phases:
            for t in range(16):
                e_mu = TS(mu2[:, t:t + 1], sum1[:, t:t + 1], 1.0 / D, None, ALU.mult,
                          ALU.bypass, waits=_w(*(ev_sum[t] or ())), inc="dv")
                e_ms = TT(stat[:, 18:19], mu2[:, t:t + 1], mu2[:, t:t + 1], ALU.mult,
                          waits=_w(e_mu), inc="dv")
                e_va = TS(stat[:, 19:20], sum2[:, t:t + 1], 1.0 / D, stat[:, 18:19],
                          ALU.mult, ALU.subtract,
                          waits=_w(ev_sum[t][1] if ev_sum[t] else None, e_ms), inc="dv")
                e_l = ACTI(stat[:, 20:21], stat[:, 19:20], AF.Ln, eps_ap, 1.0,
                           waits=_w(e_va, EV_CONST), inc="ac")
                e_r = ACTI(rstd2[:, t:t + 1], stat[:, 20:21], AF.Exp, zero_ap, -0.5,
                           waits=_w(e_l, e_mu), inc="ac")
                e_n = TS(nmr2[:, t:t + 1], mu2[:, t:t + 1], rstd2[:, t:t + 1], -1.0,
                         ALU.mult, ALU.mult, waits=_w(e_r, e_va), inc="dv")
                ev_ln2fin[t] = (e_r, e_n)

        if "E" in phases:
            DMA("sync", bias_s[:], bias_d, "ldc")
            EV_BIAS = P.ev("ldc")
            ev_out_store = {0: None, 1: None}
            ev_wf_free = [P.ev("pec"), P.ev("pec")]
            ev_xslE_free = [P.ev("dv"), P.ev("dv")]
            ev_lntmE_free = [P.ev("pes"), P.ev("pes")]
            ev_psA_freeE = [P.ev("ac"), P.ev("ac")]
            ev_fbuf_freeE = [(P.ev("dv"),), (P.ev("dv"),)]
            for th in range(2):
                act_gate = P.ev("pec") if th == 0 else P.ev("pee")
                ev_xcb2_free = [P.ev("dv"), P.ev("dv")]
                for ob in range(16):
                    wb = ob % 2
                    e_xl = DMA("sync", xcb[wb].rearrange("p (a b) -> p a b", a=8),
                               x1_scr[ob][:, th * 8:(th + 1) * 8, :], "ldx",
                               waits=_w(P.ev("stx1"), ev_xcb2_free[wb]))
                    for tc in range(8):
                        tcg = th * 8 + tc
                        lb = tc % 2
                        xt = xcb[wb][:, tc * 256:(tc + 1) * 256]
                        e_apl = ACTI(lntmE_l[lb], xt, AF.Identity,
                                     nmr2[:, tcg:tcg + 1], rstd2[:, tcg:tcg + 1],
                                     waits=_w(e_xl, *(ev_ln2fin[tcg] or ()),
                                              ev_lntmE_free[lb]), inc="ac")
                        blocks = [lntmE_l[lb][:, oh * 128:(oh + 1) * 128]
                                  for oh in range(2)]
                        e_cv = transpose_group(
                            blocks,
                            af3[:, ob * 2:ob * 2 + 2, tc * 128:(tc + 1) * 128],
                            _w(e_apl, act_gate))
                        ev_lntmE_free[lb] = P.ev("pet")
                    ev_xcb2_free[wb] = P.ev("ac")
                e_ln2fm = P.ev("dvt")
                for ob in range(16):
                    wb = ob % 2
                    e_wl = DMA("sync", wsc[wb].rearrange("p (a b) -> p a b", a=32),
                               wf_d[ob], "ldw", waits=_w(ev_wf_free[wb]))
                    e_xl = DMA("sync", xcb[wb].rearrange("p (a b) -> p a b", a=8),
                               x1_scr[ob][:, th * 8:(th + 1) * 8, :], "ldx",
                               waits=_w(ev_xslE_free[wb], P.ev("ac") if ob < 2 else None))
                    e_evf = {}
                    for hf in range(2):
                        e_mm = None
                        for tc4 in range(4):
                            tc = hf * 4 + tc4
                            for i in range(32):
                                w0 = _w(e_wl, e_ln2fm, ev_psA_freeE[hf]) if (tc4 == 0 and i == 0) else ()
                                MM(psAh[hf][:, tc4 * 256:(tc4 + 1) * 256],
                                   af3[:, i, tc * 128:(tc + 1) * 128],
                                   wsc[wb][:, i * 256:(i + 1) * 256],
                                   i == 0, False, waits=w0)
                            e_mm = MM(psAh[hf][:, tc4 * 256:(tc4 + 1) * 256], oner_s[:],
                                      bias_s[0:1, ob * 256:(ob + 1) * 256], False, True,
                                      waits=_w(EV_BIAS) if (tc4 == 0 and ob < 2) else (),
                                      inc="pee" if tc4 == 3 else None)
                        e_evf[hf] = ACTI(fbuf[:, hf * 1024:(hf + 1) * 1024], psAh[hf],
                                         AF.Copy, 0.0, 1.0,
                                         waits=_w(e_mm, *ev_fbuf_freeE[hf]), inc="ac")
                        ev_psA_freeE[hf] = e_evf[hf]
                    ev_wf_free[wb] = P.ev("pee")
                    obuf = obA if ob % 2 == 0 else obB
                    e_o = None
                    for tc in range(8):
                        hf = tc // 4
                        e_o = TT(obuf[:, tc * 256:(tc + 1) * 256],
                                 fbuf[:, tc * 256:(tc + 1) * 256],
                                 xcb[wb][:, tc * 256:(tc + 1) * 256], ALU.add,
                                 waits=_w(e_evf[hf], e_xl, ev_out_store[ob % 2]), inc="dv")
                    ev_fbuf_freeE[0] = ((P.ev("dv")),)
                    ev_fbuf_freeE[1] = ((P.ev("dv")),)
                    ev_xslE_free[wb] = P.ev("dv")
                    e_so = DMA("scalar", out_cb[ob][:, th * 8:(th + 1) * 8, :],
                               obuf.rearrange("p (a b) -> p a b", a=8), "stout",
                               waits=_w(e_o))
                    ev_out_store[ob % 2] = e_so

        final_waits = [P.ev("stout")]
        if debug:
            for s in ("stkv", "stcx", "stx1"):
                final_waits.append(P.ev(s))
        WAIT_ONLY("gpsimd", final_waits)

        semnames = sorted(P.cnt.keys())
        semh = {}
        for n in semnames:
            semh[n] = E(nc.semaphore(n))
        block = E(nc.Block())

        def mk(engname):
            def body(eng):
                last = {}
                for fn, waits, inc, k in P.ops[engname]:
                    for s, v in waits:
                        if last.get(s, -1) < v:
                            eng.wait_ge(semh[s], v)
                            last[s] = v
                    instr = fn(eng)
                    if inc is not None:
                        instr.then_inc(semh[inc], k)
            return body

        block.sync(mk("sync"))
        block.gpsimd(mk("gpsimd"))
        block.tensor(mk("tensor"))
        block.vector(mk("vector"))
        block.scalar(mk("scalar"))

    return nc


def host_inputs(core, L0_attn_x, L0_attn_Wq, L0_attn_Wk, L0_attn_Wv, L0_attn_Wo,
                L0_attn_cos, L0_attn_sin, L0_ffn_weight, L0_ffn_bias):
    b, z = core // 2, core % 2
    chunks = own_chunks(z)
    own_pos = np.concatenate([np.arange(c * 128, (c + 1) * 128) for c in chunks])
    X = np.asarray(L0_attn_x[b], dtype=np.float32)
    cos = np.asarray(L0_attn_cos, np.float32)
    sin = np.asarray(L0_attn_sin, np.float32)

    x_own = np.ascontiguousarray(X[own_pos])
    masks = np.zeros((16, 128, 256), np.float32)
    for s in range(16):
        c = chunks[s]
        kk = np.arange(KS[s] - 256, KS[s])
        qq = c * 128 + np.arange(128)
        masks[s] = np.where(kk[None, :] <= qq[:, None], 0.0, -30000.0)

    d = {
        "xf": X,
        "xo": x_own,
        "xocb": np.ascontiguousarray(
            x_own.reshape(16, 128, 16, 256).transpose(2, 1, 0, 3)),
        "cosf": np.ascontiguousarray(cos.T),
        "sinf": np.ascontiguousarray(sin.T),
        "coso": np.ascontiguousarray(cos[own_pos].T),
        "sino": np.ascontiguousarray(sin[own_pos].T),
        "masks": masks,
        "wq": np.ascontiguousarray(
            np.asarray(L0_attn_Wq, np.float32).reshape(32, 128, 32, 128).transpose(2, 0, 1, 3)),
        "wk": np.ascontiguousarray(
            np.asarray(L0_attn_Wk, np.float32).reshape(32, 128, 8, 128).transpose(2, 0, 1, 3)),
        "wv": np.ascontiguousarray(
            np.asarray(L0_attn_Wv, np.float32).reshape(32, 128, 8, 128).transpose(2, 0, 1, 3)),
        "wo": np.ascontiguousarray(
            np.asarray(L0_attn_Wo, np.float32).reshape(32, 128, 16, 256).transpose(2, 1, 0, 3)),
        "wf": np.ascontiguousarray(
            np.asarray(L0_ffn_weight).T.reshape(32, 128, 16, 256).transpose(2, 1, 0, 3)),
        "biasf": np.asarray(L0_ffn_bias).reshape(1, D),
        "idt": np.eye(128, dtype=np.float16),
        "rt": _rot_matrix_T(),
        "oner": np.ones((1, 128), np.float16),
        "cvec": np.broadcast_to(
            np.array([EPS, 0.0, 1.0, 1.0 / D], np.float32), (128, 4)).copy(),
    }
    return d, own_pos, b


def _rot_matrix_T():
    R = np.zeros((128, 128), np.float32)
    for dd in range(64):
        R[dd, dd + 64] = -1.0
        R[dd + 64, dd] = 1.0
    return R.T.astype(np.float16)


_NC_CACHE = {}


def _get_nc(debug=False, phases=("A1", "A2B", "C", "D", "E")):
    key = (debug, tuple(phases))
    if key not in _NC_CACHE:
        _NC_CACHE[key] = build_program(debug=debug, phases=phases)
    return _NC_CACHE[key]


def kernel(**inputs):
    nc = _get_nc()
    in_maps = []
    metas = []
    for core in range(8):
        d, own_pos, b = host_inputs(core, **inputs)
        in_maps.append(d)
        metas.append((own_pos, b))
    res = run_bass_kernel_spmd(nc, in_maps, core_ids=list(range(8)))
    out = np.empty((4, T, D), np.float32)
    for core in range(8):
        own_pos, b = metas[core]
        oc = np.asarray(res.results[core]["out_cb"], np.float32)
        rows = oc.transpose(2, 1, 0, 3).reshape(TOWN, D)
        out[b][own_pos] = rows
    return out


# version 07
# speedup vs baseline: 1.1365x; 1.1365x over previous
import math
import numpy as np
import ml_dtypes
import concourse.bass as bass
import concourse.mybir as mybir
from concourse.bass_utils import run_bass_kernel_spmd
from contextlib import ExitStack

F32 = mybir.dt.float32
F16 = mybir.dt.float16
AF = mybir.ActivationFunctionType
ALU = mybir.AluOpType
AX = mybir.AxisListType

T = 4096
D = 4096
HEADS = 32
GROUPS = 8
DH = 128
TOWN = 2048
NCH = 32
SCALE = 1.0 / math.sqrt(DH)
EPS = 1e-5
KS = [256 * (s + 1) for s in range(16)]


def own_chunks(z):
    return [c for c in range(NCH) if c % 4 in ((0, 3) if z == 0 else (1, 2))]


class Plan:
    ENGINES = ("sync", "gpsimd", "tensor", "vector", "scalar")

    def __init__(self):
        self.ops = {e: [] for e in self.ENGINES}
        self.cnt = {}

    def emit(self, eng, fn, waits=(), inc=None, k=1):
        ws = tuple((s, v) for (s, v) in waits if v > 0)
        self.ops[eng].append((fn, ws, inc, k))
        if inc is None:
            return None
        self.cnt[inc] = self.cnt.get(inc, 0) + k
        return (inc, self.cnt[inc])

    def ev(self, name):
        return (name, self.cnt.get(name, 0))


def _w(*evs):
    d = {}
    for e in evs:
        if e is None:
            continue
        s, v = e
        if d.get(s, -1) < v:
            d[s] = v
    return tuple(d.items())


def build_program(debug=False, phases=("A1", "A2B", "C", "D", "E")):
    nc = bass.Bass("TRN2", target_bir_lowering=False)
    P = Plan()
    scratch_kind = "ExternalOutput" if debug else "Internal"

    xf = nc.dram_tensor("xf", [T, D], F32, kind="ExternalInput").ap()
    xo = nc.dram_tensor("xo", [TOWN, D], F32, kind="ExternalInput").ap()
    xocb = nc.dram_tensor("xocb", [16, 128, 16, 256], F32, kind="ExternalInput").ap()
    cosf_d = nc.dram_tensor("cosf", [128, T], F32, kind="ExternalInput").ap()
    sinf_d = nc.dram_tensor("sinf", [128, T], F32, kind="ExternalInput").ap()
    coso_d = nc.dram_tensor("coso", [128, TOWN], F32, kind="ExternalInput").ap()
    sino_d = nc.dram_tensor("sino", [128, TOWN], F32, kind="ExternalInput").ap()
    masks_d = nc.dram_tensor("masks", [16, 128, 256], F32, kind="ExternalInput").ap()
    wq_d = nc.dram_tensor("wq", [32, 32, 128, 128], F32, kind="ExternalInput").ap()
    wk_d = nc.dram_tensor("wk", [8, 32, 128, 128], F32, kind="ExternalInput").ap()
    wv_d = nc.dram_tensor("wv", [8, 32, 128, 128], F32, kind="ExternalInput").ap()
    wo_d = nc.dram_tensor("wo", [16, 128, 32, 256], F32, kind="ExternalInput").ap()
    wf_d = nc.dram_tensor("wf", [16, 128, 32, 256], F16, kind="ExternalInput").ap()
    bias_d = nc.dram_tensor("biasf", [1, D], F16, kind="ExternalInput").ap()
    idt_d = nc.dram_tensor("idt", [128, 128], F16, kind="ExternalInput").ap()
    rt_d = nc.dram_tensor("rt", [128, 128], F16, kind="ExternalInput").ap()
    oner_d = nc.dram_tensor("oner", [1, 128], F16, kind="ExternalInput").ap()
    cvec_d = nc.dram_tensor("cvec", [128, 4], F32, kind="ExternalInput").ap()

    k_scr = nc.dram_tensor("k_scr", [GROUPS, 128, T], F16, kind=scratch_kind).ap()
    v_scr = nc.dram_tensor("v_scr", [GROUPS, 128, T], F16, kind=scratch_kind).ap()
    ctx_scr = nc.dram_tensor("ctx_scr", [HEADS, 128, TOWN], F16, kind=scratch_kind).ap()
    x1_scr = nc.dram_tensor("x1_scr", [16, 128, 16, 256], F32, kind=scratch_kind).ap()
    out_cb = nc.dram_tensor("out_cb", [16, 128, 16, 256], F32, kind="ExternalOutput").ap()

    with ExitStack() as ctx:
        E = ctx.enter_context

        act = E(nc.sbuf_tensor("act", [128, 32 * 1024], F16))
        af3 = act[:].rearrange("p (c t) -> p c t", c=32)
        AR16 = E(nc.sbuf_tensor("AR16", [128, 38912], F16))
        AR32 = E(nc.sbuf_tensor("AR32", [128, 14464], F32))
        idt_s = E(nc.sbuf_tensor("idt_s", [128, 128], F16))
        rt_s = E(nc.sbuf_tensor("rt_s", [128, 128], F16))
        oner_s = E(nc.sbuf_tensor("oner_s", [1, 128], F16))
        cvec_s = E(nc.sbuf_tensor("cvec_s", [128, 4], F32))
        bias_s = E(nc.sbuf_tensor("bias_s", [1, D], F16))

        def v16(off, n):
            assert off + n <= 38912, (off, n)
            return AR16[:, off:off + n]

        def v32(off, n):
            assert off + n <= 14464, (off, n)
            return AR32[:, off:off + n]

        lntm = v16(0, 4096)
        sqb = v16(4096, 4096)
        lntm_l = [v16(0, 4096), v16(16384, 4096)]
        sqb_l = [v16(4096, 4096), v16(20480, 4096)]
        ev_lntp = [None, None]
        ev_sqrd = {}
        wsb = [v16(8192, 4096), v16(12288, 4096)]
        cosf = v16(24576, 4096)
        sinf = v16(28672, 4096)
        kev = v16(32768, 1024)
        r1b = v16(33792, 1024)
        r2b = v16(34816, 1024)
        kgb = v16(0, 4096)
        vgf = v16(4096, 4096)
        pbufb = v16(16384, 2048)
        ptbb = v16(18432, 2048)
        qraw = v16(22528, 1024)
        coso = v16(24576, 2048)
        sino = v16(26624, 2048)
        masksb = v16(28672, 4096)
        vtm = v16(32768, 4160)
        vt3 = vtm.rearrange("p (c n) -> p c n", n=130)
        ctxhb_l = [v16(36928, 1024), v16(21504, 1024)]
        cbuf_l = [v16(37952, 128), v16(38592, 128)]
        qh_l = [v16(23552, 1024), v16(20480, 1024)]
        r2q = v16(38080, 512)
        wsc = [v16(8192, 8192), v16(16384, 8192)]
        lntmE_l = [v16(0, 256), v16(256, 256)]
        sqbC = v16(4096, 256)

        xsl = [v32(0, 4096), v32(4096, 4096)]
        stat = v32(8192, 96)
        fbuf = v32(8288, 2048)
        obA = v32(10336, 2048)
        obB = v32(12384, 2048)
        xcb = [v32(0, 2048), v32(2048, 2048)]
        ctxA_l = [v32(4096, 132), v32(4228, 132)]
        sum1 = stat[:, 32:48]
        sum2 = stat[:, 48:64]
        mu2 = stat[:, 64:80]
        rstd2 = stat[:, 32:48]
        nmr2 = stat[:, 48:64]

        psA = E(nc.psum_tensor("psA", [128, 2048], F32))
        psB = E(nc.psum_tensor("psB", [128, 512], F32))
        psC = E(nc.psum_tensor("psC", [128, 512], F32))
        psT = E(nc.psum_tensor("psT", [128, 1024], F16))
        psT2 = E(nc.psum_tensor("psT2", [128, 1024], F16))
        psAh = [psA[:, 0:1024], psA[:, 1024:2048]]
        psTb = [psT, psT2]
        ctxb = [psC, psB]

        eps_ap = cvec_s[:, 0:1]
        zero_ap = cvec_s[:, 1:2]

        def DMA(eng, dst, src, inc, waits=()):
            return P.emit(eng, lambda e, d=dst, s=src: e.dma_start(out=d, in_=s),
                          _w(*waits), inc, 16)

        def MM(out, lhsT, rhs, start, stop, waits=(), inc=None):
            return P.emit("tensor",
                          lambda e, o=out, l=lhsT, r=rhs, st=start, sp=stop:
                          e.matmul(o, l, r, start=st, stop=sp), _w(*waits), inc)

        def TP(out, in_, waits=(), inc=None):
            return P.emit("tensor",
                          lambda e, o=out, i=in_: e.transpose(o, i, idt_s[:]),
                          _w(*waits), inc)

        def ACTI(out, in_, func, bias, scale, waits=(), inc=None):
            return P.emit("scalar",
                          lambda e, o=out, i=in_, f=func, b=bias, s=scale:
                          e.activation(o, i, f, bias=b, scale=s), _w(*waits), inc)

        def TT(out, a, b, op, waits=(), inc=None, eng="vector"):
            return P.emit(eng,
                          lambda e, o=out, x=a, y=b, p=op: e.tensor_tensor(o, x, y, op=p),
                          _w(*waits), inc, 1)

        def TS(out, in0, s1, s2, op0, op1, waits=(), inc=None, eng="vector"):
            return P.emit(eng,
                          lambda e, o=out, i=in0, a=s1, b=s2, p0=op0, p1=op1:
                          e.tensor_scalar(o, i, a, b, op0=p0, op1=p1), _w(*waits), inc, 1)

        def RED(out, in_, op, waits=(), inc=None):
            return P.emit("vector",
                          lambda e, o=out, i=in_, p=op: e.tensor_reduce(o, i, axis=AX.X, op=p),
                          _w(*waits), inc)

        def CP(out, in_, waits=(), inc=None):
            return P.emit("vector", lambda e, o=out, i=in_: e.tensor_copy(o, i),
                          _w(*waits), inc)

        def RECIP(out, in_, waits=(), inc=None):
            return P.emit("vector", lambda e, o=out, i=in_: e.reciprocal(o, i),
                          _w(*waits), inc)

        def MS(ap, val, waits=(), inc=None):
            return P.emit("gpsimd", lambda e, a=ap, v=val: e.memset(a, v),
                          _w(*waits), inc)

        def WAIT_ONLY(eng, waits):
            P.emit(eng, lambda e: None, _w(*waits), None)

        DMA("sync", idt_s[:], idt_d, "ldc")
        DMA("sync", rt_s[:], rt_d, "ldc")
        DMA("sync", oner_s[:], oner_d, "ldc")
        DMA("sync", cvec_s[:], cvec_d, "ldc")
        DMA("gpsimd", cosf, cosf_d, "ldc")
        DMA("gpsimd", sinf, sinf_d, "ldc")
        EV_CONST = P.ev("ldc")

        tp_war = [None, None]
        tp_bank = [0]
        sqb_rd = [None]

        def ln_slab(xbuf, out16, waits, sqbuf=None, stcol=0, out_war=None, sq_war=None):
            if sqbuf is None:
                sqbuf = sqb
            if out_war is None:
                out_war = P.ev("pet")
            c = stcol
            e_s1 = RED(stat[:, c:c + 1], xbuf, ALU.add, waits=waits, inc="dv")
            e_sq = ACTI(sqbuf, xbuf, AF.Square, 0.0, 1.0,
                        waits=waits + _w(sq_war), inc="ac")
            e_s2 = RED(stat[:, c + 1:c + 2], sqbuf, ALU.add, waits=(e_sq,), inc="dv")
            sqb_rd[0] = e_s2
            e_mu = TS(stat[:, c + 2:c + 3], stat[:, c:c + 1], 1.0 / D, None, ALU.mult,
                      ALU.bypass, waits=(e_s1,), inc="dv")
            e_msq = TT(stat[:, c + 3:c + 4], stat[:, c + 2:c + 3], stat[:, c + 2:c + 3],
                       ALU.mult, waits=(e_mu,), inc="dv")
            e_var = TS(stat[:, c + 4:c + 5], stat[:, c + 1:c + 2], 1.0 / D,
                       stat[:, c + 3:c + 4], ALU.mult,
                       ALU.subtract, waits=(e_s2, e_msq), inc="dv")
            e_l = ACTI(stat[:, c + 5:c + 6], stat[:, c + 4:c + 5], AF.Ln, eps_ap, 1.0,
                       waits=(e_var, EV_CONST), inc="ac")
            e_r = ACTI(stat[:, c + 6:c + 7], stat[:, c + 5:c + 6], AF.Exp, zero_ap, -0.5,
                       waits=(e_l,), inc="ac")
            e_n = TS(stat[:, c + 7:c + 8], stat[:, c + 2:c + 3], stat[:, c + 6:c + 7],
                     -1.0, ALU.mult, ALU.mult, waits=(e_r, e_mu), inc="dv")
            e_ap = ACTI(out16, xbuf, AF.Identity, stat[:, c + 7:c + 8],
                        stat[:, c + 6:c + 7],
                        waits=(e_n, e_r) + _w(out_war), inc="ac")
            return e_ap

        def transpose_group(blocks, dst, waits, evac="vector"):
            bank = tp_bank[0] % 2
            tp_bank[0] += 1
            pt = psTb[bank][:, 0:len(blocks) * 128]
            tp_ev = None
            for j, b in enumerate(blocks):
                tp_ev = TP(pt[:, j * 128:(j + 1) * 128], b,
                           waits=(waits + _w(tp_war[bank])) if j == 0 else (),
                           inc="pet")
            src = pt.rearrange("p (c t) -> p c t", c=len(blocks))
            if evac == "scalar":
                ev = P.emit("scalar",
                            lambda e, o=dst, i=src: e.activation(o, i, AF.Copy,
                                                                 bias=0.0, scale=1.0),
                            _w(tp_ev), "dvt")
            else:
                ev = CP(dst, src, waits=_w(tp_ev), inc="dvt")
            tp_war[bank] = ev
            return ev

        def transpose_slab(src16, tcol, waits, war_mm):
            last = None
            for icq in range(8):
                blocks = [src16[:, (icq * 4 + j) * 128:(icq * 4 + j + 1) * 128]
                          for j in range(4)]
                dst = af3[:, icq * 4:(icq + 1) * 4, tcol:tcol + 128]
                last = transpose_group(blocks, dst, waits + _w(*war_mm))
            return last

        ev_apply = {}
        ev_kv_mm_last = None
        ev_psA_free = [None, None]
        ev_cb_free = [None, None]
        ev_ws_free = [None, None]
        ev_kev_free = None

        if "A1" in phases:
            for TC in range(4):
                tcols = slice(TC * 1024, (TC + 1) * 1024)
                for sub in range(8):
                    slab = TC * 8 + sub
                    buf = slab % 2
                    e_ld = DMA("sync", xsl[buf], xf[slab * 128:(slab + 1) * 128, :],
                               "ldx", waits=_w(ev_apply.get(slab - 2)))
                    e_ap = ln_slab(xsl[buf], lntm_l[buf], _w(e_ld), sqbuf=sqb_l[buf],
                                   stcol=buf * 8, out_war=ev_lntp[buf],
                                   sq_war=ev_sqrd.get(buf))
                    ev_sqrd[buf] = sqb_rd[0]
                    ev_apply[slab] = e_ap
                    transpose_slab(lntm_l[buf], sub * 128, _w(e_ap), (ev_kv_mm_last,))
                    ev_lntp[buf] = P.ev("pet")
                e_lnfm = P.ev("dvt")
                for idx in range(16):
                    kv, g = idx // 8, idx % 8
                    wsrc = (wk_d if kv == 0 else wv_d)[g]
                    wb = idx % 2
                    e_wl = DMA("gpsimd", wsb[wb].rearrange("p (a b) -> p a b", a=32),
                               wsrc.rearrange("a p b -> p a b"), "ldw",
                               waits=_w(ev_ws_free[wb]))
                    par = idx % 2
                    pa = psAh[par]
                    e_mm = None
                    for i in range(32):
                        w0 = _w(e_wl, e_lnfm, ev_psA_free[par]) if i == 0 else ()
                        MM(pa[:, 0:512], wsb[wb][:, i * 128:(i + 1) * 128],
                           af3[:, i, 0:512], i == 0, i == 31, waits=w0)
                        e_mm = MM(pa[:, 512:1024], wsb[wb][:, i * 128:(i + 1) * 128],
                                  af3[:, i, 512:1024], i == 0, i == 31,
                                  inc="pea" if i == 31 else None)
                    ev_ws_free[wb] = e_mm
                    e_ev = ACTI(kev, pa, AF.Copy, 0.0, 1.0,
                                waits=_w(e_mm, ev_kev_free), inc="ac")
                    ev_psA_free[par] = e_ev
                    if kv == 0:
                        e_r1 = []
                        for jj in range(2):
                            cb = ctxb[jj]
                            e_rk = MM(cb[:, 0:512], rt_s[:],
                                      kev[:, jj * 512:(jj + 1) * 512], True, True,
                                      waits=_w(e_ev, ev_cb_free[jj]), inc="per")
                            scols = slice(TC * 1024 + jj * 512, TC * 1024 + (jj + 1) * 512)
                            e1 = TT(r1b[:, jj * 512:(jj + 1) * 512], cb[:, 0:512],
                                    sinf[:, scols], ALU.mult,
                                    waits=_w(e_rk, EV_CONST), inc="dv")
                            ev_cb_free[jj] = e1
                            e_r1.append(e1)
                        e_r2 = TT(r2b, kev, cosf[:, tcols], ALU.mult,
                                  waits=_w(e_ev, EV_CONST), inc="dv")
                        e_kr = TT(kev, r1b, r2b, ALU.add,
                                  waits=_w(*e_r1, e_r2), inc="dv")
                        e_st = DMA("scalar", k_scr[g][:, tcols], kev, "stkv", waits=_w(e_kr))
                    else:
                        e_st = DMA("scalar", v_scr[g][:, tcols], kev, "stkv", waits=_w(e_ev))
                    ev_kev_free = e_st
                ev_kv_mm_last = P.ev("pea")

        ev_qproj_last = None
        ev_ctx_store = [None, None]
        ev_cbuf_free = [None, None]
        ev_qh_war = [None, None]
        ev_kg_free = None
        ev_vg_free = None
        ev_vtm_free = None
        ev_qraw_free = [None, None]
        ev_pbuf_free = [None, None]
        ev_ptb_free = [None, None]
        ev_kvready = (None, None)
        ev_maskload = None

        if "A2B" in phases:
            DMA("gpsimd", coso, coso_d, "ldc", waits=_w(P.ev("dv")))
            DMA("gpsimd", sino, sino_d, "ldc")
            ev_maskload = DMA("gpsimd", masksb.rearrange("p (s n) -> p s n", s=16),
                              masks_d.rearrange("s p n -> p s n"), "ldc",
                              waits=_w(P.ev("dv")))
            EV_BC = P.ev("ldc")

            for ph in range(2):
                gate_ac = P.ev("ac")
                gate_gp = P.ev("gp")
                for sub in range(8):
                    slab = ph * 8 + sub
                    buf = sub % 2
                    war = ev_apply.get(("o", ph, sub - 2))
                    e_ld = DMA("sync", xsl[buf], xo[slab * 128:(slab + 1) * 128, :],
                               "ldx", waits=_w(war, gate_ac if sub < 2 else None))
                    e_ap = ln_slab(xsl[buf], lntm, _w(e_ld), stcol=0,
                                   out_war=ev_lntp[0] if sub else None,
                                   sq_war=ev_sqrd.get("a2"))
                    ev_sqrd["a2"] = sqb_rd[0]
                    ev_apply[("o", ph, sub)] = e_ap
                    transpose_slab(lntm, sub * 128, _w(e_ap),
                                   (ev_kv_mm_last, ev_qproj_last))
                    ev_lntp[0] = P.ev("pet")
                e_lnfm = P.ev("dvt")

                for hp in range(HEADS // 2):
                    g = (2 * hp) // 4
                    e_wl = [None, None]
                    for lane in range(2):
                        h = 2 * hp + lane
                        e_wl[lane] = DMA("gpsimd",
                                         wsb[lane].rearrange("p (a b) -> p a b", a=32),
                                         wq_d[h].rearrange("a p b -> p a b"), "ldw",
                                         waits=_w(ev_ws_free[lane]))
                    if hp % 2 == 0:
                        a2done = (e_lnfm, P.ev("dv") if hp == 0 else None,
                                  P.ev("ac") if hp == 0 else None,
                                  P.ev("stkv") if (ph == 0 and hp == 0) else None)
                        e_kg = DMA("sync", kgb, k_scr[g], "ldkv",
                                   waits=_w(ev_kg_free, *a2done))
                        e_vg = DMA("sync", vgf, v_scr[g], "ldkv",
                                   waits=_w(ev_vg_free, *a2done))
                        e_msv = MS(vtm, 1.0, waits=_w(ev_vtm_free,
                                   P.ev("dv") if hp == 0 else None), inc="gp")
                        e_vt = None
                        for kq in range(8):
                            blocks = [vgf[:, (kq * 4 + j) * 128:(kq * 4 + j + 1) * 128]
                                      for j in range(4)]
                            e_vt = transpose_group(
                                blocks, vt3[:, kq * 4:(kq + 1) * 4, 0:128],
                                _w(e_vg, e_msv))
                        ev_kvready = (e_kg, e_vt)
                    e_qh_l = [None, None]
                    for lane in range(2):
                        qhb = qh_l[lane]
                        for tcq in range(2):
                            qc = slice(tcq * 512, (tcq + 1) * 512)
                            e_mm = None
                            for i in range(32):
                                w0 = _w(e_wl[lane], e_lnfm, ev_cb_free[1]) if i == 0 else ()
                                e_mm = MM(psB[:, 0:512],
                                          wsb[lane][:, i * 128:(i + 1) * 128],
                                          af3[:, i, qc], i == 0, i == 31, waits=w0,
                                          inc="peq" if i == 31 else None)
                            e_qe = ACTI(qraw[:, qc], psB[:, 0:512], AF.Copy, 0.0, 1.0,
                                        waits=_w(e_mm, *(ev_qraw_free[tcq] or ())),
                                        inc="ac")
                            ev_cb_free[1] = e_qe
                            e_rq = MM(psC[:, 0:512], rt_s[:], qraw[:, qc], True, True,
                                      waits=_w(e_qe, ev_cb_free[0]), inc="per")
                            ocs = slice(ph * 1024 + tcq * 512, ph * 1024 + (tcq + 1) * 512)
                            e_q1 = TT(qhb[:, qc], psC[:, 0:512], sino[:, ocs], ALU.mult,
                                      waits=_w(e_rq, EV_BC, ev_qh_war[lane]), inc="dv")
                            ev_cb_free[0] = e_q1
                            e_q2 = TT(r2q, qraw[:, qc], coso[:, ocs], ALU.mult,
                                      waits=_w(e_qe, EV_BC), inc="dv")
                            e_qh_l[lane] = TT(qhb[:, qc], qhb[:, qc], r2q, ALU.add,
                                              waits=_w(e_q1, e_q2), inc="dv")
                            ev_qraw_free[tcq] = (e_rq, e_q2)

                    subrows = []
                    for ls in range(8):
                        s = ph * 8 + ls
                        ks = KS[s]
                        n = (ks + 1023) // 1024
                        for r in range(n):
                            subrows.append(dict(
                                s=s, ls=ls, r=r, k0=r * 1024,
                                k1=min((r + 1) * 1024, ks), ks=ks,
                                first=(r == 0), last=(r == n - 1), nsub=n))

                    slot_state = [{}, {}]
                    sr_l = [None, None]

                    def emit_S(lane, srp, kvr=ev_kvready, e_qh_l=e_qh_l):
                        sr = dict(srp)
                        par = lane
                        nk = sr["k1"] - sr["k0"]
                        nj = (nk + 511) // 512
                        qtile = qh_l[lane][:, sr["ls"] * 128:(sr["ls"] + 1) * 128]
                        e_chunks = []
                        for jj in range(nj):
                            w = min(512, nk - jj * 512)
                            w0 = _w(e_qh_l[lane], kvr[0], ev_psA_free[par]) if jj == 0 else ()
                            e_chunks.append(
                                MM(psAh[par][:, jj * 512:jj * 512 + w], qtile,
                                   kgb[:, sr["k0"] + jj * 512:sr["k0"] + jj * 512 + w],
                                   True, True, waits=w0, inc="pes"))
                        sr["e_chunks"] = e_chunks
                        return sr

                    def emit_max(lane, sr):
                        par = lane
                        nk = sr["k1"] - sr["k0"]
                        nj = (nk + 511) // 512
                        e_chunks = sr["e_chunks"]
                        e_s = e_chunks[-1]
                        e_mask = None
                        if sr["last"]:
                            lo = sr["ks"] - 256 - sr["k0"]
                            mreg = psAh[par][:, lo:lo + 256]
                            e_mask = TT(mreg, mreg,
                                        masksb[:, sr["s"] * 256:(sr["s"] + 1) * 256],
                                        ALU.add, waits=_w(e_s, ev_maskload), inc="dv")
                        if nj == 2:
                            e_m0 = RED(stat[:, 16 + par:17 + par], psAh[par][:, 0:512],
                                       ALU.max, waits=_w(e_chunks[0]), inc="dv")
                            e_m1 = RED(stat[:, 8 + par:9 + par],
                                       psAh[par][:, 512:nk], ALU.max,
                                       waits=_w(e_s, e_mask), inc="dv")
                            e_max = TT(stat[:, 8 + par:9 + par], stat[:, 8 + par:9 + par],
                                       stat[:, 16 + par:17 + par], ALU.max,
                                       waits=_w(e_m0, e_m1), inc="dv")
                        else:
                            e_max = RED(stat[:, 8 + par:9 + par], psAh[par][:, 0:nk],
                                        ALU.max, waits=_w(e_s, e_mask), inc="dv")
                        st0 = slot_state[lane]
                        mbase = 10 if lane == 0 else 24
                        if sr["first"]:
                            mcol = mbase
                            old = mbase
                            e_mrun = TS(stat[:, mbase:mbase + 1], stat[:, 8 + par:9 + par],
                                        1.0, None, ALU.mult, ALU.bypass,
                                        waits=_w(e_max), inc="gp", eng="gpsimd")
                        else:
                            mcol = mbase + sr["r"] % 2
                            old = st0["mcol"]
                            e_mrun = TT(stat[:, mcol:mcol + 1], stat[:, old:old + 1],
                                        stat[:, 8 + par:9 + par], ALU.max,
                                        waits=_w(st0["e_mrun"], e_max), inc="dv")
                        e_nb = TS(stat[:, 14 + par:15 + par], stat[:, mcol:mcol + 1],
                                  -SCALE, None, ALU.mult, ALU.bypass,
                                  waits=_w(e_mrun), inc="gp", eng="gpsimd")
                        if not sr["first"]:
                            sr["e_fd"] = TT(stat[:, 12 + par:13 + par],
                                            stat[:, old:old + 1],
                                            stat[:, mcol:mcol + 1], ALU.subtract,
                                            waits=_w(e_mrun), inc="gp", eng="gpsimd")
                        st0["mcol"] = mcol
                        st0["e_mrun"] = e_mrun
                        sr["e_nb"] = e_nb
                        sr["e_mrun"], sr["mcol"] = e_mrun, mcol

                    def emit_exp(lane, sr):
                        par = lane
                        nk = sr["k1"] - sr["k0"]
                        e_exp = ACTI(pbufb[:, par * 1024:par * 1024 + nk],
                                     psAh[par][:, 0:nk], AF.Exp,
                                     stat[:, 14 + par:15 + par], SCALE,
                                     waits=_w(sr["e_nb"], ev_pbuf_free[par]), inc="ac")
                        ev_psA_free[par] = e_exp
                        if not sr["first"]:
                            e_f = ACTI(stat[:, 28 + par:29 + par],
                                       stat[:, 12 + par:13 + par], AF.Exp,
                                       zero_ap, SCALE, waits=_w(sr["e_fd"]), inc="ac")
                            sr["e_f"] = e_f
                            ev_psA_free[par] = e_f
                        sr["e_exp"] = e_exp

                    def emit_TP(lane, sr):
                        par = lane
                        nk = sr["k1"] - sr["k0"]
                        nkb = nk // 128
                        ev_evac = None
                        for kq in range((nkb + 3) // 4):
                            nblk = min(4, nkb - kq * 4)
                            blocks = [pbufb[:, par * 1024 + (kq * 4 + jj) * 128:
                                            par * 1024 + (kq * 4 + jj + 1) * 128]
                                      for jj in range(nblk)]
                            dst = ptbb[:, par * 1024 + kq * 512:
                                       par * 1024 + kq * 512 + nblk * 128]
                            ev_evac = transpose_group(
                                blocks, dst.rearrange("p (c t) -> p c t", c=nblk),
                                _w(sr["e_exp"], ev_ptb_free[par]),
                                evac="scalar" if (tp_bank[0] % 2) else "vector")
                        ev_pbuf_free[par] = P.ev("pet")
                        sr["ev_evac"] = ev_evac

                    def emit_PV_combine(lane, kvr=ev_kvready, sr=None):
                        par = lane
                        nk = sr["k1"] - sr["k0"]
                        nkb = nk // 128
                        ev_evac = sr["ev_evac"]
                        cb = ctxb[lane]
                        ctxA = ctxA_l[lane]
                        e_pv = None
                        for kc in range(nkb):
                            w0 = _w(ev_evac, kvr[1], ev_cb_free[lane]) if kc == 0 else ()
                            e_pv = MM(cb[:, 0:129],
                                      ptbb[:, par * 1024 + kc * 128:par * 1024 + (kc + 1) * 128],
                                      vt3[:, sr["k0"] // 128 + kc, 0:129],
                                      kc == 0, kc == nkb - 1, waits=w0,
                                      inc="pev" if kc == nkb - 1 else None)
                        ev_ptb_free[par] = e_pv
                        st0 = slot_state[lane]
                        if sr["nsub"] > 1:
                            if sr["first"]:
                                e_c = CP(ctxA[:, 0:129], cb[:, 0:129],
                                         waits=_w(e_pv, st0.get("e_ctxA_rd")),
                                         inc="dv")
                            else:
                                e_sc = TS(ctxA[:, 0:129], ctxA[:, 0:129],
                                          stat[:, 28 + par:29 + par],
                                          None, ALU.mult, ALU.bypass,
                                          waits=_w(sr["e_f"], st0["e_ctx"]),
                                          inc="gp", eng="gpsimd")
                                e_c = TT(ctxA[:, 0:129], ctxA[:, 0:129], cb[:, 0:129],
                                         ALU.add, waits=_w(e_sc, e_pv), inc="dv")
                            st0["e_ctx"] = e_c
                            ev_cb_free[lane] = e_c
                        if sr["last"]:
                            if sr["nsub"] == 1:
                                lsrc, csrc, e_fin = cb[:, 128:129], cb[:, 0:128], _w(e_pv)
                            else:
                                lsrc, csrc, e_fin = (ctxA[:, 128:129], ctxA[:, 0:128],
                                                     _w(st0["e_ctx"]))
                            rlc = 30 + lane
                            e_rl = RECIP(stat[:, rlc:rlc + 1], lsrc, waits=e_fin, inc="dv")
                            e_cf = ACTI(cbuf_l[lane], csrc, AF.Identity, zero_ap,
                                        stat[:, rlc:rlc + 1],
                                        waits=_w(e_rl, ev_cbuf_free[lane]), inc="ac")
                            if sr["nsub"] == 1:
                                ev_cb_free[lane] = e_cf
                            else:
                                st0["e_ctxA_rd"] = e_cf
                            e_cv = transpose_group(
                                [cbuf_l[lane]],
                                ctxhb_l[lane][:, sr["ls"] * 128:(sr["ls"] + 1) * 128]
                                .rearrange("p (c t) -> p c t", c=1),
                                _w(e_cf, ev_ctx_store[lane]))
                            ev_cbuf_free[lane] = e_cv
                            sr["e_cv"] = e_cv

                    have_prev = False
                    e_cv_last = [None, None]
                    prev_srs = [None, None]
                    for srp in subrows:
                        cur = [emit_S(0, srp), emit_S(1, srp)]
                        emit_max(0, cur[0])
                        emit_max(1, cur[1])
                        if have_prev:
                            for lane in range(2):
                                emit_TP(lane, prev_srs[lane])
                                emit_PV_combine(lane, sr=prev_srs[lane])
                                if prev_srs[lane].get("e_cv"):
                                    e_cv_last[lane] = prev_srs[lane]["e_cv"]
                        emit_exp(0, cur[0])
                        emit_exp(1, cur[1])
                        prev_srs = cur
                        have_prev = True
                    emit_TP(0, prev_srs[0])
                    emit_TP(1, prev_srs[1])
                    for lane in range(2):
                        emit_PV_combine(lane, sr=prev_srs[lane])
                        e_cv_last[lane] = prev_srs[lane]["e_cv"]

                    for lane in range(2):
                        h = 2 * hp + lane
                        e_cs = DMA("scalar", ctx_scr[h][:, ph * 1024:(ph + 1) * 1024],
                                   ctxhb_l[lane], "stcx", waits=_w(e_cv_last[lane]))
                        ev_ctx_store[lane] = e_cs
                        ev_qh_war[lane] = P.ev("pes")
                    ev_ws_free[0] = P.ev("peq")
                    ev_ws_free[1] = P.ev("peq")
                    if hp % 2 == 1:
                        ev_kg_free = P.ev("pes")
                        ev_vg_free = P.ev("pet")
                        ev_vtm_free = P.ev("pev")
                ev_qproj_last = P.ev("peq")

        ev_sum = [None] * 16
        if "C" in phases:
            gate_c = (P.ev("pet"), P.ev("pev"), P.ev("pes"), P.ev("per"),
                      P.ev("dv"), P.ev("ac"))
            e_z1 = MS(sum1, 0.0, waits=_w(P.ev("dv"), P.ev("ac")), inc="gp")
            e_z2 = MS(sum2, 0.0, inc="gp")
            for t in range(16):
                ev_sum[t] = (e_z1, e_z2)
            ev_ws_free_c = [gate_c, gate_c]
            ev_xcb_free = [(P.ev("ac"),), (P.ev("ac"),)]
            ev_fbuf_free = [(P.ev("ac"),), (P.ev("ac"),)]
            ev_psA_free = [P.ev("ac"), P.ev("ac")]
            ev_x1b_free = None
            ev_sq_free = sqb_rd[0]
            for th in range(2):
                act_gate = P.ev("peq") if th == 0 else P.ev("pec")
                e_cl = None
                for q8 in range(4):
                    src = ctx_scr[q8 * 8:(q8 + 1) * 8, :, th * 1024:(th + 1) * 1024]
                    e_cl = DMA("sync", af3[:, q8 * 8:(q8 + 1) * 8, :],
                               src.rearrange("h p t -> p h t"), "ldcx",
                               waits=_w(P.ev("stcx"), act_gate) if q8 == 0 else ())
                for ob in range(16):
                    wb = ob % 2
                    e_wl = DMA("gpsimd", wsc[wb].rearrange("p (a b) -> p a b", a=32),
                               wo_d[ob], "ldw", waits=_w(*ev_ws_free_c[wb]))
                    e_xl = DMA("sync", xcb[wb].rearrange("p (a b) -> p a b", a=8),
                               xocb[ob][:, th * 8:(th + 1) * 8, :], "ldx",
                               waits=_w(*ev_xcb_free[wb]))
                    e_evc = {}
                    for hf in range(2):
                        e_mm = None
                        for tc4 in range(4):
                            tc = hf * 4 + tc4
                            for i in range(32):
                                w0 = _w(e_wl, e_cl, ev_psA_free[hf]) if (tc4 == 0 and i == 0) else ()
                                e_mm = MM(psAh[hf][:, tc4 * 256:(tc4 + 1) * 256],
                                          af3[:, i, tc * 128:(tc + 1) * 128],
                                          wsc[wb][:, i * 256:(i + 1) * 256],
                                          i == 0, i == 31, waits=w0,
                                          inc="pec" if (tc4 == 3 and i == 31) else None)
                        e_evc[hf] = ACTI(fbuf[:, hf * 1024:(hf + 1) * 1024], psAh[hf],
                                         AF.Copy, 0.0, 1.0,
                                         waits=_w(e_mm, *ev_fbuf_free[hf]), inc="ac")
                        ev_psA_free[hf] = e_evc[hf]
                    ev_ws_free_c[wb] = ((P.ev("pec")),)
                    for tc in range(8):
                        hf = tc // 4
                        tcg = th * 8 + tc
                        ft = fbuf[:, tc * 256:(tc + 1) * 256]
                        xt = xcb[wb][:, tc * 256:(tc + 1) * 256]
                        x1t = obA[:, tc * 256:(tc + 1) * 256]
                        e_x1 = TT(x1t, ft, xt, ALU.add,
                                  waits=_w(e_evc[hf], e_xl, ev_x1b_free), inc="dv")
                        e_p1 = RED(stat[:, 16:17], x1t, ALU.add, waits=_w(e_x1), inc="dv")
                        e_a1 = TT(sum1[:, tcg:tcg + 1], sum1[:, tcg:tcg + 1],
                                  stat[:, 16:17], ALU.add,
                                  waits=_w(e_p1, *ev_sum[tcg]), inc="dv")
                        e_sq = ACTI(sqbC, x1t, AF.Square, 0.0, 1.0,
                                    waits=_w(e_x1, ev_sq_free), inc="ac")
                        e_p2 = RED(stat[:, 17:18], sqbC, ALU.add, waits=_w(e_sq), inc="dv")
                        ev_sq_free = e_p2
                        e_a2 = TT(sum2[:, tcg:tcg + 1], sum2[:, tcg:tcg + 1],
                                  stat[:, 17:18], ALU.add,
                                  waits=_w(e_p2, *ev_sum[tcg]), inc="dv")
                        ev_sum[tcg] = (e_a1, e_a2)
                    ev_fbuf_free[0] = ((P.ev("dv")),)
                    ev_fbuf_free[1] = ((P.ev("dv")),)
                    e_sx = DMA("scalar", x1_scr[ob][:, th * 8:(th + 1) * 8, :],
                               obA.rearrange("p (a b) -> p a b", a=8), "stx1",
                               waits=_w(*[x for t2 in range(8) for x in ev_sum[th * 8 + t2]]))
                    ev_x1b_free = e_sx
                    ev_xcb_free[wb] = ((P.ev("dv")),)

        ev_ln2fin = [None] * 16
        if "D" in phases:
            for t in range(16):
                e_mu = TS(mu2[:, t:t + 1], sum1[:, t:t + 1], 1.0 / D, None, ALU.mult,
                          ALU.bypass, waits=_w(*(ev_sum[t] or ())), inc="dv")
                e_ms = TT(stat[:, 18:19], mu2[:, t:t + 1], mu2[:, t:t + 1], ALU.mult,
                          waits=_w(e_mu), inc="dv")
                e_va = TS(stat[:, 19:20], sum2[:, t:t + 1], 1.0 / D, stat[:, 18:19],
                          ALU.mult, ALU.subtract,
                          waits=_w(ev_sum[t][1] if ev_sum[t] else None, e_ms), inc="dv")
                e_l = ACTI(stat[:, 20:21], stat[:, 19:20], AF.Ln, eps_ap, 1.0,
                           waits=_w(e_va, EV_CONST), inc="ac")
                e_r = ACTI(rstd2[:, t:t + 1], stat[:, 20:21], AF.Exp, zero_ap, -0.5,
                           waits=_w(e_l, e_mu), inc="ac")
                e_n = TS(nmr2[:, t:t + 1], mu2[:, t:t + 1], rstd2[:, t:t + 1], -1.0,
                         ALU.mult, ALU.mult, waits=_w(e_r, e_va), inc="dv")
                ev_ln2fin[t] = (e_r, e_n)

        if "E" in phases:
            DMA("sync", bias_s[:], bias_d, "ldc")
            EV_BIAS = P.ev("ldc")
            ev_out_store = {0: None, 1: None}
            ev_wf_free = [P.ev("pec"), P.ev("pec")]
            ev_xslE_free = [P.ev("dv"), P.ev("dv")]
            ev_lntmE_free = [P.ev("pes"), P.ev("pes")]
            ev_psA_freeE = [P.ev("ac"), P.ev("ac")]
            ev_fbuf_freeE = [(P.ev("dv"),), (P.ev("dv"),)]
            for th in range(2):
                act_gate = P.ev("pec") if th == 0 else P.ev("pee")
                ev_xcb2_free = [P.ev("dv"), P.ev("dv")]
                for ob in range(16):
                    wb = ob % 2
                    e_xl = DMA("sync", xcb[wb].rearrange("p (a b) -> p a b", a=8),
                               x1_scr[ob][:, th * 8:(th + 1) * 8, :], "ldx",
                               waits=_w(P.ev("stx1"), ev_xcb2_free[wb]))
                    for tc in range(8):
                        tcg = th * 8 + tc
                        lb = tc % 2
                        xt = xcb[wb][:, tc * 256:(tc + 1) * 256]
                        e_apl = ACTI(lntmE_l[lb], xt, AF.Identity,
                                     nmr2[:, tcg:tcg + 1], rstd2[:, tcg:tcg + 1],
                                     waits=_w(e_xl, *(ev_ln2fin[tcg] or ()),
                                              ev_lntmE_free[lb]), inc="ac")
                        blocks = [lntmE_l[lb][:, oh * 128:(oh + 1) * 128]
                                  for oh in range(2)]
                        e_cv = transpose_group(
                            blocks,
                            af3[:, ob * 2:ob * 2 + 2, tc * 128:(tc + 1) * 128],
                            _w(e_apl, act_gate))
                        ev_lntmE_free[lb] = P.ev("pet")
                    ev_xcb2_free[wb] = P.ev("ac")
                e_ln2fm = P.ev("dvt")
                for ob in range(16):
                    wb = ob % 2
                    e_wl = DMA("sync", wsc[wb].rearrange("p (a b) -> p a b", a=32),
                               wf_d[ob], "ldw", waits=_w(ev_wf_free[wb]))
                    e_xl = DMA("sync", xcb[wb].rearrange("p (a b) -> p a b", a=8),
                               x1_scr[ob][:, th * 8:(th + 1) * 8, :], "ldx",
                               waits=_w(ev_xslE_free[wb], P.ev("ac") if ob < 2 else None))
                    e_evf = {}
                    for hf in range(2):
                        e_mm = None
                        for tc4 in range(4):
                            tc = hf * 4 + tc4
                            for i in range(32):
                                w0 = _w(e_wl, e_ln2fm, ev_psA_freeE[hf]) if (tc4 == 0 and i == 0) else ()
                                MM(psAh[hf][:, tc4 * 256:(tc4 + 1) * 256],
                                   af3[:, i, tc * 128:(tc + 1) * 128],
                                   wsc[wb][:, i * 256:(i + 1) * 256],
                                   i == 0, False, waits=w0)
                            e_mm = MM(psAh[hf][:, tc4 * 256:(tc4 + 1) * 256], oner_s[:],
                                      bias_s[0:1, ob * 256:(ob + 1) * 256], False, True,
                                      waits=_w(EV_BIAS) if (tc4 == 0 and ob < 2) else (),
                                      inc="pee" if tc4 == 3 else None)
                        e_evf[hf] = ACTI(fbuf[:, hf * 1024:(hf + 1) * 1024], psAh[hf],
                                         AF.Copy, 0.0, 1.0,
                                         waits=_w(e_mm, *ev_fbuf_freeE[hf]), inc="ac")
                        ev_psA_freeE[hf] = e_evf[hf]
                    ev_wf_free[wb] = P.ev("pee")
                    obuf = obA if ob % 2 == 0 else obB
                    e_o = None
                    for tc in range(8):
                        hf = tc // 4
                        e_o = TT(obuf[:, tc * 256:(tc + 1) * 256],
                                 fbuf[:, tc * 256:(tc + 1) * 256],
                                 xcb[wb][:, tc * 256:(tc + 1) * 256], ALU.add,
                                 waits=_w(e_evf[hf], e_xl, ev_out_store[ob % 2]), inc="dv")
                    ev_fbuf_freeE[0] = ((P.ev("dv")),)
                    ev_fbuf_freeE[1] = ((P.ev("dv")),)
                    ev_xslE_free[wb] = P.ev("dv")
                    e_so = DMA("scalar", out_cb[ob][:, th * 8:(th + 1) * 8, :],
                               obuf.rearrange("p (a b) -> p a b", a=8), "stout",
                               waits=_w(e_o))
                    ev_out_store[ob % 2] = e_so

        final_waits = [P.ev("stout")]
        if debug:
            for s in ("stkv", "stcx", "stx1"):
                final_waits.append(P.ev(s))
        WAIT_ONLY("gpsimd", final_waits)

        semnames = sorted(P.cnt.keys())
        semh = {}
        for n in semnames:
            semh[n] = E(nc.semaphore(n))
        block = E(nc.Block())

        def mk(engname):
            def body(eng):
                last = {}
                for fn, waits, inc, k in P.ops[engname]:
                    for s, v in waits:
                        if last.get(s, -1) < v:
                            eng.wait_ge(semh[s], v)
                            last[s] = v
                    instr = fn(eng)
                    if inc is not None:
                        instr.then_inc(semh[inc], k)
            return body

        block.sync(mk("sync"))
        block.gpsimd(mk("gpsimd"))
        block.tensor(mk("tensor"))
        block.vector(mk("vector"))
        block.scalar(mk("scalar"))

    return nc


def host_inputs(core, L0_attn_x, L0_attn_Wq, L0_attn_Wk, L0_attn_Wv, L0_attn_Wo,
                L0_attn_cos, L0_attn_sin, L0_ffn_weight, L0_ffn_bias):
    b, z = core // 2, core % 2
    chunks = own_chunks(z)
    own_pos = np.concatenate([np.arange(c * 128, (c + 1) * 128) for c in chunks])
    X = np.asarray(L0_attn_x[b], dtype=np.float32)
    cos = np.asarray(L0_attn_cos, np.float32)
    sin = np.asarray(L0_attn_sin, np.float32)

    x_own = np.ascontiguousarray(X[own_pos])
    masks = np.zeros((16, 128, 256), np.float32)
    for s in range(16):
        c = chunks[s]
        kk = np.arange(KS[s] - 256, KS[s])
        qq = c * 128 + np.arange(128)
        masks[s] = np.where(kk[None, :] <= qq[:, None], 0.0, -30000.0)

    d = {
        "xf": X,
        "xo": x_own,
        "xocb": np.ascontiguousarray(
            x_own.reshape(16, 128, 16, 256).transpose(2, 1, 0, 3)),
        "cosf": np.ascontiguousarray(cos.T),
        "sinf": np.ascontiguousarray(sin.T),
        "coso": np.ascontiguousarray(cos[own_pos].T),
        "sino": np.ascontiguousarray(sin[own_pos].T),
        "masks": masks,
        "wq": np.ascontiguousarray(
            np.asarray(L0_attn_Wq, np.float32).reshape(32, 128, 32, 128).transpose(2, 0, 1, 3)),
        "wk": np.ascontiguousarray(
            np.asarray(L0_attn_Wk, np.float32).reshape(32, 128, 8, 128).transpose(2, 0, 1, 3)),
        "wv": np.ascontiguousarray(
            np.asarray(L0_attn_Wv, np.float32).reshape(32, 128, 8, 128).transpose(2, 0, 1, 3)),
        "wo": np.ascontiguousarray(
            np.asarray(L0_attn_Wo, np.float32).reshape(32, 128, 16, 256).transpose(2, 1, 0, 3)),
        "wf": np.ascontiguousarray(
            np.asarray(L0_ffn_weight).T.reshape(32, 128, 16, 256).transpose(2, 1, 0, 3)),
        "biasf": np.asarray(L0_ffn_bias).reshape(1, D),
        "idt": np.eye(128, dtype=np.float16),
        "rt": _rot_matrix_T(),
        "oner": np.ones((1, 128), np.float16),
        "cvec": np.broadcast_to(
            np.array([EPS, 0.0, 1.0, 1.0 / D], np.float32), (128, 4)).copy(),
    }
    return d, own_pos, b


def _rot_matrix_T():
    R = np.zeros((128, 128), np.float32)
    for dd in range(64):
        R[dd, dd + 64] = -1.0
        R[dd + 64, dd] = 1.0
    return R.T.astype(np.float16)


_NC_CACHE = {}


def _get_nc(debug=False, phases=("A1", "A2B", "C", "D", "E")):
    key = (debug, tuple(phases))
    if key not in _NC_CACHE:
        _NC_CACHE[key] = build_program(debug=debug, phases=phases)
    return _NC_CACHE[key]


def kernel(**inputs):
    nc = _get_nc()
    in_maps = []
    metas = []
    for core in range(8):
        d, own_pos, b = host_inputs(core, **inputs)
        in_maps.append(d)
        metas.append((own_pos, b))
    res = run_bass_kernel_spmd(nc, in_maps, core_ids=list(range(8)))
    out = np.empty((4, T, D), np.float32)
    for core in range(8):
        own_pos, b = metas[core]
        oc = np.asarray(res.results[core]["out_cb"], np.float32)
        rows = oc.transpose(2, 1, 0, 3).reshape(TOWN, D)
        out[b][own_pos] = rows
    return out


# version 08
# speedup vs baseline: 1.1502x; 1.0120x over previous
import math
import numpy as np
import ml_dtypes
import concourse.bass as bass
import concourse.mybir as mybir
from concourse.bass_utils import run_bass_kernel_spmd
from contextlib import ExitStack

F32 = mybir.dt.float32
F16 = mybir.dt.float16
AF = mybir.ActivationFunctionType
ALU = mybir.AluOpType
AX = mybir.AxisListType

T = 4096
D = 4096
HEADS = 32
GROUPS = 8
DH = 128
TOWN = 2048
NCH = 32
SCALE = 1.0 / math.sqrt(DH)
EPS = 1e-5
KS = [256 * (s + 1) for s in range(16)]


def own_chunks(z):
    return [c for c in range(NCH) if c % 4 in ((0, 3) if z == 0 else (1, 2))]


class Plan:
    ENGINES = ("sync", "gpsimd", "tensor", "vector", "scalar")

    def __init__(self):
        self.ops = {e: [] for e in self.ENGINES}
        self.cnt = {}

    def emit(self, eng, fn, waits=(), inc=None, k=1):
        ws = tuple((s, v) for (s, v) in waits if v > 0)
        self.ops[eng].append((fn, ws, inc, k))
        if inc is None:
            return None
        self.cnt[inc] = self.cnt.get(inc, 0) + k
        return (inc, self.cnt[inc])

    def ev(self, name):
        return (name, self.cnt.get(name, 0))


def _w(*evs):
    d = {}
    for e in evs:
        if e is None:
            continue
        s, v = e
        if d.get(s, -1) < v:
            d[s] = v
    return tuple(d.items())


def build_program(debug=False, phases=("A1", "A2B", "C", "D", "E")):
    nc = bass.Bass("TRN2", target_bir_lowering=False)
    P = Plan()
    scratch_kind = "ExternalOutput" if debug else "Internal"

    xf = nc.dram_tensor("xf", [T, D], F32, kind="ExternalInput").ap()
    xo = nc.dram_tensor("xo", [TOWN, D], F32, kind="ExternalInput").ap()
    xocb = nc.dram_tensor("xocb", [16, 128, 16, 256], F32, kind="ExternalInput").ap()
    cosf_d = nc.dram_tensor("cosf", [128, T], F32, kind="ExternalInput").ap()
    sinf_d = nc.dram_tensor("sinf", [128, T], F32, kind="ExternalInput").ap()
    coso_d = nc.dram_tensor("coso", [128, TOWN], F32, kind="ExternalInput").ap()
    sino_d = nc.dram_tensor("sino", [128, TOWN], F32, kind="ExternalInput").ap()
    masks_d = nc.dram_tensor("masks", [16, 128, 256], F32, kind="ExternalInput").ap()
    wq_d = nc.dram_tensor("wq", [32, 32, 128, 128], F32, kind="ExternalInput").ap()
    wk_d = nc.dram_tensor("wk", [8, 32, 128, 128], F32, kind="ExternalInput").ap()
    wv_d = nc.dram_tensor("wv", [8, 32, 128, 128], F32, kind="ExternalInput").ap()
    wo_d = nc.dram_tensor("wo", [16, 128, 32, 256], F32, kind="ExternalInput").ap()
    wf_d = nc.dram_tensor("wf", [16, 128, 32, 256], F16, kind="ExternalInput").ap()
    bias_d = nc.dram_tensor("biasf", [1, D], F16, kind="ExternalInput").ap()
    idt_d = nc.dram_tensor("idt", [128, 128], F16, kind="ExternalInput").ap()
    rt_d = nc.dram_tensor("rt", [128, 128], F16, kind="ExternalInput").ap()
    oner_d = nc.dram_tensor("oner", [1, 128], F16, kind="ExternalInput").ap()
    cvec_d = nc.dram_tensor("cvec", [128, 4], F32, kind="ExternalInput").ap()

    k_scr = nc.dram_tensor("k_scr", [GROUPS, 128, T], F16, kind=scratch_kind).ap()
    v_scr = nc.dram_tensor("v_scr", [GROUPS, 128, T], F16, kind=scratch_kind).ap()
    ctx_scr = nc.dram_tensor("ctx_scr", [HEADS, 128, TOWN], F16, kind=scratch_kind).ap()
    x1_scr = nc.dram_tensor("x1_scr", [16, 128, 16, 256], F32, kind=scratch_kind).ap()
    out_cb = nc.dram_tensor("out_cb", [16, 128, 16, 256], F32, kind="ExternalOutput").ap()

    with ExitStack() as ctx:
        E = ctx.enter_context

        act = E(nc.sbuf_tensor("act", [128, 32 * 1024], F16))
        af3 = act[:].rearrange("p (c t) -> p c t", c=32)
        AR16 = E(nc.sbuf_tensor("AR16", [128, 38912], F16))
        AR32 = E(nc.sbuf_tensor("AR32", [128, 14464], F32))
        idt_s = E(nc.sbuf_tensor("idt_s", [128, 128], F16))
        rt_s = E(nc.sbuf_tensor("rt_s", [128, 128], F16))
        oner_s = E(nc.sbuf_tensor("oner_s", [1, 128], F16))
        cvec_s = E(nc.sbuf_tensor("cvec_s", [128, 4], F32))
        bias_s = E(nc.sbuf_tensor("bias_s", [1, D], F16))

        def v16(off, n):
            assert off + n <= 38912, (off, n)
            return AR16[:, off:off + n]

        def v32(off, n):
            assert off + n <= 14464, (off, n)
            return AR32[:, off:off + n]

        lntm = v16(0, 4096)
        sqb = v16(4096, 4096)
        lntm_l = [v16(0, 4096), v16(16384, 4096)]
        sqb_l = [v16(4096, 4096), v16(20480, 4096)]
        ev_lntp = [None, None]
        ev_sqrd = {}
        wsb = [v16(8192, 4096), v16(12288, 4096)]
        cosf = v16(24576, 4096)
        sinf = v16(28672, 4096)
        kev = v16(32768, 1024)
        r1b = v16(33792, 1024)
        r2b = v16(34816, 1024)
        kgb = v16(0, 4096)
        vgf = v16(4096, 4096)
        pbufb = v16(16384, 2048)
        ptbb = v16(18432, 2048)
        qraw = v16(22528, 1024)
        coso = v16(24576, 2048)
        sino = v16(26624, 2048)
        masksb = v16(28672, 4096)
        vtm = v16(32768, 4160)
        vt3 = vtm.rearrange("p (c n) -> p c n", n=130)
        ctxhb_l = [v16(36928, 1024), v16(21504, 1024)]
        cbuf_l = [v16(37952, 128), v16(38592, 128)]
        qh_l = [v16(23552, 1024), v16(20480, 1024)]
        r2q = v16(38080, 512)
        wsc = [v16(8192, 8192), v16(16384, 8192)]
        lntmE_l = [v16(0, 256), v16(256, 256)]
        sqbC = v16(4096, 256)

        xsl = [v32(0, 4096), v32(4096, 4096)]
        stat = v32(8192, 96)
        fbuf = v32(8288, 2048)
        obA = v32(10336, 2048)
        obB = v32(12384, 2048)
        xcb = [v32(0, 2048), v32(2048, 2048)]
        ctxA_l = [v32(4096, 132), v32(4228, 132)]
        sum1 = stat[:, 32:48]
        sum2 = stat[:, 48:64]
        mu2 = stat[:, 64:80]
        rstd2 = stat[:, 32:48]
        nmr2 = stat[:, 48:64]

        psA = E(nc.psum_tensor("psA", [128, 2048], F32))
        psB = E(nc.psum_tensor("psB", [128, 512], F32))
        psC = E(nc.psum_tensor("psC", [128, 512], F32))
        psT = E(nc.psum_tensor("psT", [128, 1024], F16))
        psT2 = E(nc.psum_tensor("psT2", [128, 1024], F16))
        psAh = [psA[:, 0:1024], psA[:, 1024:2048]]
        psTb = [psT, psT2]
        ctxb = [psC, psB]

        eps_ap = cvec_s[:, 0:1]
        zero_ap = cvec_s[:, 1:2]

        def DMA(eng, dst, src, inc, waits=()):
            return P.emit(eng, lambda e, d=dst, s=src: e.dma_start(out=d, in_=s),
                          _w(*waits), inc, 16)

        def MM(out, lhsT, rhs, start, stop, waits=(), inc=None):
            return P.emit("tensor",
                          lambda e, o=out, l=lhsT, r=rhs, st=start, sp=stop:
                          e.matmul(o, l, r, start=st, stop=sp), _w(*waits), inc)

        def TP(out, in_, waits=(), inc=None):
            return P.emit("tensor",
                          lambda e, o=out, i=in_: e.transpose(o, i, idt_s[:]),
                          _w(*waits), inc)

        def ACTI(out, in_, func, bias, scale, waits=(), inc=None):
            return P.emit("scalar",
                          lambda e, o=out, i=in_, f=func, b=bias, s=scale:
                          e.activation(o, i, f, bias=b, scale=s), _w(*waits), inc)

        def TT(out, a, b, op, waits=(), inc=None, eng="vector"):
            return P.emit(eng,
                          lambda e, o=out, x=a, y=b, p=op: e.tensor_tensor(o, x, y, op=p),
                          _w(*waits), inc, 1)

        def TS(out, in0, s1, s2, op0, op1, waits=(), inc=None, eng="vector"):
            return P.emit(eng,
                          lambda e, o=out, i=in0, a=s1, b=s2, p0=op0, p1=op1:
                          e.tensor_scalar(o, i, a, b, op0=p0, op1=p1), _w(*waits), inc, 1)

        def RED(out, in_, op, waits=(), inc=None):
            return P.emit("vector",
                          lambda e, o=out, i=in_, p=op: e.tensor_reduce(o, i, axis=AX.X, op=p),
                          _w(*waits), inc)

        def CP(out, in_, waits=(), inc=None):
            return P.emit("vector", lambda e, o=out, i=in_: e.tensor_copy(o, i),
                          _w(*waits), inc)

        def RECIP(out, in_, waits=(), inc=None):
            return P.emit("vector", lambda e, o=out, i=in_: e.reciprocal(o, i),
                          _w(*waits), inc)

        def MS(ap, val, waits=(), inc=None):
            return P.emit("gpsimd", lambda e, a=ap, v=val: e.memset(a, v),
                          _w(*waits), inc)

        def WAIT_ONLY(eng, waits):
            P.emit(eng, lambda e: None, _w(*waits), None)

        DMA("sync", idt_s[:], idt_d, "ldc")
        DMA("sync", rt_s[:], rt_d, "ldc")
        DMA("sync", oner_s[:], oner_d, "ldc")
        DMA("sync", cvec_s[:], cvec_d, "ldc")
        DMA("gpsimd", cosf, cosf_d, "ldc")
        DMA("gpsimd", sinf, sinf_d, "ldc")
        EV_CONST = P.ev("ldc")

        tp_war = [None, None]
        tp_bank = [0]
        sqb_rd = [None]

        def ln_slab(xbuf, out16, waits, sqbuf=None, stcol=0, out_war=None, sq_war=None):
            if sqbuf is None:
                sqbuf = sqb
            if out_war is None:
                out_war = P.ev("pet")
            c = stcol
            e_s1 = RED(stat[:, c:c + 1], xbuf, ALU.add, waits=waits, inc="dv")
            e_sq = ACTI(sqbuf, xbuf, AF.Square, 0.0, 1.0,
                        waits=waits + _w(sq_war), inc="ac")
            e_s2 = RED(stat[:, c + 1:c + 2], sqbuf, ALU.add, waits=(e_sq,), inc="dv")
            sqb_rd[0] = e_s2
            e_mu = TS(stat[:, c + 2:c + 3], stat[:, c:c + 1], 1.0 / D, None, ALU.mult,
                      ALU.bypass, waits=(e_s1,), inc="dv")
            e_msq = TT(stat[:, c + 3:c + 4], stat[:, c + 2:c + 3], stat[:, c + 2:c + 3],
                       ALU.mult, waits=(e_mu,), inc="dv")
            e_var = TS(stat[:, c + 4:c + 5], stat[:, c + 1:c + 2], 1.0 / D,
                       stat[:, c + 3:c + 4], ALU.mult,
                       ALU.subtract, waits=(e_s2, e_msq), inc="dv")
            e_l = ACTI(stat[:, c + 5:c + 6], stat[:, c + 4:c + 5], AF.Ln, eps_ap, 1.0,
                       waits=(e_var, EV_CONST), inc="ac")
            e_r = ACTI(stat[:, c + 6:c + 7], stat[:, c + 5:c + 6], AF.Exp, zero_ap, -0.5,
                       waits=(e_l,), inc="ac")
            e_n = TS(stat[:, c + 7:c + 8], stat[:, c + 2:c + 3], stat[:, c + 6:c + 7],
                     -1.0, ALU.mult, ALU.mult, waits=(e_r, e_mu), inc="dv")
            e_ap = ACTI(out16, xbuf, AF.Identity, stat[:, c + 7:c + 8],
                        stat[:, c + 6:c + 7],
                        waits=(e_n, e_r) + _w(out_war), inc="ac")
            return e_ap

        def transpose_group(blocks, dst, waits, evac="vector"):
            bank = tp_bank[0] % 2
            tp_bank[0] += 1
            pt = psTb[bank][:, 0:len(blocks) * 128]
            tp_ev = None
            for j, b in enumerate(blocks):
                tp_ev = TP(pt[:, j * 128:(j + 1) * 128], b,
                           waits=(waits + _w(tp_war[bank])) if j == 0 else (),
                           inc="pet")
            src = pt.rearrange("p (c t) -> p c t", c=len(blocks))
            if evac == "scalar":
                ev = P.emit("scalar",
                            lambda e, o=dst, i=src: e.activation(o, i, AF.Copy,
                                                                 bias=0.0, scale=1.0),
                            _w(tp_ev), "dvt")
            else:
                ev = CP(dst, src, waits=_w(tp_ev), inc="dvt")
            tp_war[bank] = ev
            return ev

        def transpose_slab(src16, tcol, waits, war_mm):
            last = None
            for icq in range(8):
                blocks = [src16[:, (icq * 4 + j) * 128:(icq * 4 + j + 1) * 128]
                          for j in range(4)]
                dst = af3[:, icq * 4:(icq + 1) * 4, tcol:tcol + 128]
                last = transpose_group(blocks, dst, waits + _w(*war_mm))
            return last

        ev_apply = {}
        ev_kv_mm_last = None
        ev_psA_free = [None, None]
        ev_cb_free = [None, None]
        ev_ws_free = [None, None]
        ev_kev_free = None

        if "A1" in phases:
            for TC in range(4):
                tcols = slice(TC * 1024, (TC + 1) * 1024)
                for sub in range(8):
                    slab = TC * 8 + sub
                    buf = slab % 2
                    e_ld = DMA("sync", xsl[buf], xf[slab * 128:(slab + 1) * 128, :],
                               "ldx", waits=_w(ev_apply.get(slab - 2)))
                    e_ap = ln_slab(xsl[buf], lntm_l[buf], _w(e_ld), sqbuf=sqb_l[buf],
                                   stcol=buf * 8, out_war=ev_lntp[buf],
                                   sq_war=ev_sqrd.get(buf))
                    ev_sqrd[buf] = sqb_rd[0]
                    ev_apply[slab] = e_ap
                    transpose_slab(lntm_l[buf], sub * 128, _w(e_ap), (ev_kv_mm_last,))
                    ev_lntp[buf] = P.ev("pet")
                e_lnfm = P.ev("dvt")
                for idx in range(16):
                    kv, g = idx // 8, idx % 8
                    wsrc = (wk_d if kv == 0 else wv_d)[g]
                    wb = idx % 2
                    e_wl = DMA("gpsimd", wsb[wb].rearrange("p (a b) -> p a b", a=32),
                               wsrc.rearrange("a p b -> p a b"), "ldw",
                               waits=_w(ev_ws_free[wb]))
                    par = idx % 2
                    pa = psAh[par]
                    e_mm = None
                    for i in range(32):
                        w0 = _w(e_wl, e_lnfm, ev_psA_free[par]) if i == 0 else ()
                        MM(pa[:, 0:512], wsb[wb][:, i * 128:(i + 1) * 128],
                           af3[:, i, 0:512], i == 0, i == 31, waits=w0)
                        e_mm = MM(pa[:, 512:1024], wsb[wb][:, i * 128:(i + 1) * 128],
                                  af3[:, i, 512:1024], i == 0, i == 31,
                                  inc="pea" if i == 31 else None)
                    ev_ws_free[wb] = e_mm
                    e_ev = ACTI(kev, pa, AF.Copy, 0.0, 1.0,
                                waits=_w(e_mm, ev_kev_free), inc="ac")
                    ev_psA_free[par] = e_ev
                    if kv == 0:
                        e_r1 = []
                        for jj in range(2):
                            cb = ctxb[jj]
                            e_rk = MM(cb[:, 0:512], rt_s[:],
                                      kev[:, jj * 512:(jj + 1) * 512], True, True,
                                      waits=_w(e_ev, ev_cb_free[jj]), inc="per")
                            scols = slice(TC * 1024 + jj * 512, TC * 1024 + (jj + 1) * 512)
                            e1 = TT(r1b[:, jj * 512:(jj + 1) * 512], cb[:, 0:512],
                                    sinf[:, scols], ALU.mult,
                                    waits=_w(e_rk, EV_CONST), inc="dv")
                            ev_cb_free[jj] = e1
                            e_r1.append(e1)
                        e_r2 = TT(r2b, kev, cosf[:, tcols], ALU.mult,
                                  waits=_w(e_ev, EV_CONST), inc="dv")
                        e_kr = TT(kev, r1b, r2b, ALU.add,
                                  waits=_w(*e_r1, e_r2), inc="dv")
                        e_st = DMA("scalar", k_scr[g][:, tcols], kev, "stkv", waits=_w(e_kr))
                    else:
                        e_st = DMA("scalar", v_scr[g][:, tcols], kev, "stkv", waits=_w(e_ev))
                    ev_kev_free = e_st
                ev_kv_mm_last = P.ev("pea")

        ev_qproj_last = None
        ev_ctx_store = [None, None]
        ev_cbuf_free = [None, None]
        ev_qh_war = [None, None]
        ev_kg_free = None
        ev_vg_free = None
        ev_vtm_free = None
        ev_qraw_free = [None, None]
        ev_pbuf_free = [None, None]
        ev_ptb_free = [None, None]
        ev_kvready = (None, None)
        ev_maskload = None

        if "A2B" in phases:
            DMA("gpsimd", coso, coso_d, "ldc", waits=_w(P.ev("dv")))
            DMA("gpsimd", sino, sino_d, "ldc")
            ev_maskload = DMA("gpsimd", masksb.rearrange("p (s n) -> p s n", s=16),
                              masks_d.rearrange("s p n -> p s n"), "ldc",
                              waits=_w(P.ev("dv")))
            EV_BC = P.ev("ldc")

            for ph in range(2):
                gate_ac = P.ev("ac")
                gate_gp = P.ev("gp")
                for sub in range(8):
                    slab = ph * 8 + sub
                    buf = sub % 2
                    war = ev_apply.get(("o", ph, sub - 2))
                    e_ld = DMA("sync", xsl[buf], xo[slab * 128:(slab + 1) * 128, :],
                               "ldx", waits=_w(war, gate_ac if sub < 2 else None))
                    e_ap = ln_slab(xsl[buf], lntm, _w(e_ld), stcol=0,
                                   out_war=ev_lntp[0] if sub else None,
                                   sq_war=ev_sqrd.get("a2"))
                    ev_sqrd["a2"] = sqb_rd[0]
                    ev_apply[("o", ph, sub)] = e_ap
                    transpose_slab(lntm, sub * 128, _w(e_ap),
                                   (ev_kv_mm_last, ev_qproj_last))
                    ev_lntp[0] = P.ev("pet")
                e_lnfm = P.ev("dvt")

                for hp in range(HEADS // 2):
                    g = (2 * hp) // 4
                    e_wl = [None, None]
                    for lane in range(2):
                        h = 2 * hp + lane
                        e_wl[lane] = DMA("gpsimd",
                                         wsb[lane].rearrange("p (a b) -> p a b", a=32),
                                         wq_d[h].rearrange("a p b -> p a b"), "ldw",
                                         waits=_w(ev_ws_free[lane]))
                    if hp % 2 == 0:
                        a2done = (e_lnfm, P.ev("dv") if hp == 0 else None,
                                  P.ev("ac") if hp == 0 else None,
                                  P.ev("stkv") if (ph == 0 and hp == 0) else None)
                        e_kg = DMA("sync", kgb, k_scr[g], "ldkv",
                                   waits=_w(ev_kg_free, *a2done))
                        e_vg = DMA("sync", vgf, v_scr[g], "ldkv",
                                   waits=_w(ev_vg_free, *a2done))
                        e_msv = MS(vtm, 1.0, waits=_w(ev_vtm_free,
                                   P.ev("dv") if hp == 0 else None), inc="gp")
                        e_vt = None
                        for kq in range(8):
                            blocks = [vgf[:, (kq * 4 + j) * 128:(kq * 4 + j + 1) * 128]
                                      for j in range(4)]
                            e_vt = transpose_group(
                                blocks, vt3[:, kq * 4:(kq + 1) * 4, 0:128],
                                _w(e_vg, e_msv))
                        ev_kvready = (e_kg, e_vt)
                    e_qh_l = [None, None]
                    for lane in range(2):
                        qhb = qh_l[lane]
                        for tcq in range(2):
                            qc = slice(tcq * 512, (tcq + 1) * 512)
                            e_mm = None
                            for i in range(32):
                                w0 = _w(e_wl[lane], e_lnfm, ev_cb_free[1]) if i == 0 else ()
                                e_mm = MM(psB[:, 0:512],
                                          wsb[lane][:, i * 128:(i + 1) * 128],
                                          af3[:, i, qc], i == 0, i == 31, waits=w0,
                                          inc="peq" if i == 31 else None)
                            e_qe = ACTI(qraw[:, qc], psB[:, 0:512], AF.Copy, 0.0, 1.0,
                                        waits=_w(e_mm, *(ev_qraw_free[tcq] or ())),
                                        inc="ac")
                            ev_cb_free[1] = e_qe
                            e_rq = MM(psC[:, 0:512], rt_s[:], qraw[:, qc], True, True,
                                      waits=_w(e_qe, ev_cb_free[0]), inc="per")
                            ocs = slice(ph * 1024 + tcq * 512, ph * 1024 + (tcq + 1) * 512)
                            e_q1 = TT(qhb[:, qc], psC[:, 0:512], sino[:, ocs], ALU.mult,
                                      waits=_w(e_rq, EV_BC, ev_qh_war[lane]), inc="dv")
                            ev_cb_free[0] = e_q1
                            e_q2 = TT(r2q, qraw[:, qc], coso[:, ocs], ALU.mult,
                                      waits=_w(e_qe, EV_BC), inc="dv")
                            e_qh_l[lane] = TT(qhb[:, qc], qhb[:, qc], r2q, ALU.add,
                                              waits=_w(e_q1, e_q2), inc="dv")
                            ev_qraw_free[tcq] = (e_rq, e_q2)

                    subrows = []
                    for ls in range(8):
                        s = ph * 8 + ls
                        ks = KS[s]
                        n = (ks + 1023) // 1024
                        for r in range(n):
                            subrows.append(dict(
                                s=s, ls=ls, r=r, k0=r * 1024,
                                k1=min((r + 1) * 1024, ks), ks=ks,
                                first=(r == 0), last=(r == n - 1), nsub=n))

                    slot_state = [{}, {}]
                    sr_l = [None, None]

                    def emit_S(lane, srp, kvr=ev_kvready, e_qh_l=e_qh_l):
                        sr = dict(srp)
                        par = lane
                        nk = sr["k1"] - sr["k0"]
                        nj = (nk + 511) // 512
                        qtile = qh_l[lane][:, sr["ls"] * 128:(sr["ls"] + 1) * 128]
                        e_chunks = []
                        for jj in range(nj):
                            w = min(512, nk - jj * 512)
                            w0 = _w(e_qh_l[lane], kvr[0], ev_psA_free[par]) if jj == 0 else ()
                            e_chunks.append(
                                MM(psAh[par][:, jj * 512:jj * 512 + w], qtile,
                                   kgb[:, sr["k0"] + jj * 512:sr["k0"] + jj * 512 + w],
                                   True, True, waits=w0, inc="pes"))
                        sr["e_chunks"] = e_chunks
                        return sr

                    def emit_max(lane, sr):
                        par = lane
                        nk = sr["k1"] - sr["k0"]
                        nj = (nk + 511) // 512
                        e_chunks = sr["e_chunks"]
                        e_s = e_chunks[-1]
                        e_mask = None
                        if sr["last"]:
                            lo = sr["ks"] - 256 - sr["k0"]
                            mreg = psAh[par][:, lo:lo + 256]
                            e_mask = TT(mreg, mreg,
                                        masksb[:, sr["s"] * 256:(sr["s"] + 1) * 256],
                                        ALU.add, waits=_w(e_s, ev_maskload), inc="dv")
                        if nj == 2:
                            e_m0 = RED(stat[:, 16 + par:17 + par], psAh[par][:, 0:512],
                                       ALU.max, waits=_w(e_chunks[0]), inc="dv")
                            e_m1 = RED(stat[:, 8 + par:9 + par],
                                       psAh[par][:, 512:nk], ALU.max,
                                       waits=_w(e_s, e_mask), inc="dv")
                            e_max = TT(stat[:, 8 + par:9 + par], stat[:, 8 + par:9 + par],
                                       stat[:, 16 + par:17 + par], ALU.max,
                                       waits=_w(e_m0, e_m1), inc="dv")
                        else:
                            e_max = RED(stat[:, 8 + par:9 + par], psAh[par][:, 0:nk],
                                        ALU.max, waits=_w(e_s, e_mask), inc="dv")
                        st0 = slot_state[lane]
                        mbase = 10 if lane == 0 else 24
                        if sr["first"]:
                            mcol = mbase
                            old = mbase
                            e_mrun = CP(stat[:, mbase:mbase + 1], stat[:, 8 + par:9 + par],
                                        waits=_w(e_max), inc="dv")
                        else:
                            mcol = mbase + sr["r"] % 2
                            old = st0["mcol"]
                            e_mrun = TT(stat[:, mcol:mcol + 1], stat[:, old:old + 1],
                                        stat[:, 8 + par:9 + par], ALU.max,
                                        waits=_w(st0["e_mrun"], e_max), inc="dv")
                        e_nb = TS(stat[:, 14 + par:15 + par], stat[:, mcol:mcol + 1],
                                  -SCALE, None, ALU.mult, ALU.bypass,
                                  waits=_w(e_mrun), inc="gp", eng="gpsimd")
                        if not sr["first"]:
                            sr["e_fd"] = TT(stat[:, 12 + par:13 + par],
                                            stat[:, old:old + 1],
                                            stat[:, mcol:mcol + 1], ALU.subtract,
                                            waits=_w(e_mrun), inc="gp", eng="gpsimd")
                        st0["mcol"] = mcol
                        st0["e_mrun"] = e_mrun
                        sr["e_nb"] = e_nb
                        sr["e_mrun"], sr["mcol"] = e_mrun, mcol

                    def emit_exp(lane, sr):
                        par = lane
                        nk = sr["k1"] - sr["k0"]
                        e_exp = ACTI(pbufb[:, par * 1024:par * 1024 + nk],
                                     psAh[par][:, 0:nk], AF.Exp,
                                     stat[:, 14 + par:15 + par], SCALE,
                                     waits=_w(sr["e_nb"], ev_pbuf_free[par]), inc="ac")
                        ev_psA_free[par] = e_exp
                        if not sr["first"]:
                            e_f = ACTI(stat[:, 28 + par:29 + par],
                                       stat[:, 12 + par:13 + par], AF.Exp,
                                       zero_ap, SCALE, waits=_w(sr["e_fd"]), inc="ac")
                            sr["e_f"] = e_f
                            ev_psA_free[par] = e_f
                        sr["e_exp"] = e_exp

                    def emit_TP(lane, sr):
                        par = lane
                        nk = sr["k1"] - sr["k0"]
                        nkb = nk // 128
                        ev_evac = None
                        for kq in range((nkb + 3) // 4):
                            nblk = min(4, nkb - kq * 4)
                            blocks = [pbufb[:, par * 1024 + (kq * 4 + jj) * 128:
                                            par * 1024 + (kq * 4 + jj + 1) * 128]
                                      for jj in range(nblk)]
                            dst = ptbb[:, par * 1024 + kq * 512:
                                       par * 1024 + kq * 512 + nblk * 128]
                            ev_evac = transpose_group(
                                blocks, dst.rearrange("p (c t) -> p c t", c=nblk),
                                _w(sr["e_exp"], ev_ptb_free[par]),
                                evac="scalar" if (tp_bank[0] % 2) else "vector")
                        ev_pbuf_free[par] = P.ev("pet")
                        sr["ev_evac"] = ev_evac

                    def emit_PV_combine(lane, kvr=ev_kvready, sr=None):
                        par = lane
                        nk = sr["k1"] - sr["k0"]
                        nkb = nk // 128
                        ev_evac = sr["ev_evac"]
                        cb = ctxb[lane]
                        ctxA = ctxA_l[lane]
                        e_pv = None
                        for kc in range(nkb):
                            w0 = _w(ev_evac, kvr[1], ev_cb_free[lane]) if kc == 0 else ()
                            e_pv = MM(cb[:, 0:129],
                                      ptbb[:, par * 1024 + kc * 128:par * 1024 + (kc + 1) * 128],
                                      vt3[:, sr["k0"] // 128 + kc, 0:129],
                                      kc == 0, kc == nkb - 1, waits=w0,
                                      inc="pev" if kc == nkb - 1 else None)
                        ev_ptb_free[par] = e_pv
                        st0 = slot_state[lane]
                        if sr["nsub"] > 1:
                            if sr["first"]:
                                e_c = CP(ctxA[:, 0:129], cb[:, 0:129],
                                         waits=_w(e_pv, st0.get("e_ctxA_rd")),
                                         inc="dv")
                            else:
                                e_sc = TS(ctxA[:, 0:129], ctxA[:, 0:129],
                                          stat[:, 28 + par:29 + par],
                                          None, ALU.mult, ALU.bypass,
                                          waits=_w(sr["e_f"], st0["e_ctx"]),
                                          inc="dv")
                                e_c = TT(ctxA[:, 0:129], ctxA[:, 0:129], cb[:, 0:129],
                                         ALU.add, waits=_w(e_sc, e_pv), inc="dv")
                            st0["e_ctx"] = e_c
                            ev_cb_free[lane] = e_c
                        if sr["last"]:
                            if sr["nsub"] == 1:
                                lsrc, csrc, e_fin = cb[:, 128:129], cb[:, 0:128], _w(e_pv)
                            else:
                                lsrc, csrc, e_fin = (ctxA[:, 128:129], ctxA[:, 0:128],
                                                     _w(st0["e_ctx"]))
                            rlc = 30 + lane
                            e_rl = RECIP(stat[:, rlc:rlc + 1], lsrc, waits=e_fin, inc="dv")
                            e_cf = ACTI(cbuf_l[lane], csrc, AF.Identity, zero_ap,
                                        stat[:, rlc:rlc + 1],
                                        waits=_w(e_rl, ev_cbuf_free[lane]), inc="ac")
                            if sr["nsub"] == 1:
                                ev_cb_free[lane] = e_cf
                            else:
                                st0["e_ctxA_rd"] = e_cf
                            e_cv = transpose_group(
                                [cbuf_l[lane]],
                                ctxhb_l[lane][:, sr["ls"] * 128:(sr["ls"] + 1) * 128]
                                .rearrange("p (c t) -> p c t", c=1),
                                _w(e_cf, ev_ctx_store[lane]))
                            ev_cbuf_free[lane] = e_cv
                            sr["e_cv"] = e_cv

                    have_prev = False
                    e_cv_last = [None, None]
                    prev_srs = [None, None]
                    for srp in subrows:
                        cur = [emit_S(0, srp), emit_S(1, srp)]
                        emit_max(0, cur[0])
                        emit_max(1, cur[1])
                        if have_prev:
                            for lane in range(2):
                                emit_TP(lane, prev_srs[lane])
                                emit_PV_combine(lane, sr=prev_srs[lane])
                                if prev_srs[lane].get("e_cv"):
                                    e_cv_last[lane] = prev_srs[lane]["e_cv"]
                        emit_exp(0, cur[0])
                        emit_exp(1, cur[1])
                        prev_srs = cur
                        have_prev = True
                    emit_TP(0, prev_srs[0])
                    emit_TP(1, prev_srs[1])
                    for lane in range(2):
                        emit_PV_combine(lane, sr=prev_srs[lane])
                        e_cv_last[lane] = prev_srs[lane]["e_cv"]

                    for lane in range(2):
                        h = 2 * hp + lane
                        e_cs = DMA("scalar", ctx_scr[h][:, ph * 1024:(ph + 1) * 1024],
                                   ctxhb_l[lane], "stcx", waits=_w(e_cv_last[lane]))
                        ev_ctx_store[lane] = e_cs
                        ev_qh_war[lane] = P.ev("pes")
                    ev_ws_free[0] = P.ev("peq")
                    ev_ws_free[1] = P.ev("peq")
                    if hp % 2 == 1:
                        ev_kg_free = P.ev("pes")
                        ev_vg_free = P.ev("pet")
                        ev_vtm_free = P.ev("pev")
                ev_qproj_last = P.ev("peq")

        ev_sum = [None] * 16
        if "C" in phases:
            gate_c = (P.ev("pet"), P.ev("pev"), P.ev("pes"), P.ev("per"),
                      P.ev("dv"), P.ev("ac"))
            e_z1 = MS(sum1, 0.0, waits=_w(P.ev("dv"), P.ev("ac")), inc="gp")
            e_z2 = MS(sum2, 0.0, inc="gp")
            for t in range(16):
                ev_sum[t] = (e_z1, e_z2)
            ev_ws_free_c = [gate_c, gate_c]
            ev_xcb_free = [(P.ev("ac"),), (P.ev("ac"),)]
            ev_fbuf_free = [(P.ev("ac"),), (P.ev("ac"),)]
            ev_psA_free = [P.ev("ac"), P.ev("ac")]
            ev_x1b_free = None
            ev_sq_free = sqb_rd[0]
            for th in range(2):
                act_gate = P.ev("peq") if th == 0 else P.ev("pec")
                e_cl = None
                for q8 in range(4):
                    src = ctx_scr[q8 * 8:(q8 + 1) * 8, :, th * 1024:(th + 1) * 1024]
                    e_cl = DMA("sync", af3[:, q8 * 8:(q8 + 1) * 8, :],
                               src.rearrange("h p t -> p h t"), "ldcx",
                               waits=_w(P.ev("stcx"), act_gate) if q8 == 0 else ())
                for ob in range(16):
                    wb = ob % 2
                    e_wl = DMA("gpsimd", wsc[wb].rearrange("p (a b) -> p a b", a=32),
                               wo_d[ob], "ldw", waits=_w(*ev_ws_free_c[wb]))
                    e_xl = DMA("sync", xcb[wb].rearrange("p (a b) -> p a b", a=8),
                               xocb[ob][:, th * 8:(th + 1) * 8, :], "ldx",
                               waits=_w(*ev_xcb_free[wb]))
                    e_evc = {}
                    for hf in range(2):
                        e_mm = None
                        for tc4 in range(4):
                            tc = hf * 4 + tc4
                            for i in range(32):
                                w0 = _w(e_wl, e_cl, ev_psA_free[hf]) if (tc4 == 0 and i == 0) else ()
                                e_mm = MM(psAh[hf][:, tc4 * 256:(tc4 + 1) * 256],
                                          af3[:, i, tc * 128:(tc + 1) * 128],
                                          wsc[wb][:, i * 256:(i + 1) * 256],
                                          i == 0, i == 31, waits=w0,
                                          inc="pec" if (tc4 == 3 and i == 31) else None)
                        e_evc[hf] = ACTI(fbuf[:, hf * 1024:(hf + 1) * 1024], psAh[hf],
                                         AF.Copy, 0.0, 1.0,
                                         waits=_w(e_mm, *ev_fbuf_free[hf]), inc="ac")
                        ev_psA_free[hf] = e_evc[hf]
                    ev_ws_free_c[wb] = ((P.ev("pec")),)
                    for tc in range(8):
                        hf = tc // 4
                        tcg = th * 8 + tc
                        ft = fbuf[:, tc * 256:(tc + 1) * 256]
                        xt = xcb[wb][:, tc * 256:(tc + 1) * 256]
                        x1t = obA[:, tc * 256:(tc + 1) * 256]
                        e_x1 = TT(x1t, ft, xt, ALU.add,
                                  waits=_w(e_evc[hf], e_xl, ev_x1b_free), inc="dv")
                        e_p1 = RED(stat[:, 16:17], x1t, ALU.add, waits=_w(e_x1), inc="dv")
                        e_a1 = TT(sum1[:, tcg:tcg + 1], sum1[:, tcg:tcg + 1],
                                  stat[:, 16:17], ALU.add,
                                  waits=_w(e_p1, *ev_sum[tcg]), inc="dv")
                        e_sq = ACTI(sqbC, x1t, AF.Square, 0.0, 1.0,
                                    waits=_w(e_x1, ev_sq_free), inc="ac")
                        e_p2 = RED(stat[:, 17:18], sqbC, ALU.add, waits=_w(e_sq), inc="dv")
                        ev_sq_free = e_p2
                        e_a2 = TT(sum2[:, tcg:tcg + 1], sum2[:, tcg:tcg + 1],
                                  stat[:, 17:18], ALU.add,
                                  waits=_w(e_p2, *ev_sum[tcg]), inc="dv")
                        ev_sum[tcg] = (e_a1, e_a2)
                    ev_fbuf_free[0] = ((P.ev("dv")),)
                    ev_fbuf_free[1] = ((P.ev("dv")),)
                    e_sx = DMA("scalar", x1_scr[ob][:, th * 8:(th + 1) * 8, :],
                               obA.rearrange("p (a b) -> p a b", a=8), "stx1",
                               waits=_w(*[x for t2 in range(8) for x in ev_sum[th * 8 + t2]]))
                    ev_x1b_free = e_sx
                    ev_xcb_free[wb] = ((P.ev("dv")),)

        ev_ln2fin = [None] * 16
        if "D" in phases:
            for t in range(16):
                e_mu = TS(mu2[:, t:t + 1], sum1[:, t:t + 1], 1.0 / D, None, ALU.mult,
                          ALU.bypass, waits=_w(*(ev_sum[t] or ())), inc="dv")
                e_ms = TT(stat[:, 18:19], mu2[:, t:t + 1], mu2[:, t:t + 1], ALU.mult,
                          waits=_w(e_mu), inc="dv")
                e_va = TS(stat[:, 19:20], sum2[:, t:t + 1], 1.0 / D, stat[:, 18:19],
                          ALU.mult, ALU.subtract,
                          waits=_w(ev_sum[t][1] if ev_sum[t] else None, e_ms), inc="dv")
                e_l = ACTI(stat[:, 20:21], stat[:, 19:20], AF.Ln, eps_ap, 1.0,
                           waits=_w(e_va, EV_CONST), inc="ac")
                e_r = ACTI(rstd2[:, t:t + 1], stat[:, 20:21], AF.Exp, zero_ap, -0.5,
                           waits=_w(e_l, e_mu), inc="ac")
                e_n = TS(nmr2[:, t:t + 1], mu2[:, t:t + 1], rstd2[:, t:t + 1], -1.0,
                         ALU.mult, ALU.mult, waits=_w(e_r, e_va), inc="dv")
                ev_ln2fin[t] = (e_r, e_n)

        if "E" in phases:
            DMA("sync", bias_s[:], bias_d, "ldc")
            EV_BIAS = P.ev("ldc")
            ev_out_store = {0: None, 1: None}
            ev_wf_free = [P.ev("pec"), P.ev("pec")]
            ev_xslE_free = [P.ev("dv"), P.ev("dv")]
            ev_lntmE_free = [P.ev("pes"), P.ev("pes")]
            ev_psA_freeE = [P.ev("ac"), P.ev("ac")]
            ev_fbuf_freeE = [(P.ev("dv"),), (P.ev("dv"),)]
            for th in range(2):
                act_gate = P.ev("pec") if th == 0 else P.ev("pee")
                ev_xcb2_free = [P.ev("dv"), P.ev("dv")]
                for ob in range(16):
                    wb = ob % 2
                    e_xl = DMA("sync", xcb[wb].rearrange("p (a b) -> p a b", a=8),
                               x1_scr[ob][:, th * 8:(th + 1) * 8, :], "ldx",
                               waits=_w(P.ev("stx1"), ev_xcb2_free[wb]))
                    for tc in range(8):
                        tcg = th * 8 + tc
                        lb = tc % 2
                        xt = xcb[wb][:, tc * 256:(tc + 1) * 256]
                        e_apl = ACTI(lntmE_l[lb], xt, AF.Identity,
                                     nmr2[:, tcg:tcg + 1], rstd2[:, tcg:tcg + 1],
                                     waits=_w(e_xl, *(ev_ln2fin[tcg] or ()),
                                              ev_lntmE_free[lb]), inc="ac")
                        blocks = [lntmE_l[lb][:, oh * 128:(oh + 1) * 128]
                                  for oh in range(2)]
                        e_cv = transpose_group(
                            blocks,
                            af3[:, ob * 2:ob * 2 + 2, tc * 128:(tc + 1) * 128],
                            _w(e_apl, act_gate))
                        ev_lntmE_free[lb] = P.ev("pet")
                    ev_xcb2_free[wb] = P.ev("ac")
                e_ln2fm = P.ev("dvt")
                for ob in range(16):
                    wb = ob % 2
                    e_wl = DMA("sync", wsc[wb].rearrange("p (a b) -> p a b", a=32),
                               wf_d[ob], "ldw", waits=_w(ev_wf_free[wb]))
                    e_xl = DMA("sync", xcb[wb].rearrange("p (a b) -> p a b", a=8),
                               x1_scr[ob][:, th * 8:(th + 1) * 8, :], "ldx",
                               waits=_w(ev_xslE_free[wb], P.ev("ac") if ob < 2 else None))
                    e_evf = {}
                    for hf in range(2):
                        e_mm = None
                        for tc4 in range(4):
                            tc = hf * 4 + tc4
                            for i in range(32):
                                w0 = _w(e_wl, e_ln2fm, ev_psA_freeE[hf]) if (tc4 == 0 and i == 0) else ()
                                MM(psAh[hf][:, tc4 * 256:(tc4 + 1) * 256],
                                   af3[:, i, tc * 128:(tc + 1) * 128],
                                   wsc[wb][:, i * 256:(i + 1) * 256],
                                   i == 0, False, waits=w0)
                            e_mm = MM(psAh[hf][:, tc4 * 256:(tc4 + 1) * 256], oner_s[:],
                                      bias_s[0:1, ob * 256:(ob + 1) * 256], False, True,
                                      waits=_w(EV_BIAS) if (tc4 == 0 and ob < 2) else (),
                                      inc="pee" if tc4 == 3 else None)
                        e_evf[hf] = ACTI(fbuf[:, hf * 1024:(hf + 1) * 1024], psAh[hf],
                                         AF.Copy, 0.0, 1.0,
                                         waits=_w(e_mm, *ev_fbuf_freeE[hf]), inc="ac")
                        ev_psA_freeE[hf] = e_evf[hf]
                    ev_wf_free[wb] = P.ev("pee")
                    obuf = obA if ob % 2 == 0 else obB
                    e_o = None
                    for tc in range(8):
                        hf = tc // 4
                        e_o = TT(obuf[:, tc * 256:(tc + 1) * 256],
                                 fbuf[:, tc * 256:(tc + 1) * 256],
                                 xcb[wb][:, tc * 256:(tc + 1) * 256], ALU.add,
                                 waits=_w(e_evf[hf], e_xl, ev_out_store[ob % 2]), inc="dv")
                    ev_fbuf_freeE[0] = ((P.ev("dv")),)
                    ev_fbuf_freeE[1] = ((P.ev("dv")),)
                    ev_xslE_free[wb] = P.ev("dv")
                    e_so = DMA("scalar", out_cb[ob][:, th * 8:(th + 1) * 8, :],
                               obuf.rearrange("p (a b) -> p a b", a=8), "stout",
                               waits=_w(e_o))
                    ev_out_store[ob % 2] = e_so

        final_waits = [P.ev("stout")]
        if debug:
            for s in ("stkv", "stcx", "stx1"):
                final_waits.append(P.ev(s))
        WAIT_ONLY("gpsimd", final_waits)

        semnames = sorted(P.cnt.keys())
        semh = {}
        for n in semnames:
            semh[n] = E(nc.semaphore(n))
        block = E(nc.Block())

        def mk(engname):
            def body(eng):
                last = {}
                for fn, waits, inc, k in P.ops[engname]:
                    for s, v in waits:
                        if last.get(s, -1) < v:
                            eng.wait_ge(semh[s], v)
                            last[s] = v
                    instr = fn(eng)
                    if inc is not None:
                        instr.then_inc(semh[inc], k)
            return body

        block.sync(mk("sync"))
        block.gpsimd(mk("gpsimd"))
        block.tensor(mk("tensor"))
        block.vector(mk("vector"))
        block.scalar(mk("scalar"))

    return nc


def host_inputs(core, L0_attn_x, L0_attn_Wq, L0_attn_Wk, L0_attn_Wv, L0_attn_Wo,
                L0_attn_cos, L0_attn_sin, L0_ffn_weight, L0_ffn_bias):
    b, z = core // 2, core % 2
    chunks = own_chunks(z)
    own_pos = np.concatenate([np.arange(c * 128, (c + 1) * 128) for c in chunks])
    X = np.asarray(L0_attn_x[b], dtype=np.float32)
    cos = np.asarray(L0_attn_cos, np.float32)
    sin = np.asarray(L0_attn_sin, np.float32)

    x_own = np.ascontiguousarray(X[own_pos])
    masks = np.zeros((16, 128, 256), np.float32)
    for s in range(16):
        c = chunks[s]
        kk = np.arange(KS[s] - 256, KS[s])
        qq = c * 128 + np.arange(128)
        masks[s] = np.where(kk[None, :] <= qq[:, None], 0.0, -30000.0)

    d = {
        "xf": X,
        "xo": x_own,
        "xocb": np.ascontiguousarray(
            x_own.reshape(16, 128, 16, 256).transpose(2, 1, 0, 3)),
        "cosf": np.ascontiguousarray(cos.T),
        "sinf": np.ascontiguousarray(sin.T),
        "coso": np.ascontiguousarray(cos[own_pos].T),
        "sino": np.ascontiguousarray(sin[own_pos].T),
        "masks": masks,
        "wq": np.ascontiguousarray(
            np.asarray(L0_attn_Wq, np.float32).reshape(32, 128, 32, 128).transpose(2, 0, 1, 3)),
        "wk": np.ascontiguousarray(
            np.asarray(L0_attn_Wk, np.float32).reshape(32, 128, 8, 128).transpose(2, 0, 1, 3)),
        "wv": np.ascontiguousarray(
            np.asarray(L0_attn_Wv, np.float32).reshape(32, 128, 8, 128).transpose(2, 0, 1, 3)),
        "wo": np.ascontiguousarray(
            np.asarray(L0_attn_Wo, np.float32).reshape(32, 128, 16, 256).transpose(2, 1, 0, 3)),
        "wf": np.ascontiguousarray(
            np.asarray(L0_ffn_weight).T.reshape(32, 128, 16, 256).transpose(2, 1, 0, 3)),
        "biasf": np.asarray(L0_ffn_bias).reshape(1, D),
        "idt": np.eye(128, dtype=np.float16),
        "rt": _rot_matrix_T(),
        "oner": np.ones((1, 128), np.float16),
        "cvec": np.broadcast_to(
            np.array([EPS, 0.0, 1.0, 1.0 / D], np.float32), (128, 4)).copy(),
    }
    return d, own_pos, b


def _rot_matrix_T():
    R = np.zeros((128, 128), np.float32)
    for dd in range(64):
        R[dd, dd + 64] = -1.0
        R[dd + 64, dd] = 1.0
    return R.T.astype(np.float16)


_NC_CACHE = {}


def _get_nc(debug=False, phases=("A1", "A2B", "C", "D", "E")):
    key = (debug, tuple(phases))
    if key not in _NC_CACHE:
        _NC_CACHE[key] = build_program(debug=debug, phases=phases)
    return _NC_CACHE[key]


def kernel(**inputs):
    nc = _get_nc()
    in_maps = []
    metas = []
    for core in range(8):
        d, own_pos, b = host_inputs(core, **inputs)
        in_maps.append(d)
        metas.append((own_pos, b))
    res = run_bass_kernel_spmd(nc, in_maps, core_ids=list(range(8)))
    out = np.empty((4, T, D), np.float32)
    for core in range(8):
        own_pos, b = metas[core]
        oc = np.asarray(res.results[core]["out_cb"], np.float32)
        rows = oc.transpose(2, 1, 0, 3).reshape(TOWN, D)
        out[b][own_pos] = rows
    return out


# version 09
# speedup vs baseline: 1.1575x; 1.0063x over previous
import math
import numpy as np
import ml_dtypes
import concourse.bass as bass
import concourse.mybir as mybir
from concourse.bass_utils import run_bass_kernel_spmd
from contextlib import ExitStack

F32 = mybir.dt.float32
F16 = mybir.dt.float16
AF = mybir.ActivationFunctionType
ALU = mybir.AluOpType
AX = mybir.AxisListType

T = 4096
D = 4096
HEADS = 32
GROUPS = 8
DH = 128
TOWN = 2048
NCH = 32
SCALE = 1.0 / math.sqrt(DH)
EPS = 1e-5
KS = [256 * (s + 1) for s in range(16)]


def own_chunks(z):
    return [c for c in range(NCH) if c % 4 in ((0, 3) if z == 0 else (1, 2))]


class Plan:
    ENGINES = ("sync", "gpsimd", "tensor", "vector", "scalar")

    def __init__(self):
        self.ops = {e: [] for e in self.ENGINES}
        self.cnt = {}

    def emit(self, eng, fn, waits=(), inc=None, k=1):
        ws = tuple((s, v) for (s, v) in waits if v > 0)
        self.ops[eng].append((fn, ws, inc, k))
        if inc is None:
            return None
        self.cnt[inc] = self.cnt.get(inc, 0) + k
        return (inc, self.cnt[inc])

    def ev(self, name):
        return (name, self.cnt.get(name, 0))


def _w(*evs):
    d = {}
    for e in evs:
        if e is None:
            continue
        s, v = e
        if d.get(s, -1) < v:
            d[s] = v
    return tuple(d.items())


def build_program(debug=False, phases=("A1", "A2B", "C", "D", "E")):
    nc = bass.Bass("TRN2", target_bir_lowering=False)
    P = Plan()
    scratch_kind = "ExternalOutput" if debug else "Internal"

    xf = nc.dram_tensor("xf", [T, D], F32, kind="ExternalInput").ap()
    xo = nc.dram_tensor("xo", [TOWN, D], F32, kind="ExternalInput").ap()
    xocb = nc.dram_tensor("xocb", [16, 128, 16, 256], F32, kind="ExternalInput").ap()
    cosf_d = nc.dram_tensor("cosf", [128, T], F32, kind="ExternalInput").ap()
    sinf_d = nc.dram_tensor("sinf", [128, T], F32, kind="ExternalInput").ap()
    coso_d = nc.dram_tensor("coso", [128, TOWN], F32, kind="ExternalInput").ap()
    sino_d = nc.dram_tensor("sino", [128, TOWN], F32, kind="ExternalInput").ap()
    masks_d = nc.dram_tensor("masks", [16, 128, 256], F32, kind="ExternalInput").ap()
    wq_d = nc.dram_tensor("wq", [32, 32, 128, 128], F32, kind="ExternalInput").ap()
    wk_d = nc.dram_tensor("wk", [8, 32, 128, 128], F32, kind="ExternalInput").ap()
    wv_d = nc.dram_tensor("wv", [8, 32, 128, 128], F32, kind="ExternalInput").ap()
    wo_d = nc.dram_tensor("wo", [16, 128, 32, 256], F32, kind="ExternalInput").ap()
    wf_d = nc.dram_tensor("wf", [16, 128, 32, 256], F16, kind="ExternalInput").ap()
    bias_d = nc.dram_tensor("biasf", [1, D], F16, kind="ExternalInput").ap()
    idt_d = nc.dram_tensor("idt", [128, 128], F16, kind="ExternalInput").ap()
    rt_d = nc.dram_tensor("rt", [128, 128], F16, kind="ExternalInput").ap()
    oner_d = nc.dram_tensor("oner", [1, 128], F16, kind="ExternalInput").ap()
    cvec_d = nc.dram_tensor("cvec", [128, 4], F32, kind="ExternalInput").ap()

    k_scr = nc.dram_tensor("k_scr", [GROUPS, 128, T], F16, kind=scratch_kind).ap()
    v_scr = nc.dram_tensor("v_scr", [GROUPS, 128, T], F16, kind=scratch_kind).ap()
    ctx_scr = nc.dram_tensor("ctx_scr", [HEADS, 128, TOWN], F16, kind=scratch_kind).ap()
    x1_scr = nc.dram_tensor("x1_scr", [16, 128, 16, 256], F32, kind=scratch_kind).ap()
    out_cb = nc.dram_tensor("out_cb", [16, 128, 16, 256], F32, kind="ExternalOutput").ap()

    with ExitStack() as ctx:
        E = ctx.enter_context

        act = E(nc.sbuf_tensor("act", [128, 32 * 1024], F16))
        af3 = act[:].rearrange("p (c t) -> p c t", c=32)
        AR16 = E(nc.sbuf_tensor("AR16", [128, 38912], F16))
        AR32 = E(nc.sbuf_tensor("AR32", [128, 14464], F32))
        idt_s = E(nc.sbuf_tensor("idt_s", [128, 128], F16))
        rt_s = E(nc.sbuf_tensor("rt_s", [128, 128], F16))
        oner_s = E(nc.sbuf_tensor("oner_s", [1, 128], F16))
        cvec_s = E(nc.sbuf_tensor("cvec_s", [128, 4], F32))
        bias_s = E(nc.sbuf_tensor("bias_s", [1, D], F16))

        def v16(off, n):
            assert off + n <= 38912, (off, n)
            return AR16[:, off:off + n]

        def v32(off, n):
            assert off + n <= 14464, (off, n)
            return AR32[:, off:off + n]

        lntm = v16(0, 4096)
        sqb = v16(4096, 4096)
        lntm_l = [v16(0, 4096), v16(16384, 4096)]
        sqb_l = [v16(4096, 4096), v16(20480, 4096)]
        ev_lntp = [None, None]
        ev_sqrd = {}
        wsb = [v16(8192, 4096), v16(12288, 4096)]
        cosf = v16(24576, 4096)
        sinf = v16(28672, 4096)
        kev = v16(32768, 1024)
        r1b = v16(33792, 1024)
        r2b = v16(34816, 1024)
        kgb = v16(0, 4096)
        vgf = v16(4096, 4096)
        pbufb = v16(16384, 2048)
        ptbb = v16(18432, 2048)
        qraw = v16(22528, 1024)
        coso = v16(24576, 2048)
        sino = v16(26624, 2048)
        masksb = v16(28672, 4096)
        vtm = v16(32768, 4160)
        vt3 = vtm.rearrange("p (c n) -> p c n", n=130)
        ctxhb_l = [v16(36928, 1024), v16(21504, 1024)]
        cbuf_l = [v16(37952, 128), v16(38592, 128)]
        qh_l = [v16(23552, 1024), v16(20480, 1024)]
        r2q = v16(38080, 512)
        wsc = [v16(8192, 8192), v16(16384, 8192)]
        lntmE_l = [v16(0, 256), v16(256, 256)]
        sqbC = v16(4096, 256)

        xsl = [v32(0, 4096), v32(4096, 4096)]
        stat = v32(8192, 96)
        fbuf = v32(8288, 2048)
        obA = v32(10336, 2048)
        obB = v32(12384, 2048)
        xcb = [v32(0, 2048), v32(2048, 2048)]
        ctxA_l = [v32(4096, 132), v32(4228, 132)]
        sum1 = stat[:, 32:48]
        sum2 = stat[:, 48:64]
        mu2 = stat[:, 64:80]
        rstd2 = stat[:, 32:48]
        nmr2 = stat[:, 48:64]

        psA = E(nc.psum_tensor("psA", [128, 2048], F32))
        psB = E(nc.psum_tensor("psB", [128, 512], F32))
        psC = E(nc.psum_tensor("psC", [128, 512], F32))
        psT = E(nc.psum_tensor("psT", [128, 1024], F16))
        psT2 = E(nc.psum_tensor("psT2", [128, 1024], F16))
        psAh = [psA[:, 0:1024], psA[:, 1024:2048]]
        psTb = [psT, psT2]
        ctxb = [psC, psB]

        eps_ap = cvec_s[:, 0:1]
        zero_ap = cvec_s[:, 1:2]

        def DMA(eng, dst, src, inc, waits=()):
            return P.emit(eng, lambda e, d=dst, s=src: e.dma_start(out=d, in_=s),
                          _w(*waits), inc, 16)

        def MM(out, lhsT, rhs, start, stop, waits=(), inc=None):
            return P.emit("tensor",
                          lambda e, o=out, l=lhsT, r=rhs, st=start, sp=stop:
                          e.matmul(o, l, r, start=st, stop=sp), _w(*waits), inc)

        def TP(out, in_, waits=(), inc=None):
            return P.emit("tensor",
                          lambda e, o=out, i=in_: e.transpose(o, i, idt_s[:]),
                          _w(*waits), inc)

        def ACTI(out, in_, func, bias, scale, waits=(), inc=None):
            return P.emit("scalar",
                          lambda e, o=out, i=in_, f=func, b=bias, s=scale:
                          e.activation(o, i, f, bias=b, scale=s), _w(*waits), inc)

        def TT(out, a, b, op, waits=(), inc=None, eng="vector"):
            return P.emit(eng,
                          lambda e, o=out, x=a, y=b, p=op: e.tensor_tensor(o, x, y, op=p),
                          _w(*waits), inc, 1)

        def TS(out, in0, s1, s2, op0, op1, waits=(), inc=None, eng="vector"):
            return P.emit(eng,
                          lambda e, o=out, i=in0, a=s1, b=s2, p0=op0, p1=op1:
                          e.tensor_scalar(o, i, a, b, op0=p0, op1=p1), _w(*waits), inc, 1)

        def RED(out, in_, op, waits=(), inc=None):
            return P.emit("vector",
                          lambda e, o=out, i=in_, p=op: e.tensor_reduce(o, i, axis=AX.X, op=p),
                          _w(*waits), inc)

        def CP(out, in_, waits=(), inc=None):
            return P.emit("vector", lambda e, o=out, i=in_: e.tensor_copy(o, i),
                          _w(*waits), inc)

        def RECIP(out, in_, waits=(), inc=None):
            return P.emit("vector", lambda e, o=out, i=in_: e.reciprocal(o, i),
                          _w(*waits), inc)

        def MS(ap, val, waits=(), inc=None):
            return P.emit("gpsimd", lambda e, a=ap, v=val: e.memset(a, v),
                          _w(*waits), inc)

        def WAIT_ONLY(eng, waits):
            P.emit(eng, lambda e: None, _w(*waits), None)

        DMA("sync", idt_s[:], idt_d, "ldc")
        DMA("sync", rt_s[:], rt_d, "ldc")
        DMA("sync", oner_s[:], oner_d, "ldc")
        DMA("sync", cvec_s[:], cvec_d, "ldc")
        DMA("gpsimd", cosf, cosf_d, "ldc")
        DMA("gpsimd", sinf, sinf_d, "ldc")
        EV_CONST = P.ev("ldc")

        tp_war = [None, None]
        tp_bank = [0]
        sqb_rd = [None]

        def ln_slab(xbuf, out16, waits, sqbuf=None, stcol=0, out_war=None, sq_war=None):
            if sqbuf is None:
                sqbuf = sqb
            if out_war is None:
                out_war = P.ev("pet")
            c = stcol
            e_s1 = RED(stat[:, c:c + 1], xbuf, ALU.add, waits=waits, inc="dv")
            e_sq = ACTI(sqbuf, xbuf, AF.Square, 0.0, 1.0,
                        waits=waits + _w(sq_war), inc="ac")
            e_s2 = RED(stat[:, c + 1:c + 2], sqbuf, ALU.add, waits=(e_sq,), inc="dv")
            sqb_rd[0] = e_s2
            e_mu = TS(stat[:, c + 2:c + 3], stat[:, c:c + 1], 1.0 / D, None, ALU.mult,
                      ALU.bypass, waits=(e_s1,), inc="dv")
            e_msq = TT(stat[:, c + 3:c + 4], stat[:, c + 2:c + 3], stat[:, c + 2:c + 3],
                       ALU.mult, waits=(e_mu,), inc="dv")
            e_var = TS(stat[:, c + 4:c + 5], stat[:, c + 1:c + 2], 1.0 / D,
                       stat[:, c + 3:c + 4], ALU.mult,
                       ALU.subtract, waits=(e_s2, e_msq), inc="dv")
            e_l = ACTI(stat[:, c + 5:c + 6], stat[:, c + 4:c + 5], AF.Ln, eps_ap, 1.0,
                       waits=(e_var, EV_CONST), inc="ac")
            e_r = ACTI(stat[:, c + 6:c + 7], stat[:, c + 5:c + 6], AF.Exp, zero_ap, -0.5,
                       waits=(e_l,), inc="ac")
            e_n = TS(stat[:, c + 7:c + 8], stat[:, c + 2:c + 3], stat[:, c + 6:c + 7],
                     -1.0, ALU.mult, ALU.mult, waits=(e_r, e_mu), inc="dv")
            e_ap = ACTI(out16, xbuf, AF.Identity, stat[:, c + 7:c + 8],
                        stat[:, c + 6:c + 7],
                        waits=(e_n, e_r) + _w(out_war), inc="ac")
            return e_ap

        def transpose_group(blocks, dst, waits, evac="vector"):
            bank = tp_bank[0] % 2
            tp_bank[0] += 1
            pt = psTb[bank][:, 0:len(blocks) * 128]
            tp_ev = None
            for j, b in enumerate(blocks):
                tp_ev = TP(pt[:, j * 128:(j + 1) * 128], b,
                           waits=(waits + _w(tp_war[bank])) if j == 0 else (),
                           inc="pet")
            src = pt.rearrange("p (c t) -> p c t", c=len(blocks))
            if evac == "scalar":
                ev = P.emit("scalar",
                            lambda e, o=dst, i=src: e.activation(o, i, AF.Copy,
                                                                 bias=0.0, scale=1.0),
                            _w(tp_ev), "dct")
            else:
                ev = CP(dst, src, waits=_w(tp_ev), inc="dvt")
            tp_war[bank] = ev
            return ev

        def transpose_slab(src16, tcol, waits, war_mm):
            last = None
            for icq in range(8):
                blocks = [src16[:, (icq * 4 + j) * 128:(icq * 4 + j + 1) * 128]
                          for j in range(4)]
                dst = af3[:, icq * 4:(icq + 1) * 4, tcol:tcol + 128]
                last = transpose_group(blocks, dst, waits + _w(*war_mm))
            return last

        ev_apply = {}
        ev_kv_mm_last = None
        ev_psA_free = [None, None]
        ev_cb_free = [None, None]
        ev_ws_free = [None, None]
        ev_kev_free = None

        if "A1" in phases:
            for TC in range(4):
                tcols = slice(TC * 1024, (TC + 1) * 1024)
                for sub in range(8):
                    slab = TC * 8 + sub
                    buf = slab % 2
                    e_ld = DMA("sync", xsl[buf], xf[slab * 128:(slab + 1) * 128, :],
                               "ldx", waits=_w(ev_apply.get(slab - 2)))
                    e_ap = ln_slab(xsl[buf], lntm_l[buf], _w(e_ld), sqbuf=sqb_l[buf],
                                   stcol=buf * 8, out_war=ev_lntp[buf],
                                   sq_war=ev_sqrd.get(buf))
                    ev_sqrd[buf] = sqb_rd[0]
                    ev_apply[slab] = e_ap
                    transpose_slab(lntm_l[buf], sub * 128, _w(e_ap), (ev_kv_mm_last,))
                    ev_lntp[buf] = P.ev("pet")
                e_lnfm = P.ev("dvt")
                for idx in range(16):
                    kv, g = idx // 8, idx % 8
                    wsrc = (wk_d if kv == 0 else wv_d)[g]
                    wb = idx % 2
                    e_wl = DMA("gpsimd", wsb[wb].rearrange("p (a b) -> p a b", a=32),
                               wsrc.rearrange("a p b -> p a b"), "ldw",
                               waits=_w(ev_ws_free[wb]))
                    par = idx % 2
                    pa = psAh[par]
                    e_mm = None
                    for i in range(32):
                        w0 = _w(e_wl, e_lnfm, ev_psA_free[par]) if i == 0 else ()
                        MM(pa[:, 0:512], wsb[wb][:, i * 128:(i + 1) * 128],
                           af3[:, i, 0:512], i == 0, i == 31, waits=w0)
                        e_mm = MM(pa[:, 512:1024], wsb[wb][:, i * 128:(i + 1) * 128],
                                  af3[:, i, 512:1024], i == 0, i == 31,
                                  inc="pea" if i == 31 else None)
                    ev_ws_free[wb] = e_mm
                    if idx % 2 == 0:
                        e_ev = ACTI(kev, pa, AF.Copy, 0.0, 1.0,
                                    waits=_w(e_mm, ev_kev_free), inc="ac")
                    else:
                        e_ev = CP(kev, pa, waits=_w(e_mm, ev_kev_free), inc="dv")
                    ev_psA_free[par] = e_ev
                    if kv == 0:
                        e_r1 = []
                        for jj in range(2):
                            cb = ctxb[jj]
                            e_rk = MM(cb[:, 0:512], rt_s[:],
                                      kev[:, jj * 512:(jj + 1) * 512], True, True,
                                      waits=_w(e_ev, ev_cb_free[jj]), inc="per")
                            scols = slice(TC * 1024 + jj * 512, TC * 1024 + (jj + 1) * 512)
                            e1 = TT(r1b[:, jj * 512:(jj + 1) * 512], cb[:, 0:512],
                                    sinf[:, scols], ALU.mult,
                                    waits=_w(e_rk, EV_CONST), inc="dv")
                            ev_cb_free[jj] = e1
                            e_r1.append(e1)
                        e_r2 = TT(r2b, kev, cosf[:, tcols], ALU.mult,
                                  waits=_w(e_ev, EV_CONST), inc="dv")
                        e_kr = TT(kev, r1b, r2b, ALU.add,
                                  waits=_w(*e_r1, e_r2), inc="dv")
                        e_st = DMA("scalar", k_scr[g][:, tcols], kev, "stkv", waits=_w(e_kr))
                    else:
                        e_st = DMA("scalar", v_scr[g][:, tcols], kev, "stkv", waits=_w(e_ev))
                    ev_kev_free = e_st
                ev_kv_mm_last = P.ev("pea")

        ev_qproj_last = None
        ev_ctx_store = [None, None]
        ev_cbuf_free = [None, None]
        ev_qh_war = [None, None]
        ev_kg_free = None
        ev_vg_free = None
        ev_vtm_free = None
        ev_qraw_free = [None, None]
        ev_pbuf_free = [None, None]
        ev_ptb_free = [None, None]
        ev_kvready = (None, None)
        ev_maskload = None

        if "A2B" in phases:
            DMA("gpsimd", coso, coso_d, "ldc", waits=_w(P.ev("dv")))
            DMA("gpsimd", sino, sino_d, "ldc")
            ev_maskload = DMA("gpsimd", masksb.rearrange("p (s n) -> p s n", s=16),
                              masks_d.rearrange("s p n -> p s n"), "ldc",
                              waits=_w(P.ev("dv")))
            EV_BC = P.ev("ldc")
            e_msv = MS(vtm, 1.0, waits=_w(P.ev("stkv"), P.ev("dv"), P.ev("ac")), inc="gp")

            for ph in range(2):
                gate_ac = P.ev("ac")
                gate_gp = P.ev("gp")
                for sub in range(8):
                    slab = ph * 8 + sub
                    buf = sub % 2
                    war = ev_apply.get(("o", ph, sub - 2))
                    e_ld = DMA("sync", xsl[buf], xo[slab * 128:(slab + 1) * 128, :],
                               "ldx", waits=_w(war, gate_ac if sub < 2 else None))
                    e_ap = ln_slab(xsl[buf], lntm, _w(e_ld), stcol=0,
                                   out_war=ev_lntp[0] if sub else None,
                                   sq_war=ev_sqrd.get("a2"))
                    ev_sqrd["a2"] = sqb_rd[0]
                    ev_apply[("o", ph, sub)] = e_ap
                    transpose_slab(lntm, sub * 128, _w(e_ap),
                                   (ev_kv_mm_last, ev_qproj_last))
                    ev_lntp[0] = P.ev("pet")
                e_lnfm = P.ev("dvt")

                for hp in range(HEADS // 2):
                    g = (2 * hp) // 4
                    e_wl = [None, None]
                    for lane in range(2):
                        h = 2 * hp + lane
                        e_wl[lane] = DMA("gpsimd",
                                         wsb[lane].rearrange("p (a b) -> p a b", a=32),
                                         wq_d[h].rearrange("a p b -> p a b"), "ldw",
                                         waits=_w(ev_ws_free[lane]))
                    if hp % 2 == 0:
                        a2done = (e_lnfm, P.ev("dv") if hp == 0 else None,
                                  P.ev("ac") if hp == 0 else None,
                                  P.ev("stkv") if (ph == 0 and hp == 0) else None)
                        e_kg = DMA("sync", kgb, k_scr[g], "ldkv",
                                   waits=_w(ev_kg_free, *a2done))
                        e_vg = DMA("sync", vgf, v_scr[g], "ldkv",
                                   waits=_w(ev_vg_free, *a2done))
                        e_vt = None
                        for kq in range(8):
                            blocks = [vgf[:, (kq * 4 + j) * 128:(kq * 4 + j + 1) * 128]
                                      for j in range(4)]
                            e_vt = transpose_group(
                                blocks, vt3[:, kq * 4:(kq + 1) * 4, 0:128],
                                _w(e_vg, e_msv, ev_vtm_free))
                        ev_kvready = (e_kg, e_vt)
                    e_qh_l = [None, None]
                    for lane in range(2):
                        qhb = qh_l[lane]
                        for tcq in range(2):
                            qc = slice(tcq * 512, (tcq + 1) * 512)
                            e_mm = None
                            for i in range(32):
                                w0 = _w(e_wl[lane], e_lnfm, ev_cb_free[1]) if i == 0 else ()
                                e_mm = MM(psB[:, 0:512],
                                          wsb[lane][:, i * 128:(i + 1) * 128],
                                          af3[:, i, qc], i == 0, i == 31, waits=w0,
                                          inc="peq" if i == 31 else None)
                            e_qe = ACTI(qraw[:, qc], psB[:, 0:512], AF.Copy, 0.0, 1.0,
                                        waits=_w(e_mm, *(ev_qraw_free[tcq] or ())),
                                        inc="ac")
                            ev_cb_free[1] = e_qe
                            e_rq = MM(psC[:, 0:512], rt_s[:], qraw[:, qc], True, True,
                                      waits=_w(e_qe, ev_cb_free[0]), inc="per")
                            ocs = slice(ph * 1024 + tcq * 512, ph * 1024 + (tcq + 1) * 512)
                            e_q1 = TT(qhb[:, qc], psC[:, 0:512], sino[:, ocs], ALU.mult,
                                      waits=_w(e_rq, EV_BC, ev_qh_war[lane]), inc="dv")
                            ev_cb_free[0] = e_q1
                            e_q2 = TT(r2q, qraw[:, qc], coso[:, ocs], ALU.mult,
                                      waits=_w(e_qe, EV_BC), inc="dv")
                            e_qh_l[lane] = TT(qhb[:, qc], qhb[:, qc], r2q, ALU.add,
                                              waits=_w(e_q1, e_q2), inc="dv")
                            ev_qraw_free[tcq] = (e_rq, e_q2)

                    subrows = []
                    for ls in range(8):
                        s = ph * 8 + ls
                        ks = KS[s]
                        n = (ks + 1023) // 1024
                        for r in range(n):
                            subrows.append(dict(
                                s=s, ls=ls, r=r, k0=r * 1024,
                                k1=min((r + 1) * 1024, ks), ks=ks,
                                first=(r == 0), last=(r == n - 1), nsub=n))

                    slot_state = [{}, {}]
                    sr_l = [None, None]

                    def emit_S(lane, srp, kvr=ev_kvready, e_qh_l=e_qh_l):
                        sr = dict(srp)
                        par = lane
                        nk = sr["k1"] - sr["k0"]
                        nj = (nk + 511) // 512
                        qtile = qh_l[lane][:, sr["ls"] * 128:(sr["ls"] + 1) * 128]
                        e_chunks = []
                        for jj in range(nj):
                            w = min(512, nk - jj * 512)
                            w0 = _w(e_qh_l[lane], kvr[0], ev_psA_free[par]) if jj == 0 else ()
                            e_chunks.append(
                                MM(psAh[par][:, jj * 512:jj * 512 + w], qtile,
                                   kgb[:, sr["k0"] + jj * 512:sr["k0"] + jj * 512 + w],
                                   True, True, waits=w0, inc="pes"))
                        sr["e_chunks"] = e_chunks
                        return sr

                    def emit_max(lane, sr):
                        par = lane
                        nk = sr["k1"] - sr["k0"]
                        nj = (nk + 511) // 512
                        e_chunks = sr["e_chunks"]
                        e_s = e_chunks[-1]
                        e_mask = None
                        if sr["last"]:
                            lo = sr["ks"] - 256 - sr["k0"]
                            mreg = psAh[par][:, lo:lo + 256]
                            e_mask = TT(mreg, mreg,
                                        masksb[:, sr["s"] * 256:(sr["s"] + 1) * 256],
                                        ALU.add, waits=_w(e_s, ev_maskload), inc="dv")
                        if nj == 2:
                            e_m0 = RED(stat[:, 16 + par:17 + par], psAh[par][:, 0:512],
                                       ALU.max, waits=_w(e_chunks[0]), inc="dv")
                            e_m1 = RED(stat[:, 8 + par:9 + par],
                                       psAh[par][:, 512:nk], ALU.max,
                                       waits=_w(e_s, e_mask), inc="dv")
                            e_max = TT(stat[:, 8 + par:9 + par], stat[:, 8 + par:9 + par],
                                       stat[:, 16 + par:17 + par], ALU.max,
                                       waits=_w(e_m0, e_m1), inc="dv")
                        else:
                            e_max = RED(stat[:, 8 + par:9 + par], psAh[par][:, 0:nk],
                                        ALU.max, waits=_w(e_s, e_mask), inc="dv")
                        st0 = slot_state[lane]
                        mbase = 10 if lane == 0 else 24
                        if sr["first"]:
                            mcol = mbase
                            old = mbase
                            e_mrun = CP(stat[:, mbase:mbase + 1], stat[:, 8 + par:9 + par],
                                        waits=_w(e_max), inc="dv")
                        else:
                            mcol = mbase + sr["r"] % 2
                            old = st0["mcol"]
                            e_mrun = TT(stat[:, mcol:mcol + 1], stat[:, old:old + 1],
                                        stat[:, 8 + par:9 + par], ALU.max,
                                        waits=_w(st0["e_mrun"], e_max), inc="dv")
                        e_nb = TS(stat[:, 14 + par:15 + par], stat[:, mcol:mcol + 1],
                                  -SCALE, None, ALU.mult, ALU.bypass,
                                  waits=_w(e_mrun), inc="gp", eng="gpsimd")
                        if not sr["first"]:
                            sr["e_fd"] = TT(stat[:, 12 + par:13 + par],
                                            stat[:, old:old + 1],
                                            stat[:, mcol:mcol + 1], ALU.subtract,
                                            waits=_w(e_mrun), inc="gp", eng="gpsimd")
                        st0["mcol"] = mcol
                        st0["e_mrun"] = e_mrun
                        sr["e_nb"] = e_nb
                        sr["e_mrun"], sr["mcol"] = e_mrun, mcol

                    def emit_exp(lane, sr):
                        par = lane
                        nk = sr["k1"] - sr["k0"]
                        e_exp = ACTI(pbufb[:, par * 1024:par * 1024 + nk],
                                     psAh[par][:, 0:nk], AF.Exp,
                                     stat[:, 14 + par:15 + par], SCALE,
                                     waits=_w(sr["e_nb"], ev_pbuf_free[par]), inc="ac")
                        ev_psA_free[par] = e_exp
                        if not sr["first"]:
                            e_f = ACTI(stat[:, 28 + par:29 + par],
                                       stat[:, 12 + par:13 + par], AF.Exp,
                                       zero_ap, SCALE, waits=_w(sr["e_fd"]), inc="ac")
                            sr["e_f"] = e_f
                            ev_psA_free[par] = e_f
                        sr["e_exp"] = e_exp

                    def emit_TP(lane, sr):
                        par = lane
                        nk = sr["k1"] - sr["k0"]
                        nkb = nk // 128
                        ev_evac = None
                        sr["ev_evacs"] = []
                        for kq in range((nkb + 3) // 4):
                            nblk = min(4, nkb - kq * 4)
                            blocks = [pbufb[:, par * 1024 + (kq * 4 + jj) * 128:
                                            par * 1024 + (kq * 4 + jj + 1) * 128]
                                      for jj in range(nblk)]
                            dst = ptbb[:, par * 1024 + kq * 512:
                                       par * 1024 + kq * 512 + nblk * 128]
                            ev_evac = transpose_group(
                                blocks, dst.rearrange("p (c t) -> p c t", c=nblk),
                                _w(sr["e_exp"], ev_ptb_free[par]),
                                evac="scalar" if (tp_bank[0] % 2) else "vector")
                            sr["ev_evacs"].append(ev_evac)
                        ev_pbuf_free[par] = P.ev("pet")
                        sr["ev_evac"] = ev_evac

                    def emit_PV_combine(lane, kvr=ev_kvready, sr=None):
                        par = lane
                        nk = sr["k1"] - sr["k0"]
                        nkb = nk // 128
                        ev_evac = sr["ev_evac"]
                        cb = ctxb[lane]
                        ctxA = ctxA_l[lane]
                        e_pv = None
                        for kc in range(nkb):
                            if kc == 0:
                                w0 = _w(sr["ev_evacs"][0], kvr[1], ev_cb_free[lane])
                            elif kc == 4:
                                w0 = _w(*sr["ev_evacs"])
                            else:
                                w0 = ()
                            e_pv = MM(cb[:, 0:129],
                                      ptbb[:, par * 1024 + kc * 128:par * 1024 + (kc + 1) * 128],
                                      vt3[:, sr["k0"] // 128 + kc, 0:129],
                                      kc == 0, kc == nkb - 1, waits=w0,
                                      inc="pev" if kc == nkb - 1 else None)
                        ev_ptb_free[par] = e_pv
                        st0 = slot_state[lane]
                        if sr["nsub"] > 1:
                            if sr["first"]:
                                e_c = CP(ctxA[:, 0:129], cb[:, 0:129],
                                         waits=_w(e_pv, st0.get("e_ctxA_rd")),
                                         inc="dv")
                            else:
                                e_sc = TS(ctxA[:, 0:129], ctxA[:, 0:129],
                                          stat[:, 28 + par:29 + par],
                                          None, ALU.mult, ALU.bypass,
                                          waits=_w(sr["e_f"], st0["e_ctx"]),
                                          inc="dv")
                                e_c = TT(ctxA[:, 0:129], ctxA[:, 0:129], cb[:, 0:129],
                                         ALU.add, waits=_w(e_sc, e_pv), inc="dv")
                            st0["e_ctx"] = e_c
                            ev_cb_free[lane] = e_c
                        if sr["last"]:
                            if sr["nsub"] == 1:
                                lsrc, csrc, e_fin = cb[:, 128:129], cb[:, 0:128], _w(e_pv)
                            else:
                                lsrc, csrc, e_fin = (ctxA[:, 128:129], ctxA[:, 0:128],
                                                     _w(st0["e_ctx"]))
                            rlc = 30 + lane
                            e_rl = RECIP(stat[:, rlc:rlc + 1], lsrc, waits=e_fin, inc="dv")
                            e_cf = ACTI(cbuf_l[lane], csrc, AF.Identity, zero_ap,
                                        stat[:, rlc:rlc + 1],
                                        waits=_w(e_rl, ev_cbuf_free[lane]), inc="ac")
                            if sr["nsub"] == 1:
                                ev_cb_free[lane] = e_cf
                            else:
                                st0["e_ctxA_rd"] = e_cf
                            e_cv = transpose_group(
                                [cbuf_l[lane]],
                                ctxhb_l[lane][:, sr["ls"] * 128:(sr["ls"] + 1) * 128]
                                .rearrange("p (c t) -> p c t", c=1),
                                _w(e_cf, ev_ctx_store[lane]))
                            ev_cbuf_free[lane] = e_cv
                            sr["e_cv"] = e_cv

                    have_prev = False
                    e_cv_last = [None, None]
                    prev_srs = [None, None]
                    for srp in subrows:
                        cur = [emit_S(0, srp), emit_S(1, srp)]
                        emit_max(0, cur[0])
                        emit_max(1, cur[1])
                        if have_prev:
                            for lane in range(2):
                                emit_TP(lane, prev_srs[lane])
                                emit_PV_combine(lane, sr=prev_srs[lane])
                                if prev_srs[lane].get("e_cv"):
                                    e_cv_last[lane] = prev_srs[lane]["e_cv"]
                        emit_exp(0, cur[0])
                        emit_exp(1, cur[1])
                        prev_srs = cur
                        have_prev = True
                    emit_TP(0, prev_srs[0])
                    emit_TP(1, prev_srs[1])
                    for lane in range(2):
                        emit_PV_combine(lane, sr=prev_srs[lane])
                        e_cv_last[lane] = prev_srs[lane]["e_cv"]

                    for lane in range(2):
                        h = 2 * hp + lane
                        e_cs = DMA("scalar", ctx_scr[h][:, ph * 1024:(ph + 1) * 1024],
                                   ctxhb_l[lane], "stcx", waits=_w(e_cv_last[lane]))
                        ev_ctx_store[lane] = e_cs
                        ev_qh_war[lane] = P.ev("pes")
                    ev_ws_free[0] = P.ev("peq")
                    ev_ws_free[1] = P.ev("peq")
                    if hp % 2 == 1:
                        ev_kg_free = P.ev("pes")
                        ev_vg_free = P.ev("pet")
                        ev_vtm_free = P.ev("pev")
                ev_qproj_last = P.ev("peq")

        ev_sum = [None] * 16
        if "C" in phases:
            gate_c = (P.ev("pet"), P.ev("pev"), P.ev("pes"), P.ev("per"),
                      P.ev("dv"), P.ev("ac"))
            e_z1 = MS(sum1, 0.0, waits=_w(P.ev("dv"), P.ev("ac")), inc="gp")
            e_z2 = MS(sum2, 0.0, inc="gp")
            for t in range(16):
                ev_sum[t] = (e_z1, e_z2)
            ev_ws_free_c = [gate_c, gate_c]
            ev_xcb_free = [(P.ev("ac"),), (P.ev("ac"),)]
            ev_fbuf_free = [(P.ev("ac"),), (P.ev("ac"),)]
            ev_psA_free = [P.ev("ac"), P.ev("ac")]
            ev_x1b_free = None
            ev_sq_free = sqb_rd[0]
            for th in range(2):
                act_gate = P.ev("peq") if th == 0 else P.ev("pec")
                e_cl = None
                for q8 in range(4):
                    src = ctx_scr[q8 * 8:(q8 + 1) * 8, :, th * 1024:(th + 1) * 1024]
                    e_cl = DMA("sync", af3[:, q8 * 8:(q8 + 1) * 8, :],
                               src.rearrange("h p t -> p h t"), "ldcx",
                               waits=_w(P.ev("stcx"), act_gate) if q8 == 0 else ())
                for ob in range(16):
                    wb = ob % 2
                    e_wl = DMA("gpsimd", wsc[wb].rearrange("p (a b) -> p a b", a=32),
                               wo_d[ob], "ldw", waits=_w(*ev_ws_free_c[wb]))
                    e_xl = DMA("sync", xcb[wb].rearrange("p (a b) -> p a b", a=8),
                               xocb[ob][:, th * 8:(th + 1) * 8, :], "ldx",
                               waits=_w(*ev_xcb_free[wb]))
                    e_evc = {}
                    fb_c = fbuf if ob % 2 == 0 else obB
                    for hf in range(2):
                        e_mm = None
                        for tc4 in range(4):
                            tc = hf * 4 + tc4
                            for i in range(32):
                                w0 = _w(e_wl, e_cl, ev_psA_free[hf]) if (tc4 == 0 and i == 0) else ()
                                e_mm = MM(psAh[hf][:, tc4 * 256:(tc4 + 1) * 256],
                                          af3[:, i, tc * 128:(tc + 1) * 128],
                                          wsc[wb][:, i * 256:(i + 1) * 256],
                                          i == 0, i == 31, waits=w0,
                                          inc="pec" if (tc4 == 3 and i == 31) else None)
                        e_evc[hf] = ACTI(fb_c[:, hf * 1024:(hf + 1) * 1024], psAh[hf],
                                         AF.Copy, 0.0, 1.0,
                                         waits=_w(e_mm, *ev_fbuf_free[ob % 2]), inc="ac")
                        ev_psA_free[hf] = e_evc[hf]
                    ev_ws_free_c[wb] = ((P.ev("pec")),)
                    for tc in range(8):
                        hf = tc // 4
                        tcg = th * 8 + tc
                        ft = fb_c[:, tc * 256:(tc + 1) * 256]
                        xt = xcb[wb][:, tc * 256:(tc + 1) * 256]
                        x1t = obA[:, tc * 256:(tc + 1) * 256]
                        e_x1 = TT(x1t, ft, xt, ALU.add,
                                  waits=_w(e_evc[hf], e_xl, ev_x1b_free), inc="dv")
                        e_p1 = RED(stat[:, 16:17], x1t, ALU.add, waits=_w(e_x1), inc="dv")
                        e_a1 = TT(sum1[:, tcg:tcg + 1], sum1[:, tcg:tcg + 1],
                                  stat[:, 16:17], ALU.add,
                                  waits=_w(e_p1, *ev_sum[tcg]), inc="dv")
                        e_sq = ACTI(sqbC, x1t, AF.Square, 0.0, 1.0,
                                    waits=_w(e_x1, ev_sq_free), inc="ac")
                        e_p2 = RED(stat[:, 17:18], sqbC, ALU.add, waits=_w(e_sq), inc="dv")
                        ev_sq_free = e_p2
                        e_a2 = TT(sum2[:, tcg:tcg + 1], sum2[:, tcg:tcg + 1],
                                  stat[:, 17:18], ALU.add,
                                  waits=_w(e_p2, *ev_sum[tcg]), inc="dv")
                        ev_sum[tcg] = (e_a1, e_a2)
                    ev_fbuf_free[ob % 2] = ((P.ev("dv")),)
                    e_sx = DMA("scalar", x1_scr[ob][:, th * 8:(th + 1) * 8, :],
                               obA.rearrange("p (a b) -> p a b", a=8), "stx1",
                               waits=_w(*[x for t2 in range(8) for x in ev_sum[th * 8 + t2]]))
                    ev_x1b_free = e_sx
                    ev_xcb_free[wb] = ((P.ev("dv")),)

        ev_ln2fin = [None] * 16
        if "D" in phases:
            for t in range(16):
                e_mu = TS(mu2[:, t:t + 1], sum1[:, t:t + 1], 1.0 / D, None, ALU.mult,
                          ALU.bypass, waits=_w(*(ev_sum[t] or ())), inc="dv")
                e_ms = TT(stat[:, 18:19], mu2[:, t:t + 1], mu2[:, t:t + 1], ALU.mult,
                          waits=_w(e_mu), inc="dv")
                e_va = TS(stat[:, 19:20], sum2[:, t:t + 1], 1.0 / D, stat[:, 18:19],
                          ALU.mult, ALU.subtract,
                          waits=_w(ev_sum[t][1] if ev_sum[t] else None, e_ms), inc="dv")
                e_l = ACTI(stat[:, 20:21], stat[:, 19:20], AF.Ln, eps_ap, 1.0,
                           waits=_w(e_va, EV_CONST), inc="ac")
                e_r = ACTI(rstd2[:, t:t + 1], stat[:, 20:21], AF.Exp, zero_ap, -0.5,
                           waits=_w(e_l, e_mu), inc="ac")
                e_n = TS(nmr2[:, t:t + 1], mu2[:, t:t + 1], rstd2[:, t:t + 1], -1.0,
                         ALU.mult, ALU.mult, waits=_w(e_r, e_va), inc="dv")
                ev_ln2fin[t] = (e_r, e_n)

        if "E" in phases:
            DMA("sync", bias_s[:], bias_d, "ldc")
            EV_BIAS = P.ev("ldc")
            ev_out_store = {0: None, 1: None}
            ev_wf_free = [P.ev("pec"), P.ev("pec")]
            ev_xslE_free = [P.ev("dv"), P.ev("dv")]
            ev_lntmE_free = [P.ev("pes"), P.ev("pes")]
            ev_psA_freeE = [P.ev("ac"), P.ev("ac")]
            ev_fbuf_freeE = [(P.ev("dv"),), (P.ev("dv"),)]
            for th in range(2):
                act_gate = P.ev("pec") if th == 0 else P.ev("pee")
                ev_xcb2_free = [P.ev("dv"), P.ev("dv")]
                for ob in range(16):
                    wb = ob % 2
                    e_xl = DMA("sync", xcb[wb].rearrange("p (a b) -> p a b", a=8),
                               x1_scr[ob][:, th * 8:(th + 1) * 8, :], "ldx",
                               waits=_w(P.ev("stx1"), ev_xcb2_free[wb]))
                    for tc in range(8):
                        tcg = th * 8 + tc
                        lb = tc % 2
                        xt = xcb[wb][:, tc * 256:(tc + 1) * 256]
                        e_apl = ACTI(lntmE_l[lb], xt, AF.Identity,
                                     nmr2[:, tcg:tcg + 1], rstd2[:, tcg:tcg + 1],
                                     waits=_w(e_xl, *(ev_ln2fin[tcg] or ()),
                                              ev_lntmE_free[lb]), inc="ac")
                        blocks = [lntmE_l[lb][:, oh * 128:(oh + 1) * 128]
                                  for oh in range(2)]
                        e_cv = transpose_group(
                            blocks,
                            af3[:, ob * 2:ob * 2 + 2, tc * 128:(tc + 1) * 128],
                            _w(e_apl, act_gate))
                        ev_lntmE_free[lb] = P.ev("pet")
                    ev_xcb2_free[wb] = P.ev("ac")
                e_ln2fm = P.ev("dvt")
                for ob in range(16):
                    wb = ob % 2
                    e_wl = DMA("sync", wsc[wb].rearrange("p (a b) -> p a b", a=32),
                               wf_d[ob], "ldw", waits=_w(ev_wf_free[wb]))
                    e_xl = DMA("sync", xcb[wb].rearrange("p (a b) -> p a b", a=8),
                               x1_scr[ob][:, th * 8:(th + 1) * 8, :], "ldx",
                               waits=_w(ev_xslE_free[wb], P.ev("ac") if ob < 2 else None))
                    e_evf = {}
                    for hf in range(2):
                        e_mm = None
                        for tc4 in range(4):
                            tc = hf * 4 + tc4
                            for i in range(32):
                                w0 = _w(e_wl, e_ln2fm, ev_psA_freeE[hf]) if (tc4 == 0 and i == 0) else ()
                                MM(psAh[hf][:, tc4 * 256:(tc4 + 1) * 256],
                                   af3[:, i, tc * 128:(tc + 1) * 128],
                                   wsc[wb][:, i * 256:(i + 1) * 256],
                                   i == 0, False, waits=w0)
                            e_mm = MM(psAh[hf][:, tc4 * 256:(tc4 + 1) * 256], oner_s[:],
                                      bias_s[0:1, ob * 256:(ob + 1) * 256], False, True,
                                      waits=_w(EV_BIAS) if (tc4 == 0 and ob < 2) else (),
                                      inc="pee" if tc4 == 3 else None)
                        e_evf[hf] = ACTI(fbuf[:, hf * 1024:(hf + 1) * 1024], psAh[hf],
                                         AF.Copy, 0.0, 1.0,
                                         waits=_w(e_mm, *ev_fbuf_freeE[hf]), inc="ac")
                        ev_psA_freeE[hf] = e_evf[hf]
                    ev_wf_free[wb] = P.ev("pee")
                    obuf = obA if ob % 2 == 0 else obB
                    e_o = None
                    for tc in range(8):
                        hf = tc // 4
                        e_o = TT(obuf[:, tc * 256:(tc + 1) * 256],
                                 fbuf[:, tc * 256:(tc + 1) * 256],
                                 xcb[wb][:, tc * 256:(tc + 1) * 256], ALU.add,
                                 waits=_w(e_evf[hf], e_xl, ev_out_store[ob % 2]), inc="dv")
                    ev_fbuf_freeE[0] = ((P.ev("dv")),)
                    ev_fbuf_freeE[1] = ((P.ev("dv")),)
                    ev_xslE_free[wb] = P.ev("dv")
                    e_so = DMA("scalar", out_cb[ob][:, th * 8:(th + 1) * 8, :],
                               obuf.rearrange("p (a b) -> p a b", a=8), "stout",
                               waits=_w(e_o))
                    ev_out_store[ob % 2] = e_so

        final_waits = [P.ev("stout")]
        if debug:
            for s in ("stkv", "stcx", "stx1"):
                final_waits.append(P.ev(s))
        WAIT_ONLY("gpsimd", final_waits)

        semnames = sorted(P.cnt.keys())
        semh = {}
        for n in semnames:
            semh[n] = E(nc.semaphore(n))
        block = E(nc.Block())

        def mk(engname):
            def body(eng):
                last = {}
                for fn, waits, inc, k in P.ops[engname]:
                    for s, v in waits:
                        if last.get(s, -1) < v:
                            eng.wait_ge(semh[s], v)
                            last[s] = v
                    instr = fn(eng)
                    if inc is not None:
                        instr.then_inc(semh[inc], k)
            return body

        block.sync(mk("sync"))
        block.gpsimd(mk("gpsimd"))
        block.tensor(mk("tensor"))
        block.vector(mk("vector"))
        block.scalar(mk("scalar"))

    return nc


def host_inputs(core, L0_attn_x, L0_attn_Wq, L0_attn_Wk, L0_attn_Wv, L0_attn_Wo,
                L0_attn_cos, L0_attn_sin, L0_ffn_weight, L0_ffn_bias):
    b, z = core // 2, core % 2
    chunks = own_chunks(z)
    own_pos = np.concatenate([np.arange(c * 128, (c + 1) * 128) for c in chunks])
    X = np.asarray(L0_attn_x[b], dtype=np.float32)
    cos = np.asarray(L0_attn_cos, np.float32)
    sin = np.asarray(L0_attn_sin, np.float32)

    x_own = np.ascontiguousarray(X[own_pos])
    masks = np.zeros((16, 128, 256), np.float32)
    for s in range(16):
        c = chunks[s]
        kk = np.arange(KS[s] - 256, KS[s])
        qq = c * 128 + np.arange(128)
        masks[s] = np.where(kk[None, :] <= qq[:, None], 0.0, -30000.0)

    d = {
        "xf": X,
        "xo": x_own,
        "xocb": np.ascontiguousarray(
            x_own.reshape(16, 128, 16, 256).transpose(2, 1, 0, 3)),
        "cosf": np.ascontiguousarray(cos.T),
        "sinf": np.ascontiguousarray(sin.T),
        "coso": np.ascontiguousarray(cos[own_pos].T),
        "sino": np.ascontiguousarray(sin[own_pos].T),
        "masks": masks,
        "wq": np.ascontiguousarray(
            np.asarray(L0_attn_Wq, np.float32).reshape(32, 128, 32, 128).transpose(2, 0, 1, 3)),
        "wk": np.ascontiguousarray(
            np.asarray(L0_attn_Wk, np.float32).reshape(32, 128, 8, 128).transpose(2, 0, 1, 3)),
        "wv": np.ascontiguousarray(
            np.asarray(L0_attn_Wv, np.float32).reshape(32, 128, 8, 128).transpose(2, 0, 1, 3)),
        "wo": np.ascontiguousarray(
            np.asarray(L0_attn_Wo, np.float32).reshape(32, 128, 16, 256).transpose(2, 1, 0, 3)),
        "wf": np.ascontiguousarray(
            np.asarray(L0_ffn_weight).T.reshape(32, 128, 16, 256).transpose(2, 1, 0, 3)),
        "biasf": np.asarray(L0_ffn_bias).reshape(1, D),
        "idt": np.eye(128, dtype=np.float16),
        "rt": _rot_matrix_T(),
        "oner": np.ones((1, 128), np.float16),
        "cvec": np.broadcast_to(
            np.array([EPS, 0.0, 1.0, 1.0 / D], np.float32), (128, 4)).copy(),
    }
    return d, own_pos, b


def _rot_matrix_T():
    R = np.zeros((128, 128), np.float32)
    for dd in range(64):
        R[dd, dd + 64] = -1.0
        R[dd + 64, dd] = 1.0
    return R.T.astype(np.float16)


_NC_CACHE = {}


def _get_nc(debug=False, phases=("A1", "A2B", "C", "D", "E")):
    key = (debug, tuple(phases))
    if key not in _NC_CACHE:
        _NC_CACHE[key] = build_program(debug=debug, phases=phases)
    return _NC_CACHE[key]


def kernel(**inputs):
    nc = _get_nc()
    in_maps = []
    metas = []
    for core in range(8):
        d, own_pos, b = host_inputs(core, **inputs)
        in_maps.append(d)
        metas.append((own_pos, b))
    res = run_bass_kernel_spmd(nc, in_maps, core_ids=list(range(8)))
    out = np.empty((4, T, D), np.float32)
    for core in range(8):
        own_pos, b = metas[core]
        oc = np.asarray(res.results[core]["out_cb"], np.float32)
        rows = oc.transpose(2, 1, 0, 3).reshape(TOWN, D)
        out[b][own_pos] = rows
    return out


# version 10
# speedup vs baseline: 1.1789x; 1.0185x over previous
import math
import numpy as np
import ml_dtypes
import concourse.bass as bass
import concourse.mybir as mybir
from concourse.bass_utils import run_bass_kernel_spmd
from contextlib import ExitStack

F32 = mybir.dt.float32
F16 = mybir.dt.float16
AF = mybir.ActivationFunctionType
ALU = mybir.AluOpType
AX = mybir.AxisListType

T = 4096
D = 4096
HEADS = 32
GROUPS = 8
DH = 128
TOWN = 2048
NCH = 32
SCALE = 1.0 / math.sqrt(DH)
EPS = 1e-5
KS = [256 * (s + 1) for s in range(16)]


def own_chunks(z):
    return [c for c in range(NCH) if c % 4 in ((0, 3) if z == 0 else (1, 2))]


class Plan:
    ENGINES = ("sync", "gpsimd", "tensor", "vector", "scalar")

    def __init__(self):
        self.ops = {e: [] for e in self.ENGINES}
        self.cnt = {}

    def emit(self, eng, fn, waits=(), inc=None, k=1):
        ws = tuple((s, v) for (s, v) in waits if v > 0)
        self.ops[eng].append((fn, ws, inc, k))
        if inc is None:
            return None
        self.cnt[inc] = self.cnt.get(inc, 0) + k
        return (inc, self.cnt[inc])

    def ev(self, name):
        return (name, self.cnt.get(name, 0))


def _w(*evs):
    d = {}
    for e in evs:
        if e is None:
            continue
        s, v = e
        if d.get(s, -1) < v:
            d[s] = v
    return tuple(d.items())


def build_program(debug=False, phases=("A1", "A2B", "C", "D", "E")):
    nc = bass.Bass("TRN2", target_bir_lowering=False)
    P = Plan()
    scratch_kind = "ExternalOutput" if debug else "Internal"

    xf = nc.dram_tensor("xf", [T, D], F32, kind="ExternalInput").ap()
    xo = nc.dram_tensor("xo", [TOWN, D], F32, kind="ExternalInput").ap()
    xocb = nc.dram_tensor("xocb", [16, 128, 16, 256], F32, kind="ExternalInput").ap()
    cosf_d = nc.dram_tensor("cosf", [128, T], F32, kind="ExternalInput").ap()
    sinf_d = nc.dram_tensor("sinf", [128, T], F32, kind="ExternalInput").ap()
    coso_d = nc.dram_tensor("coso", [128, TOWN], F32, kind="ExternalInput").ap()
    sino_d = nc.dram_tensor("sino", [128, TOWN], F32, kind="ExternalInput").ap()
    masks_d = nc.dram_tensor("masks", [16, 128, 256], F32, kind="ExternalInput").ap()
    wq_d = nc.dram_tensor("wq", [32, 32, 128, 128], F32, kind="ExternalInput").ap()
    wk_d = nc.dram_tensor("wk", [8, 32, 128, 128], F32, kind="ExternalInput").ap()
    wv_d = nc.dram_tensor("wv", [8, 32, 128, 128], F32, kind="ExternalInput").ap()
    wo_d = nc.dram_tensor("wo", [16, 128, 32, 256], F32, kind="ExternalInput").ap()
    wf_d = nc.dram_tensor("wf", [16, 128, 32, 256], F16, kind="ExternalInput").ap()
    bias_d = nc.dram_tensor("biasf", [1, D], F16, kind="ExternalInput").ap()
    idt_d = nc.dram_tensor("idt", [128, 128], F16, kind="ExternalInput").ap()
    rt_d = nc.dram_tensor("rt", [128, 128], F16, kind="ExternalInput").ap()
    oner_d = nc.dram_tensor("oner", [1, 128], F16, kind="ExternalInput").ap()
    cvec_d = nc.dram_tensor("cvec", [128, 4], F32, kind="ExternalInput").ap()

    k_scr = nc.dram_tensor("k_scr", [GROUPS, 128, T], F16, kind=scratch_kind).ap()
    v_scr = nc.dram_tensor("v_scr", [GROUPS, 128, T], F16, kind=scratch_kind).ap()
    ctx_scr = nc.dram_tensor("ctx_scr", [HEADS, 128, TOWN], F16, kind=scratch_kind).ap()
    x1_scr = nc.dram_tensor("x1_scr", [16, 128, 16, 256], F32, kind=scratch_kind).ap()
    out_cb = nc.dram_tensor("out_cb", [16, 128, 16, 256], F32, kind="ExternalOutput").ap()

    with ExitStack() as ctx:
        E = ctx.enter_context

        act = E(nc.sbuf_tensor("act", [128, 32 * 1024], F16))
        af3 = act[:].rearrange("p (c t) -> p c t", c=32)
        AR16 = E(nc.sbuf_tensor("AR16", [128, 38912], F16))
        AR32 = E(nc.sbuf_tensor("AR32", [128, 14464], F32))
        idt_s = E(nc.sbuf_tensor("idt_s", [128, 128], F16))
        rt_s = E(nc.sbuf_tensor("rt_s", [128, 128], F16))
        oner_s = E(nc.sbuf_tensor("oner_s", [1, 128], F16))
        cvec_s = E(nc.sbuf_tensor("cvec_s", [128, 4], F32))
        bias_s = E(nc.sbuf_tensor("bias_s", [1, D], F16))

        def v16(off, n):
            assert off + n <= 38912, (off, n)
            return AR16[:, off:off + n]

        def v32(off, n):
            assert off + n <= 14464, (off, n)
            return AR32[:, off:off + n]

        lntm = v16(0, 4096)
        sqb = v16(4096, 4096)
        lntm_l = [v16(0, 4096), v16(16384, 4096)]
        sqb_l = [v16(4096, 4096), v16(20480, 4096)]
        ev_lntp = [None, None]
        ev_sqrd = {}
        wsb = [v16(8192, 4096), v16(12288, 4096)]
        cosf = v16(24576, 4096)
        sinf = v16(28672, 4096)
        kev = v16(32768, 1024)
        r1b = v16(33792, 1024)
        r2b = v16(34816, 1024)
        kgb = v16(0, 4096)
        vgf = v16(4096, 4096)
        pbufb = v16(16384, 2048)
        ptbb = v16(18432, 2048)
        qraw = v16(22528, 1024)
        coso = v16(24576, 2048)
        sino = v16(26624, 2048)
        masksb = v16(28672, 4096)
        vtm = v16(32768, 4160)
        vt3 = vtm.rearrange("p (c n) -> p c n", n=130)
        ctxhb_l = [v16(36928, 1024), v16(21504, 1024)]
        cbuf_l = [v16(37952, 128), v16(38592, 128)]
        qh_l = [v16(23552, 1024), v16(20480, 1024)]
        r2q = v16(38080, 512)
        wsc = [v16(8192, 8192), v16(16384, 8192)]
        lntmE_l = [v16(0, 256), v16(256, 256)]
        sqbC = v16(4096, 256)

        xsl = [v32(0, 4096), v32(4096, 4096)]
        stat = v32(8192, 96)
        fbuf = v32(8288, 2048)
        obA = v32(10336, 2048)
        obB = v32(12384, 2048)
        xcb = [v32(0, 2048), v32(2048, 2048)]
        ctxA_l = [v32(4096, 132), v32(4228, 132)]
        sum1 = stat[:, 32:48]
        sum2 = stat[:, 48:64]
        mu2 = stat[:, 64:80]
        rstd2 = stat[:, 32:48]
        nmr2 = stat[:, 48:64]

        psA = E(nc.psum_tensor("psA", [128, 2048], F32))
        psB = E(nc.psum_tensor("psB", [128, 512], F32))
        psC = E(nc.psum_tensor("psC", [128, 512], F32))
        psT = E(nc.psum_tensor("psT", [128, 1024], F16))
        psT2 = E(nc.psum_tensor("psT2", [128, 1024], F16))
        psAh = [psA[:, 0:1024], psA[:, 1024:2048]]
        psTb = [psT, psT2]
        ctxb = [psC, psB]

        eps_ap = cvec_s[:, 0:1]
        zero_ap = cvec_s[:, 1:2]

        def DMA(eng, dst, src, inc, waits=()):
            return P.emit(eng, lambda e, d=dst, s=src: e.dma_start(out=d, in_=s),
                          _w(*waits), inc, 16)

        def MM(out, lhsT, rhs, start, stop, waits=(), inc=None):
            return P.emit("tensor",
                          lambda e, o=out, l=lhsT, r=rhs, st=start, sp=stop:
                          e.matmul(o, l, r, start=st, stop=sp), _w(*waits), inc)

        def TP(out, in_, waits=(), inc=None):
            return P.emit("tensor",
                          lambda e, o=out, i=in_: e.transpose(o, i, idt_s[:]),
                          _w(*waits), inc)

        def ACTI(out, in_, func, bias, scale, waits=(), inc=None):
            return P.emit("scalar",
                          lambda e, o=out, i=in_, f=func, b=bias, s=scale:
                          e.activation(o, i, f, bias=b, scale=s), _w(*waits), inc)

        def TT(out, a, b, op, waits=(), inc=None, eng="vector"):
            return P.emit(eng,
                          lambda e, o=out, x=a, y=b, p=op: e.tensor_tensor(o, x, y, op=p),
                          _w(*waits), inc, 1)

        def TS(out, in0, s1, s2, op0, op1, waits=(), inc=None, eng="vector"):
            return P.emit(eng,
                          lambda e, o=out, i=in0, a=s1, b=s2, p0=op0, p1=op1:
                          e.tensor_scalar(o, i, a, b, op0=p0, op1=p1), _w(*waits), inc, 1)

        def RED(out, in_, op, waits=(), inc=None):
            return P.emit("vector",
                          lambda e, o=out, i=in_, p=op: e.tensor_reduce(o, i, axis=AX.X, op=p),
                          _w(*waits), inc)

        def CP(out, in_, waits=(), inc=None):
            return P.emit("vector", lambda e, o=out, i=in_: e.tensor_copy(o, i),
                          _w(*waits), inc)

        def RECIP(out, in_, waits=(), inc=None):
            return P.emit("vector", lambda e, o=out, i=in_: e.reciprocal(o, i),
                          _w(*waits), inc)

        def MS(ap, val, waits=(), inc=None):
            return P.emit("gpsimd", lambda e, a=ap, v=val: e.memset(a, v),
                          _w(*waits), inc)

        def WAIT_ONLY(eng, waits):
            P.emit(eng, lambda e: None, _w(*waits), None)

        DMA("sync", idt_s[:], idt_d, "ldc")
        DMA("sync", rt_s[:], rt_d, "ldc")
        DMA("sync", oner_s[:], oner_d, "ldc")
        DMA("sync", cvec_s[:], cvec_d, "ldc")
        DMA("gpsimd", cosf, cosf_d, "ldc")
        DMA("gpsimd", sinf, sinf_d, "ldc")
        EV_CONST = P.ev("ldc")

        tp_war = [None, None]
        tp_bank = [0]
        sqb_rd = [None]

        def ln_slab(xbuf, out16, waits, sqbuf=None, stcol=0, out_war=None, sq_war=None):
            if sqbuf is None:
                sqbuf = sqb
            if out_war is None:
                out_war = P.ev("pet")
            c = stcol
            e_s1 = RED(stat[:, c:c + 1], xbuf, ALU.add, waits=waits, inc="dv")
            e_sq = ACTI(sqbuf, xbuf, AF.Square, 0.0, 1.0,
                        waits=waits + _w(sq_war), inc="ac")
            e_s2 = RED(stat[:, c + 1:c + 2], sqbuf, ALU.add, waits=(e_sq,), inc="dv")
            sqb_rd[0] = e_s2
            e_mu = TS(stat[:, c + 2:c + 3], stat[:, c:c + 1], 1.0 / D, None, ALU.mult,
                      ALU.bypass, waits=(e_s1,), inc="dv")
            e_msq = TT(stat[:, c + 3:c + 4], stat[:, c + 2:c + 3], stat[:, c + 2:c + 3],
                       ALU.mult, waits=(e_mu,), inc="dv")
            e_var = TS(stat[:, c + 4:c + 5], stat[:, c + 1:c + 2], 1.0 / D,
                       stat[:, c + 3:c + 4], ALU.mult,
                       ALU.subtract, waits=(e_s2, e_msq), inc="dv")
            e_l = ACTI(stat[:, c + 5:c + 6], stat[:, c + 4:c + 5], AF.Ln, eps_ap, 1.0,
                       waits=(e_var, EV_CONST), inc="ac")
            e_r = ACTI(stat[:, c + 6:c + 7], stat[:, c + 5:c + 6], AF.Exp, zero_ap, -0.5,
                       waits=(e_l,), inc="ac")
            e_n = TS(stat[:, c + 7:c + 8], stat[:, c + 2:c + 3], stat[:, c + 6:c + 7],
                     -1.0, ALU.mult, ALU.mult, waits=(e_r, e_mu), inc="dv")
            e_ap = ACTI(out16, xbuf, AF.Identity, stat[:, c + 7:c + 8],
                        stat[:, c + 6:c + 7],
                        waits=(e_n, e_r) + _w(out_war), inc="ac")
            return e_ap

        def transpose_group(blocks, dst, waits, evac="vector"):
            bank = tp_bank[0] % 2
            tp_bank[0] += 1
            pt = psTb[bank][:, 0:len(blocks) * 128]
            tp_ev = None
            for j, b in enumerate(blocks):
                tp_ev = TP(pt[:, j * 128:(j + 1) * 128], b,
                           waits=(waits + _w(tp_war[bank])) if j == 0 else (),
                           inc="pet")
            src = pt.rearrange("p (c t) -> p c t", c=len(blocks))
            if evac == "scalar":
                ev = P.emit("scalar",
                            lambda e, o=dst, i=src: e.activation(o, i, AF.Copy,
                                                                 bias=0.0, scale=1.0),
                            _w(tp_ev), "dct")
            else:
                ev = CP(dst, src, waits=_w(tp_ev), inc="dvt")
            tp_war[bank] = ev
            return ev

        def transpose_slab(src16, tcol, waits, war_mm):
            last = None
            for icq in range(8):
                blocks = [src16[:, (icq * 4 + j) * 128:(icq * 4 + j + 1) * 128]
                          for j in range(4)]
                dst = af3[:, icq * 4:(icq + 1) * 4, tcol:tcol + 128]
                last = transpose_group(blocks, dst, waits + _w(*war_mm))
            return last

        ev_apply = {}
        ev_kv_mm_last = None
        ev_psA_free = [None, None]
        ev_cb_free = [None, None]
        ev_ws_free = [None, None]
        ev_kev_free = None

        if "A1" in phases:
            for TC in range(4):
                tcols = slice(TC * 1024, (TC + 1) * 1024)
                for sub in range(8):
                    slab = TC * 8 + sub
                    buf = slab % 2
                    e_ld = DMA("sync", xsl[buf], xf[slab * 128:(slab + 1) * 128, :],
                               "ldx", waits=_w(ev_apply.get(slab - 2)))
                    e_ap = ln_slab(xsl[buf], lntm_l[buf], _w(e_ld), sqbuf=sqb_l[buf],
                                   stcol=buf * 8, out_war=ev_lntp[buf],
                                   sq_war=ev_sqrd.get(buf))
                    ev_sqrd[buf] = sqb_rd[0]
                    ev_apply[slab] = e_ap
                    transpose_slab(lntm_l[buf], sub * 128, _w(e_ap), (ev_kv_mm_last,))
                    ev_lntp[buf] = P.ev("pet")
                e_lnfm = P.ev("dvt")
                for idx in range(16):
                    kv, g = idx // 8, idx % 8
                    wsrc = (wk_d if kv == 0 else wv_d)[g]
                    wb = idx % 2
                    e_wl = DMA("gpsimd", wsb[wb].rearrange("p (a b) -> p a b", a=32),
                               wsrc.rearrange("a p b -> p a b"), "ldw",
                               waits=_w(ev_ws_free[wb]))
                    par = idx % 2
                    pa = psAh[par]
                    e_mm = None
                    for i in range(32):
                        w0 = _w(e_wl, e_lnfm, ev_psA_free[par]) if i == 0 else ()
                        MM(pa[:, 0:512], wsb[wb][:, i * 128:(i + 1) * 128],
                           af3[:, i, 0:512], i == 0, i == 31, waits=w0)
                        e_mm = MM(pa[:, 512:1024], wsb[wb][:, i * 128:(i + 1) * 128],
                                  af3[:, i, 512:1024], i == 0, i == 31,
                                  inc="pea" if i == 31 else None)
                    ev_ws_free[wb] = e_mm
                    e_ev = ACTI(kev, pa, AF.Copy, 0.0, 1.0,
                                waits=_w(e_mm, ev_kev_free), inc="ac")
                    ev_psA_free[par] = e_ev
                    if kv == 0:
                        e_r1 = []
                        for jj in range(2):
                            cb = ctxb[jj]
                            e_rk = MM(cb[:, 0:512], rt_s[:],
                                      kev[:, jj * 512:(jj + 1) * 512], True, True,
                                      waits=_w(e_ev, ev_cb_free[jj]), inc="per")
                            scols = slice(TC * 1024 + jj * 512, TC * 1024 + (jj + 1) * 512)
                            e1 = TT(r1b[:, jj * 512:(jj + 1) * 512], cb[:, 0:512],
                                    sinf[:, scols], ALU.mult,
                                    waits=_w(e_rk, EV_CONST), inc="dv")
                            ev_cb_free[jj] = e1
                            e_r1.append(e1)
                        e_r2 = TT(r2b, kev, cosf[:, tcols], ALU.mult,
                                  waits=_w(e_ev, EV_CONST), inc="dv")
                        e_kr = TT(kev, r1b, r2b, ALU.add,
                                  waits=_w(*e_r1, e_r2), inc="dv")
                        e_st = DMA("scalar", k_scr[g][:, tcols], kev, "stkv", waits=_w(e_kr))
                    else:
                        e_st = DMA("scalar", v_scr[g][:, tcols], kev, "stkv", waits=_w(e_ev))
                    ev_kev_free = e_st
                ev_kv_mm_last = P.ev("pea")

        ev_qproj_last = None
        ev_ctx_store = [None, None]
        ev_cbuf_free = [None, None]
        ev_qh_war = [None, None]
        ev_kg_free = None
        ev_vg_free = None
        ev_vtm_free = None
        ev_qraw_free = [None, None]
        ev_pbuf_free = [None, None]
        ev_ptb_free = [None, None]
        ev_kvready = (None, None)
        ev_maskload = None

        if "A2B" in phases:
            DMA("gpsimd", coso, coso_d, "ldc", waits=_w(P.ev("dv")))
            DMA("gpsimd", sino, sino_d, "ldc")
            ev_maskload = DMA("gpsimd", masksb.rearrange("p (s n) -> p s n", s=16),
                              masks_d.rearrange("s p n -> p s n"), "ldc",
                              waits=_w(P.ev("dv")))
            EV_BC = P.ev("ldc")
            e_msv = MS(vtm, 1.0, waits=_w(P.ev("stkv"), P.ev("dv"), P.ev("ac")), inc="gp")

            for ph in range(2):
                gate_ac = P.ev("ac")
                gate_gp = P.ev("gp")
                for sub in range(8):
                    slab = ph * 8 + sub
                    buf = sub % 2
                    war = ev_apply.get(("o", ph, sub - 2))
                    e_ld = DMA("sync", xsl[buf], xo[slab * 128:(slab + 1) * 128, :],
                               "ldx", waits=_w(war, gate_ac if sub < 2 else None))
                    e_ap = ln_slab(xsl[buf], lntm, _w(e_ld), stcol=0,
                                   out_war=ev_lntp[0] if sub else None,
                                   sq_war=ev_sqrd.get("a2"))
                    ev_sqrd["a2"] = sqb_rd[0]
                    ev_apply[("o", ph, sub)] = e_ap
                    transpose_slab(lntm, sub * 128, _w(e_ap),
                                   (ev_kv_mm_last, ev_qproj_last))
                    ev_lntp[0] = P.ev("pet")
                e_lnfm = P.ev("dvt")

                for hp in range(HEADS // 2):
                    g = (2 * hp) // 4
                    e_wl = [None, None]
                    for lane in range(2):
                        h = 2 * hp + lane
                        e_wl[lane] = DMA("gpsimd",
                                         wsb[lane].rearrange("p (a b) -> p a b", a=32),
                                         wq_d[h].rearrange("a p b -> p a b"), "ldw",
                                         waits=_w(ev_ws_free[lane]))
                    if hp % 2 == 0:
                        a2done = (e_lnfm, P.ev("dv") if hp == 0 else None,
                                  P.ev("ac") if hp == 0 else None,
                                  P.ev("stkv") if (ph == 0 and hp == 0) else None)
                        e_kg = DMA("sync", kgb, k_scr[g], "ldkv",
                                   waits=_w(ev_kg_free, *a2done))
                        e_vg = DMA("sync", vgf, v_scr[g], "ldkv",
                                   waits=_w(ev_vg_free, *a2done))
                        e_vt = None
                        for kq in range(8):
                            blocks = [vgf[:, (kq * 4 + j) * 128:(kq * 4 + j + 1) * 128]
                                      for j in range(4)]
                            e_vt = transpose_group(
                                blocks, vt3[:, kq * 4:(kq + 1) * 4, 0:128],
                                _w(e_vg, e_msv, ev_vtm_free))
                        ev_kvready = (e_kg, e_vt)
                    e_qh_l = [None, None]
                    for lane in range(2):
                        qhb = qh_l[lane]
                        for tcq in range(2):
                            qc = slice(tcq * 512, (tcq + 1) * 512)
                            e_mm = None
                            for i in range(32):
                                w0 = _w(e_wl[lane], e_lnfm, ev_cb_free[1]) if i == 0 else ()
                                e_mm = MM(psB[:, 0:512],
                                          wsb[lane][:, i * 128:(i + 1) * 128],
                                          af3[:, i, qc], i == 0, i == 31, waits=w0,
                                          inc="peq" if i == 31 else None)
                            e_qe = ACTI(qraw[:, qc], psB[:, 0:512], AF.Copy, 0.0, 1.0,
                                        waits=_w(e_mm, *(ev_qraw_free[tcq] or ())),
                                        inc="ac")
                            ev_cb_free[1] = e_qe
                            e_rq = MM(psC[:, 0:512], rt_s[:], qraw[:, qc], True, True,
                                      waits=_w(e_qe, ev_cb_free[0]), inc="per")
                            ocs = slice(ph * 1024 + tcq * 512, ph * 1024 + (tcq + 1) * 512)
                            e_q1 = TT(qhb[:, qc], psC[:, 0:512], sino[:, ocs], ALU.mult,
                                      waits=_w(e_rq, EV_BC, ev_qh_war[lane]), inc="dv")
                            ev_cb_free[0] = e_q1
                            e_q2 = TT(r2q, qraw[:, qc], coso[:, ocs], ALU.mult,
                                      waits=_w(e_qe, EV_BC), inc="dv")
                            e_qh_l[lane] = TT(qhb[:, qc], qhb[:, qc], r2q, ALU.add,
                                              waits=_w(e_q1, e_q2), inc="dv")
                            ev_qraw_free[tcq] = (e_rq, e_q2)

                    subrows = []
                    for ls in range(8):
                        s = ph * 8 + ls
                        ks = KS[s]
                        n = (ks + 1023) // 1024
                        for r in range(n):
                            subrows.append(dict(
                                s=s, ls=ls, r=r, k0=r * 1024,
                                k1=min((r + 1) * 1024, ks), ks=ks,
                                first=(r == 0), last=(r == n - 1), nsub=n))

                    slot_state = [{}, {}]
                    sr_l = [None, None]

                    def emit_S(lane, srp, kvr=ev_kvready, e_qh_l=e_qh_l):
                        sr = dict(srp)
                        par = lane
                        nk = sr["k1"] - sr["k0"]
                        nj = (nk + 511) // 512
                        qtile = qh_l[lane][:, sr["ls"] * 128:(sr["ls"] + 1) * 128]
                        e_chunks = []
                        for jj in range(nj):
                            w = min(512, nk - jj * 512)
                            w0 = _w(e_qh_l[lane], kvr[0], ev_psA_free[par]) if jj == 0 else ()
                            e_chunks.append(
                                MM(psAh[par][:, jj * 512:jj * 512 + w], qtile,
                                   kgb[:, sr["k0"] + jj * 512:sr["k0"] + jj * 512 + w],
                                   True, True, waits=w0, inc="pes"))
                        sr["e_chunks"] = e_chunks
                        return sr

                    def emit_max(lane, sr):
                        par = lane
                        nk = sr["k1"] - sr["k0"]
                        nj = (nk + 511) // 512
                        e_chunks = sr["e_chunks"]
                        e_s = e_chunks[-1]
                        e_mask = None
                        if sr["last"]:
                            lo = sr["ks"] - 256 - sr["k0"]
                            mreg = psAh[par][:, lo:lo + 256]
                            e_mask = TT(mreg, mreg,
                                        masksb[:, sr["s"] * 256:(sr["s"] + 1) * 256],
                                        ALU.add, waits=_w(e_s, ev_maskload), inc="dv")
                        if nj == 2:
                            e_m0 = RED(stat[:, 16 + par:17 + par], psAh[par][:, 0:512],
                                       ALU.max, waits=_w(e_chunks[0]), inc="dv")
                            e_m1 = RED(stat[:, 8 + par:9 + par],
                                       psAh[par][:, 512:nk], ALU.max,
                                       waits=_w(e_s, e_mask), inc="dv")
                            e_max = TT(stat[:, 8 + par:9 + par], stat[:, 8 + par:9 + par],
                                       stat[:, 16 + par:17 + par], ALU.max,
                                       waits=_w(e_m0, e_m1), inc="dv")
                        else:
                            e_max = RED(stat[:, 8 + par:9 + par], psAh[par][:, 0:nk],
                                        ALU.max, waits=_w(e_s, e_mask), inc="dv")
                        st0 = slot_state[lane]
                        mbase = 10 if lane == 0 else 24
                        if sr["first"]:
                            mcol = mbase
                            old = mbase
                            e_mrun = CP(stat[:, mbase:mbase + 1], stat[:, 8 + par:9 + par],
                                        waits=_w(e_max), inc="dv")
                        else:
                            mcol = mbase + sr["r"] % 2
                            old = st0["mcol"]
                            e_mrun = TT(stat[:, mcol:mcol + 1], stat[:, old:old + 1],
                                        stat[:, 8 + par:9 + par], ALU.max,
                                        waits=_w(st0["e_mrun"], e_max), inc="dv")
                        e_nb = TS(stat[:, 14 + par:15 + par], stat[:, mcol:mcol + 1],
                                  -SCALE, None, ALU.mult, ALU.bypass,
                                  waits=_w(e_mrun), inc="gp", eng="gpsimd")
                        if not sr["first"]:
                            sr["e_fd"] = TT(stat[:, 12 + par:13 + par],
                                            stat[:, old:old + 1],
                                            stat[:, mcol:mcol + 1], ALU.subtract,
                                            waits=_w(e_mrun), inc="gp", eng="gpsimd")
                        st0["mcol"] = mcol
                        st0["e_mrun"] = e_mrun
                        sr["e_nb"] = e_nb
                        sr["e_mrun"], sr["mcol"] = e_mrun, mcol

                    def emit_exp(lane, sr):
                        par = lane
                        nk = sr["k1"] - sr["k0"]
                        e_exp = ACTI(pbufb[:, par * 1024:par * 1024 + nk],
                                     psAh[par][:, 0:nk], AF.Exp,
                                     stat[:, 14 + par:15 + par], SCALE,
                                     waits=_w(sr["e_nb"], ev_pbuf_free[par]), inc="ac")
                        ev_psA_free[par] = e_exp
                        if not sr["first"]:
                            e_f = ACTI(stat[:, 28 + par:29 + par],
                                       stat[:, 12 + par:13 + par], AF.Exp,
                                       zero_ap, SCALE, waits=_w(sr["e_fd"]), inc="ac")
                            sr["e_f"] = e_f
                            ev_psA_free[par] = e_f
                        sr["e_exp"] = e_exp

                    def emit_TP(lane, sr):
                        par = lane
                        nk = sr["k1"] - sr["k0"]
                        nkb = nk // 128
                        ev_evac = None
                        sr["ev_evacs"] = []
                        for kq in range((nkb + 3) // 4):
                            nblk = min(4, nkb - kq * 4)
                            blocks = [pbufb[:, par * 1024 + (kq * 4 + jj) * 128:
                                            par * 1024 + (kq * 4 + jj + 1) * 128]
                                      for jj in range(nblk)]
                            dst = ptbb[:, par * 1024 + kq * 512:
                                       par * 1024 + kq * 512 + nblk * 128]
                            ev_evac = transpose_group(
                                blocks, dst.rearrange("p (c t) -> p c t", c=nblk),
                                _w(sr["e_exp"], ev_ptb_free[par]),
                                evac="scalar" if (tp_bank[0] % 2) else "vector")
                            sr["ev_evacs"].append(ev_evac)
                        ev_pbuf_free[par] = P.ev("pet")
                        sr["ev_evac"] = ev_evac

                    def emit_PV_combine(lane, kvr=ev_kvready, sr=None):
                        par = lane
                        nk = sr["k1"] - sr["k0"]
                        nkb = nk // 128
                        ev_evac = sr["ev_evac"]
                        cb = ctxb[lane]
                        ctxA = ctxA_l[lane]
                        e_pv = None
                        for kc in range(nkb):
                            if kc == 0:
                                w0 = _w(sr["ev_evacs"][0], kvr[1], ev_cb_free[lane])
                            elif kc == 4:
                                w0 = _w(*sr["ev_evacs"])
                            else:
                                w0 = ()
                            e_pv = MM(cb[:, 0:129],
                                      ptbb[:, par * 1024 + kc * 128:par * 1024 + (kc + 1) * 128],
                                      vt3[:, sr["k0"] // 128 + kc, 0:129],
                                      kc == 0, kc == nkb - 1, waits=w0,
                                      inc="pev" if kc == nkb - 1 else None)
                        ev_ptb_free[par] = e_pv
                        st0 = slot_state[lane]
                        if sr["nsub"] > 1:
                            if sr["first"]:
                                e_c = CP(ctxA[:, 0:129], cb[:, 0:129],
                                         waits=_w(e_pv, st0.get("e_ctxA_rd")),
                                         inc="dv")
                            else:
                                e_sc = TS(ctxA[:, 0:129], ctxA[:, 0:129],
                                          stat[:, 28 + par:29 + par],
                                          None, ALU.mult, ALU.bypass,
                                          waits=_w(sr["e_f"], st0["e_ctx"]),
                                          inc="dv")
                                e_c = TT(ctxA[:, 0:129], ctxA[:, 0:129], cb[:, 0:129],
                                         ALU.add, waits=_w(e_sc, e_pv), inc="dv")
                            st0["e_ctx"] = e_c
                            ev_cb_free[lane] = e_c
                        if sr["last"]:
                            if sr["nsub"] == 1:
                                lsrc, csrc, e_fin = cb[:, 128:129], cb[:, 0:128], _w(e_pv)
                            else:
                                lsrc, csrc, e_fin = (ctxA[:, 128:129], ctxA[:, 0:128],
                                                     _w(st0["e_ctx"]))
                            rlc = 30 + lane
                            e_rl = RECIP(stat[:, rlc:rlc + 1], lsrc, waits=e_fin, inc="dv")
                            e_cf = ACTI(cbuf_l[lane], csrc, AF.Identity, zero_ap,
                                        stat[:, rlc:rlc + 1],
                                        waits=_w(e_rl, ev_cbuf_free[lane]), inc="ac")
                            if sr["nsub"] == 1:
                                ev_cb_free[lane] = e_cf
                            else:
                                st0["e_ctxA_rd"] = e_cf
                            e_cv = transpose_group(
                                [cbuf_l[lane]],
                                ctxhb_l[lane][:, sr["ls"] * 128:(sr["ls"] + 1) * 128]
                                .rearrange("p (c t) -> p c t", c=1),
                                _w(e_cf, ev_ctx_store[lane]))
                            ev_cbuf_free[lane] = e_cv
                            sr["e_cv"] = e_cv

                    have_prev = False
                    e_cv_last = [None, None]
                    prev_srs = [None, None]
                    for srp in subrows:
                        cur = [emit_S(0, srp), emit_S(1, srp)]
                        emit_max(0, cur[0])
                        emit_max(1, cur[1])
                        if have_prev:
                            for lane in range(2):
                                emit_TP(lane, prev_srs[lane])
                                emit_PV_combine(lane, sr=prev_srs[lane])
                                if prev_srs[lane].get("e_cv"):
                                    e_cv_last[lane] = prev_srs[lane]["e_cv"]
                        emit_exp(0, cur[0])
                        emit_exp(1, cur[1])
                        prev_srs = cur
                        have_prev = True
                    emit_TP(0, prev_srs[0])
                    emit_TP(1, prev_srs[1])
                    for lane in range(2):
                        emit_PV_combine(lane, sr=prev_srs[lane])
                        e_cv_last[lane] = prev_srs[lane]["e_cv"]

                    for lane in range(2):
                        h = 2 * hp + lane
                        e_cs = DMA("scalar", ctx_scr[h][:, ph * 1024:(ph + 1) * 1024],
                                   ctxhb_l[lane], "stcx", waits=_w(e_cv_last[lane]))
                        ev_ctx_store[lane] = e_cs
                        ev_qh_war[lane] = P.ev("pes")
                    ev_ws_free[0] = P.ev("peq")
                    ev_ws_free[1] = P.ev("peq")
                    if hp % 2 == 1:
                        ev_kg_free = P.ev("pes")
                        ev_vg_free = P.ev("pet")
                        ev_vtm_free = P.ev("pev")
                ev_qproj_last = P.ev("peq")

        ev_sum = [None] * 16
        if "C" in phases:
            gate_c = (P.ev("pet"), P.ev("pev"), P.ev("pes"), P.ev("per"),
                      P.ev("dv"), P.ev("ac"))
            e_z1 = MS(sum1, 0.0, waits=_w(P.ev("dv"), P.ev("ac")), inc="gp")
            e_z2 = MS(sum2, 0.0, inc="gp")
            for t in range(16):
                ev_sum[t] = (e_z1, e_z2)
            ev_ws_free_c = [gate_c, gate_c]
            ev_xcb_free = [(P.ev("ac"),), (P.ev("ac"),)]
            ev_fbuf_free = [(P.ev("ac"),), (P.ev("ac"),)]
            ev_psA_free = [P.ev("ac"), P.ev("ac")]
            ev_x1b_free = None
            ev_sq_free = sqb_rd[0]
            for th in range(2):
                act_gate = P.ev("peq") if th == 0 else P.ev("pec")
                e_cl = None
                for q8 in range(4):
                    src = ctx_scr[q8 * 8:(q8 + 1) * 8, :, th * 1024:(th + 1) * 1024]
                    e_cl = DMA("sync", af3[:, q8 * 8:(q8 + 1) * 8, :],
                               src.rearrange("h p t -> p h t"), "ldcx",
                               waits=_w(P.ev("stcx"), act_gate) if q8 == 0 else ())
                for ob in range(16):
                    wb = ob % 2
                    e_wl = DMA("gpsimd", wsc[wb].rearrange("p (a b) -> p a b", a=32),
                               wo_d[ob], "ldw", waits=_w(*ev_ws_free_c[wb]))
                    e_xl = DMA("sync", xcb[wb].rearrange("p (a b) -> p a b", a=8),
                               xocb[ob][:, th * 8:(th + 1) * 8, :], "ldx",
                               waits=_w(*ev_xcb_free[wb]))
                    e_evc = {}
                    fb_c = fbuf if ob % 2 == 0 else obB
                    for hf in range(2):
                        e_mm = None
                        for tc4 in range(4):
                            tc = hf * 4 + tc4
                            for i in range(32):
                                w0 = _w(e_wl, e_cl, ev_psA_free[hf]) if (tc4 == 0 and i == 0) else ()
                                e_mm = MM(psAh[hf][:, tc4 * 256:(tc4 + 1) * 256],
                                          af3[:, i, tc * 128:(tc + 1) * 128],
                                          wsc[wb][:, i * 256:(i + 1) * 256],
                                          i == 0, i == 31, waits=w0,
                                          inc="pec" if (tc4 == 3 and i == 31) else None)
                        e_evc[hf] = ACTI(fb_c[:, hf * 1024:(hf + 1) * 1024], psAh[hf],
                                         AF.Copy, 0.0, 1.0,
                                         waits=_w(e_mm, *ev_fbuf_free[ob % 2]), inc="ac")
                        ev_psA_free[hf] = e_evc[hf]
                    ev_ws_free_c[wb] = ((P.ev("pec")),)
                    for tc in range(8):
                        hf = tc // 4
                        tcg = th * 8 + tc
                        ft = fb_c[:, tc * 256:(tc + 1) * 256]
                        xt = xcb[wb][:, tc * 256:(tc + 1) * 256]
                        x1t = obA[:, tc * 256:(tc + 1) * 256]
                        e_x1 = TT(x1t, ft, xt, ALU.add,
                                  waits=_w(e_evc[hf], e_xl, ev_x1b_free), inc="dv")
                        e_p1 = RED(stat[:, 16:17], x1t, ALU.add, waits=_w(e_x1), inc="dv")
                        e_a1 = TT(sum1[:, tcg:tcg + 1], sum1[:, tcg:tcg + 1],
                                  stat[:, 16:17], ALU.add,
                                  waits=_w(e_p1, *ev_sum[tcg]), inc="dv")
                        e_sq = ACTI(sqbC, x1t, AF.Square, 0.0, 1.0,
                                    waits=_w(e_x1, ev_sq_free), inc="ac")
                        e_p2 = RED(stat[:, 17:18], sqbC, ALU.add, waits=_w(e_sq), inc="dv")
                        ev_sq_free = e_p2
                        e_a2 = TT(sum2[:, tcg:tcg + 1], sum2[:, tcg:tcg + 1],
                                  stat[:, 17:18], ALU.add,
                                  waits=_w(e_p2, *ev_sum[tcg]), inc="dv")
                        ev_sum[tcg] = (e_a1, e_a2)
                    ev_fbuf_free[ob % 2] = ((P.ev("dv")),)
                    e_sx = DMA("scalar", x1_scr[ob][:, th * 8:(th + 1) * 8, :],
                               obA.rearrange("p (a b) -> p a b", a=8), "stx1",
                               waits=_w(*[x for t2 in range(8) for x in ev_sum[th * 8 + t2]]))
                    ev_x1b_free = e_sx
                    ev_xcb_free[wb] = ((P.ev("dv")),)

        ev_ln2fin = [None] * 16
        if "D" in phases:
            for t in range(16):
                e_mu = TS(mu2[:, t:t + 1], sum1[:, t:t + 1], 1.0 / D, None, ALU.mult,
                          ALU.bypass, waits=_w(*(ev_sum[t] or ())), inc="dv")
                e_ms = TT(stat[:, 18:19], mu2[:, t:t + 1], mu2[:, t:t + 1], ALU.mult,
                          waits=_w(e_mu), inc="dv")
                e_va = TS(stat[:, 19:20], sum2[:, t:t + 1], 1.0 / D, stat[:, 18:19],
                          ALU.mult, ALU.subtract,
                          waits=_w(ev_sum[t][1] if ev_sum[t] else None, e_ms), inc="dv")
                e_l = ACTI(stat[:, 20:21], stat[:, 19:20], AF.Ln, eps_ap, 1.0,
                           waits=_w(e_va, EV_CONST), inc="ac")
                e_r = ACTI(rstd2[:, t:t + 1], stat[:, 20:21], AF.Exp, zero_ap, -0.5,
                           waits=_w(e_l, e_mu), inc="ac")
                e_n = TS(nmr2[:, t:t + 1], mu2[:, t:t + 1], rstd2[:, t:t + 1], -1.0,
                         ALU.mult, ALU.mult, waits=_w(e_r, e_va), inc="dv")
                ev_ln2fin[t] = (e_r, e_n)

        if "E" in phases:
            DMA("sync", bias_s[:], bias_d, "ldc")
            EV_BIAS = P.ev("ldc")
            ev_out_store = {0: None, 1: None}
            ev_wf_free = [P.ev("pec"), P.ev("pec")]
            ev_xslE_free = [P.ev("dv"), P.ev("dv")]
            ev_lntmE_free = [P.ev("pes"), P.ev("pes")]
            ev_psA_freeE = [P.ev("ac"), P.ev("ac")]
            ev_fbuf_freeE = [(P.ev("dv"),), (P.ev("dv"),)]
            for th in range(2):
                act_gate = P.ev("pec") if th == 0 else P.ev("pee")
                ev_xcb2_free = [P.ev("dv"), P.ev("dv")]
                for ob in range(16):
                    wb = ob % 2
                    e_xl = DMA("sync", xcb[wb].rearrange("p (a b) -> p a b", a=8),
                               x1_scr[ob][:, th * 8:(th + 1) * 8, :], "ldx",
                               waits=_w(P.ev("stx1"), ev_xcb2_free[wb]))
                    for tc in range(8):
                        tcg = th * 8 + tc
                        lb = tc % 2
                        xt = xcb[wb][:, tc * 256:(tc + 1) * 256]
                        e_apl = ACTI(lntmE_l[lb], xt, AF.Identity,
                                     nmr2[:, tcg:tcg + 1], rstd2[:, tcg:tcg + 1],
                                     waits=_w(e_xl, *(ev_ln2fin[tcg] or ()),
                                              ev_lntmE_free[lb]), inc="ac")
                        blocks = [lntmE_l[lb][:, oh * 128:(oh + 1) * 128]
                                  for oh in range(2)]
                        e_cv = transpose_group(
                            blocks,
                            af3[:, ob * 2:ob * 2 + 2, tc * 128:(tc + 1) * 128],
                            _w(e_apl, act_gate))
                        ev_lntmE_free[lb] = P.ev("pet")
                    ev_xcb2_free[wb] = P.ev("ac")
                e_ln2fm = P.ev("dvt")
                for ob in range(16):
                    wb = ob % 2
                    e_wl = DMA("sync", wsc[wb].rearrange("p (a b) -> p a b", a=32),
                               wf_d[ob], "ldw", waits=_w(ev_wf_free[wb]))
                    e_xl = DMA("sync", xcb[wb].rearrange("p (a b) -> p a b", a=8),
                               x1_scr[ob][:, th * 8:(th + 1) * 8, :], "ldx",
                               waits=_w(ev_xslE_free[wb], P.ev("ac") if ob < 2 else None))
                    e_evf = {}
                    for hf in range(2):
                        e_mm = None
                        for tc4 in range(4):
                            tc = hf * 4 + tc4
                            for i in range(32):
                                w0 = _w(e_wl, e_ln2fm, ev_psA_freeE[hf]) if (tc4 == 0 and i == 0) else ()
                                MM(psAh[hf][:, tc4 * 256:(tc4 + 1) * 256],
                                   af3[:, i, tc * 128:(tc + 1) * 128],
                                   wsc[wb][:, i * 256:(i + 1) * 256],
                                   i == 0, False, waits=w0)
                            e_mm = MM(psAh[hf][:, tc4 * 256:(tc4 + 1) * 256], oner_s[:],
                                      bias_s[0:1, ob * 256:(ob + 1) * 256], False, True,
                                      waits=_w(EV_BIAS) if (tc4 == 0 and ob < 2) else (),
                                      inc="pee" if tc4 == 3 else None)
                        e_evf[hf] = ACTI(fbuf[:, hf * 1024:(hf + 1) * 1024], psAh[hf],
                                         AF.Copy, 0.0, 1.0,
                                         waits=_w(e_mm, *ev_fbuf_freeE[hf]), inc="ac")
                        ev_psA_freeE[hf] = e_evf[hf]
                    ev_wf_free[wb] = P.ev("pee")
                    obuf = obA if ob % 2 == 0 else obB
                    e_o = None
                    for tc in range(8):
                        hf = tc // 4
                        e_o = TT(obuf[:, tc * 256:(tc + 1) * 256],
                                 fbuf[:, tc * 256:(tc + 1) * 256],
                                 xcb[wb][:, tc * 256:(tc + 1) * 256], ALU.add,
                                 waits=_w(e_evf[hf], e_xl, ev_out_store[ob % 2]), inc="dv")
                    ev_fbuf_freeE[0] = ((P.ev("dv")),)
                    ev_fbuf_freeE[1] = ((P.ev("dv")),)
                    ev_xslE_free[wb] = P.ev("dv")
                    e_so = DMA("scalar", out_cb[ob][:, th * 8:(th + 1) * 8, :],
                               obuf.rearrange("p (a b) -> p a b", a=8), "stout",
                               waits=_w(e_o))
                    ev_out_store[ob % 2] = e_so

        final_waits = [P.ev("stout")]
        if debug:
            for s in ("stkv", "stcx", "stx1"):
                final_waits.append(P.ev(s))
        WAIT_ONLY("gpsimd", final_waits)

        semnames = sorted(P.cnt.keys())
        semh = {}
        for n in semnames:
            semh[n] = E(nc.semaphore(n))
        block = E(nc.Block())

        def mk(engname):
            def body(eng):
                last = {}
                for fn, waits, inc, k in P.ops[engname]:
                    for s, v in waits:
                        if last.get(s, -1) < v:
                            eng.wait_ge(semh[s], v)
                            last[s] = v
                    instr = fn(eng)
                    if inc is not None:
                        instr.then_inc(semh[inc], k)
            return body

        block.sync(mk("sync"))
        block.gpsimd(mk("gpsimd"))
        block.tensor(mk("tensor"))
        block.vector(mk("vector"))
        block.scalar(mk("scalar"))

    return nc


def host_inputs(core, L0_attn_x, L0_attn_Wq, L0_attn_Wk, L0_attn_Wv, L0_attn_Wo,
                L0_attn_cos, L0_attn_sin, L0_ffn_weight, L0_ffn_bias):
    b, z = core // 2, core % 2
    chunks = own_chunks(z)
    own_pos = np.concatenate([np.arange(c * 128, (c + 1) * 128) for c in chunks])
    X = np.asarray(L0_attn_x[b], dtype=np.float32)
    cos = np.asarray(L0_attn_cos, np.float32)
    sin = np.asarray(L0_attn_sin, np.float32)

    x_own = np.ascontiguousarray(X[own_pos])
    masks = np.zeros((16, 128, 256), np.float32)
    for s in range(16):
        c = chunks[s]
        kk = np.arange(KS[s] - 256, KS[s])
        qq = c * 128 + np.arange(128)
        masks[s] = np.where(kk[None, :] <= qq[:, None], 0.0, -30000.0)

    d = {
        "xf": X,
        "xo": x_own,
        "xocb": np.ascontiguousarray(
            x_own.reshape(16, 128, 16, 256).transpose(2, 1, 0, 3)),
        "cosf": np.ascontiguousarray(cos.T),
        "sinf": np.ascontiguousarray(sin.T),
        "coso": np.ascontiguousarray(cos[own_pos].T),
        "sino": np.ascontiguousarray(sin[own_pos].T),
        "masks": masks,
        "wq": np.ascontiguousarray(
            np.asarray(L0_attn_Wq, np.float32).reshape(32, 128, 32, 128).transpose(2, 0, 1, 3)),
        "wk": np.ascontiguousarray(
            np.asarray(L0_attn_Wk, np.float32).reshape(32, 128, 8, 128).transpose(2, 0, 1, 3)),
        "wv": np.ascontiguousarray(
            np.asarray(L0_attn_Wv, np.float32).reshape(32, 128, 8, 128).transpose(2, 0, 1, 3)),
        "wo": np.ascontiguousarray(
            np.asarray(L0_attn_Wo, np.float32).reshape(32, 128, 16, 256).transpose(2, 1, 0, 3)),
        "wf": np.ascontiguousarray(
            np.asarray(L0_ffn_weight).T.reshape(32, 128, 16, 256).transpose(2, 1, 0, 3)),
        "biasf": np.asarray(L0_ffn_bias).reshape(1, D),
        "idt": np.eye(128, dtype=np.float16),
        "rt": _rot_matrix_T(),
        "oner": np.ones((1, 128), np.float16),
        "cvec": np.broadcast_to(
            np.array([EPS, 0.0, 1.0, 1.0 / D], np.float32), (128, 4)).copy(),
    }
    return d, own_pos, b


def _rot_matrix_T():
    R = np.zeros((128, 128), np.float32)
    for dd in range(64):
        R[dd, dd + 64] = -1.0
        R[dd + 64, dd] = 1.0
    return R.T.astype(np.float16)


_NC_CACHE = {}


def _get_nc(debug=False, phases=("A1", "A2B", "C", "D", "E")):
    key = (debug, tuple(phases))
    if key not in _NC_CACHE:
        _NC_CACHE[key] = build_program(debug=debug, phases=phases)
    return _NC_CACHE[key]


def kernel(**inputs):
    nc = _get_nc()
    in_maps = []
    metas = []
    for core in range(8):
        d, own_pos, b = host_inputs(core, **inputs)
        in_maps.append(d)
        metas.append((own_pos, b))
    res = run_bass_kernel_spmd(nc, in_maps, core_ids=list(range(8)))
    out = np.empty((4, T, D), np.float32)
    for core in range(8):
        own_pos, b = metas[core]
        oc = np.asarray(res.results[core]["out_cb"], np.float32)
        rows = oc.transpose(2, 1, 0, 3).reshape(TOWN, D)
        out[b][own_pos] = rows
    return out
